# version 03
# baseline (speedup 1.0000x reference)
import math
import numpy as np
import concourse.bass as bass
import concourse.mybir as mybir
from concourse.alu_op_type import AluOpType
from concourse.bass_utils import run_bass_kernel_spmd

F32 = mybir.dt.float32
BF16 = mybir.dt.bfloat16
I32 = mybir.dt.int32
AF = mybir.ActivationFunctionType
AX = mybir.AxisListType

P = 128
D = 2048
S = 2048
NT = 16
NS = 2
H = 16
FF = 8192
NTT = NS * NT
GQ = 64

SC_ATT = 1.0 / math.sqrt(128.0)
EPS = 1e-5
C2HI = float(np.float32(2.0 * math.pi))
C2LO = float(np.float32(2.0 * math.pi - np.float64(np.float32(2.0 * math.pi))))
INV2PI = float(np.float32(1.0 / (2.0 * math.pi)))
PI_CLAMP = 3.1415925
LNTH = math.log(10000.0) / 64.0
LNHI = float(np.float32(LNTH))
LNLO = float(np.float32(LNTH - np.float64(np.float32(LNTH))))

_DBG = False


def _span(ap):
    try:
        t = ap.tensor
        tn = type(t).__name__
        if "SB" not in tn and "PSum" not in tn:
            return None
        isz = mybir.dt.size(ap.dtype)
        lo = ap.offset * isz
        ext = 1
        for step, cnt in ap.ap[1:]:
            ext += abs(step) * (cnt - 1)
        return (t.name, lo, lo + ext * isz)
    except Exception:
        return None


class EP:

    def __init__(self, e):
        self.e = e
        self.w = []
        self.r = []

    @staticmethod
    def _hits(span, lst):
        if span is None:
            return False
        n, lo, hi = span
        for (n2, lo2, hi2) in lst:
            if n == n2 and lo < hi2 and lo2 < hi:
                return True
        return False

    def _go(self, outs, ins, f, *a, **k):
        on = [s for s in (_span(o) for o in outs) if s]
        inn = [s for s in (_span(i) for i in ins
                           if not isinstance(i, (int, float, type(None)))) if s]
        if any(self._hits(s, self.w) for s in inn) or \
           any(self._hits(s, self.w) or self._hits(s, self.r) for s in on):
            self.e.drain()
            self.w, self.r = [], []
        self.w.extend(on)
        self.r.extend(inn)
        return f(*a, **k)

    def wait_ge(self, sem, v):
        return self.e.wait_ge(sem, v)

    def drain(self):
        self.w, self.r = [], []
        return self.e.drain()

    def nop(self):
        return self.e.nop()

    def dma_start(self, **k):
        return self.e.dma_start(**k)

    def memset(self, ap, c):
        return self._go([ap], [], self.e.memset, ap, c)

    def tensor_copy(self, out, in_):
        return self._go([out], [in_], self.e.tensor_copy, out=out, in_=in_)

    def tensor_tensor(self, out, in0, in1, op):
        return self._go([out], [in0, in1], self.e.tensor_tensor, out=out,
                        in0=in0, in1=in1, op=op)

    def tensor_scalar(self, out, in0, scalar1, scalar2, op0, op1=None,
                      accum_out=None):
        kw = dict(out=out, in0=in0, scalar1=scalar1, scalar2=scalar2, op0=op0)
        if op1 is not None:
            kw["op1"] = op1
        if accum_out is not None:
            kw["accum_out"] = accum_out
        return self._go([out, accum_out], [in0, scalar1, scalar2],
                        self.e.tensor_scalar, **kw)

    def scalar_tensor_tensor(self, out, in0, scalar, in1, op0, op1):
        return self._go([out], [in0, scalar, in1], self.e.scalar_tensor_tensor,
                        out=out, in0=in0, scalar=scalar, in1=in1, op0=op0,
                        op1=op1)

    def reduce_sum(self, out, in_, axis, apply_absolute_value=None):
        kw = dict(out=out, in_=in_, axis=axis)
        if apply_absolute_value is not None:
            kw["apply_absolute_value"] = apply_absolute_value
        return self._go([out], [in_], self.e.reduce_sum, **kw)

    def reduce_max(self, out, in_, axis, negate=None):
        kw = dict(out=out, in_=in_, axis=axis)
        if negate is not None:
            kw["negate"] = negate
        return self._go([out], [in_], self.e.reduce_max, **kw)

    def tensor_reduce(self, out, in_, axis, op, apply_absolute_value=None):
        kw = dict(out=out, in_=in_, axis=axis, op=op)
        if apply_absolute_value is not None:
            kw["apply_absolute_value"] = apply_absolute_value
        return self._go([out], [in_], self.e.tensor_reduce, **kw)

    def reciprocal(self, out, in_):
        return self._go([out], [in_], self.e.reciprocal, out=out, in_=in_)

    def activation(self, out, in_, func, bias=None, scale=None, accum_out=None):
        kw = dict(out=out, in_=in_, func=func)
        if bias is not None:
            kw["bias"] = bias
        if scale is not None:
            kw["scale"] = scale
        if accum_out is not None:
            kw["accum_out"] = accum_out
        return self._go([out, accum_out], [in_, bias, scale],
                        self.e.activation, **kw)

    def copy(self, out, in_):
        return self._go([out], [in_], self.e.copy, out=out, in_=in_)

    def sign(self, out, in_):
        return self._go([out], [in_], self.e.sign, out=out, in_=in_)

    def iota(self, out, pattern, base, channel_multiplier,
             allow_small_or_imprecise_dtypes=True):
        return self._go([out], [], self.e.iota, out=out, pattern=pattern,
                        base=base, channel_multiplier=channel_multiplier,
                        allow_small_or_imprecise_dtypes=allow_small_or_imprecise_dtypes)

    def affine_select(self, out, in_, pattern, compare_op, fill, base,
                      channel_multiplier):
        return self._go([out], [in_], self.e.affine_select, out=out, in_=in_,
                        pattern=pattern, compare_op=compare_op, fill=fill,
                        base=base, channel_multiplier=channel_multiplier)


ARENA_COLS = 30720


class B:
    def __init__(self, nc):
        self.nc = nc
        d = nc.dram_tensor
        self.x_h = d("x", [NS, S, D], F32, kind="ExternalInput").ap()
        self.wqt_h = d("wqt", [D, D], F32, kind="ExternalInput").ap()
        self.wkt_h = d("wkt", [D, D], F32, kind="ExternalInput").ap()
        self.wvt_h = d("wvt", [D, D], F32, kind="ExternalInput").ap()
        self.wot_h = d("wot", [D, D], F32, kind="ExternalInput").ap()
        self.w1t_h = d("w1t", [D, FF], F32, kind="ExternalInput").ap()
        self.w2t_h = d("w2t", [D, FF], F32, kind="ExternalInput").ap()
        self.w3t_h = d("w3t", [FF, D], F32, kind="ExternalInput").ap()
        self.out_h = d("out", [NS, S, D], F32, kind="ExternalOutput").ap()
        ks = "ExternalOutput" if _DBG else "Internal"
        self.xt_h = d("xt_s", [NS, NT, 16, P, P], BF16, kind=ks).ap()
        self.tab_h = d("tab_s", [NT, 2, P, H * GQ], F32, kind=ks).ap()
        self.q_h = d("q_s", [NS, H, NT, P, P], BF16, kind=ks).ap()
        self.k_h = d("k_s", [NS, H, NT, P, P], BF16, kind=ks).ap()
        self.v_h = d("v_s", [NS, H, NT, P, P], BF16, kind=ks).ap()
        self.x2_h = d("x2_s", [NS, S, D], F32, kind=ks).ap()
        self.xq_h = d("xq_s", [NTT, 16, P, P], BF16, kind=ks).ap()
        self.hq_h = d("hq_s", [64, NTT, P, P], BF16, kind=ks).ap()
        self.w3s_h = d("w3s_s", [64, P, D], BF16, kind=ks).ap()

        sb = nc.alloc_sbuf_tensor
        self.wA = sb("wA", [P, 16 * D], BF16)
        self.arena = sb("arena", [P, ARENA_COLS], F32)
        self.ident = sb("ident", [P, P], BF16)
        self.ident_f = sb("ident_f", [P, P], F32)
        self.zero_bf = sb("zero_bf", [P, P], BF16)
        self.mask_neg = sb("mask_neg", [P, P], F32)
        self.ones_bf = sb("ones_bf", [P, 1], BF16)
        self.ones_f = sb("ones_f", [P, 1], F32)
        self.onesrow = sb("onesrow", [1, P], F32)
        self.negm_row = sb("negm_row", [1, P * NTT], BF16)
        self.negm_c = sb("negm_c", [P, NTT], F32)
        self.r_c = sb("r_c", [P, NTT], F32)
        self.negm2_c = sb("negm2_c", [P, NTT], F32)
        self.r2_c = sb("r2_c", [P, NTT], F32)
        self.s1acc = sb("s1acc", [P, 1], F32)
        self.s2acc = sb("s2acc", [P, 1], F32)
        self.s3acc = sb("s3acc", [P, 1], F32)
        self.s1_col = sb("s1_col", [P, 1], F32)
        self.s23_col = sb("s23_col", [P, 1], F32)
        self.csw = sb("csw", [1, D], BF16)
        self.st = sb("st", [P, 16], F32)
        self.qs = sb("qs", [P, 4 * GQ], F32)
        self.ssum = sb("ssum", [P, 4], F32)
        self.sbias = sb("sbias", [P, 4], F32)
        self.srs = sb("srs", [P, 4], F32)
        self.posr = sb("posr", [1, P], F32)
        self.dum = sb("dum", [1, 2], F32)
        self.cst_b = sb("cst_b", [P, P], BF16)
        self.cst_f = sb("cst_f", [P, P], F32)
        self.cst_z = sb("cst_z", [P, P], F32)

        def cv(off, cols, dt=F32):
            a = self.arena[:, off:off + cols]
            if dt is F32:
                return a
            return a.bitcast(dt)
        self.xin = [cv(0, 2048), cv(2048, 2048)]
        self.ptok = cv(4096, 2048)
        self.sqb = self.ptok
        self.wstr = [cv(6144, 2048), cv(8192, 2048)]
        self.invf_full = cv(6144, 1024)
        self.stage = [cv(10240, 1024, BF16), cv(11264, 1024, BF16)]
        self.xbf = cv(12288, 1024, BF16)
        self.xtt = [cv(13312, 1024, BF16), cv(14336, 1024, BF16)]
        self.qbf = cv(15360, 1024, BF16)
        self.tA = cv(16384, 1024)
        self.tB = cv(17408, 1024)
        self.tC = cv(18432, 1024)
        self.tki = cv(19456, 1024, I32)
        self.cosb = cv(20480, 1024)
        self.sinb = cv(21504, 1024)
        self.ctab = [cv(22528, 1024), cv(23552, 1024)]
        self.stab = [cv(24576, 1024), cv(25600, 1024)]
        self.re0 = cv(26624, 1024)
        self.re1 = cv(27648, 1024)
        self.re2 = cv(28672, 1024)
        self.re3 = cv(29696, 1024)
        self.wB = cv(0, 16384, BF16)
        self.Kb = [cv(16384, 1024, BF16), cv(17408, 1024, BF16)]
        self.Qb = [cv(18432, 1024, BF16), cv(19456, 1024, BF16)]
        self.Vb = [cv(20480, 1024, BF16), cv(21504, 1024, BF16)]
        self.probs = [cv(22528 + 1024 * i, 1024, BF16) for i in range(4)]
        self.pTe = [cv(26624, 256, BF16), cv(26880, 256, BF16)]
        self.xres = cv(16384, 2048)
        self.x2t = [cv(18432, 2048), cv(20480, 2048)]
        self.xn2 = cv(22528, 2048)
        self.k0i4 = cv(24576, 1024, I32)
        self.rr4 = cv(25600, 1024)
        self.u14 = cv(26624, 1024)
        self.xqb = cv(27648, 1024, BF16)
        self.st4 = [cv(28672, 1024, BF16), cv(29696, 1024, BF16)]
        self.xqt = [cv(16384, 1024, BF16), cv(17408, 1024, BF16)]
        self.z1s = cv(18432, 1024)
        self.hh = cv(19456, 1024)
        self.k0i5 = cv(20480, 1024, I32)
        self.rr5 = cv(21504, 1024)
        self.u15 = cv(22528, 1024)
        self.hqb = cv(23552, 1024, BF16)
        self.st5 = [cv(24576, 1024, BF16), cv(25600, 1024, BF16)]
        self.w5 = [cv(26624, 2048), cv(28672, 2048)]
        self.w6 = [cv(16384, 2048), cv(18432, 2048)]
        self.w3sb = [cv(20480, 1024, BF16), cv(21504, 1024, BF16)]
        self.acc = cv(0, 16384)
        self.hqt = [cv(16384, 1024, BF16), cv(17408, 1024, BF16)]
        self.x2r = [cv(18432, 2048), cv(20480, 2048)]
        self.outb = [cv(22528, 2048), cv(24576, 2048)]

        ps = nc.alloc_psum_tensor
        self.pQ = ps("pQ", [P, D], F32)
        self.pT = ps("pT", [P, D], BF16)
        self.pAng = ps("pAng", [P, 1024], F32)
        self.pQb = self.pQ[:].bitcast(BF16)
        self.pTf = self.pT[:].bitcast(F32)

        for name in ["s0sq", "s0sqr", "s0g", "s0r", "s0xb", "s0tp",
                     "s0ev", "s0w1r", "sIv", "sTps", "sTan", "sTev",
                     "sTrr", "sTsc", "s1w", "s1cs", "s1cse",
                     "s1mm", "s1ev", "s1rp", "s1tr", "s1te",
                     "s3dn", "s3sc", "s3mx", "s3ex", "s3pn", "s3tp", "s3tev",
                     "s3av", "s3ae",
                     "s4w", "s4mm", "s4x2", "s4sq", "s4g", "s4r", "s4q",
                     "s4tr", "s4te",
                     "s5sg", "s5ac", "s5mm", "s5m0", "s5z0", "s5z1", "s5e1", "s5h", "s5q", "s5tr", "s5te",
                     "s5s1c",
                     "s6sg", "s6ac", "s6sc",
                     "s7mm", "s7ac", "s7ev"]:
            setattr(self, name, nc.alloc_semaphore(name))
        for name in ["s0x", "s0sp", "s0w1", "sTsp", "s1xt", "s1tb", "s1sp",
                     "s3kv", "s4x", "s4spx", "s4sp", "s5w", "s5xq", "s5sp",
                     "s6w", "s6sp", "s7w", "s7h", "s7x2", "s7out"]:
            setattr(self, name, [nc.alloc_semaphore(name + "A"),
                                 nc.alloc_semaphore(name + "B")])

    def round_(self, e, t, k0i, r, u1, out):
        e.tensor_copy(out=k0i, in_=t)
        e.tensor_tensor(out=r, in0=t, in1=k0i, op=AluOpType.subtract)
        e.tensor_scalar(out=u1, in0=r, scalar1=0.5, scalar2=-1.0,
                        op0=AluOpType.is_ge, op1=AluOpType.add)
        e.tensor_tensor(out=u1, in0=u1, in1=k0i, op=AluOpType.add)
        e.scalar_tensor_tensor(out=out, in0=r, scalar=-0.5, in1=u1,
                               op0=AluOpType.is_ge, op1=AluOpType.add)

    def quant_(self, e, src, dst, k0i, r, u1):
        g = 16
        sg = self.qs[:, 0:g]
        inv = self.qs[:, g:2 * g]
        sinv = self.qs[:, 2 * g:3 * g]
        e.tensor_reduce(out=sg, in_=src.rearrange("p (g q) -> p g q", q=GQ),
                        axis=AX.X, op=AluOpType.max, apply_absolute_value=True)
        e.tensor_scalar(out=sg, in0=sg, scalar1=EPS, scalar2=None,
                        op0=AluOpType.max)
        e.reciprocal(out=inv, in_=sg)
        e.tensor_scalar(out=inv, in0=inv, scalar1=127.0, scalar2=None,
                        op0=AluOpType.mult)
        e.tensor_scalar(out=sinv, in0=sg, scalar1=1.0 / 127.0, scalar2=None,
                        op0=AluOpType.mult)
        ib = inv.unsqueeze(2).broadcast_to([P, g, GQ])
        sb_ = sinv.unsqueeze(2).broadcast_to([P, g, GQ])
        s3 = src.rearrange("p (g q) -> p g q", q=GQ)
        t = r
        e.tensor_tensor(out=t.rearrange("p (g q) -> p g q", q=GQ), in0=s3,
                        in1=ib, op=AluOpType.mult)
        e.tensor_copy(out=k0i, in_=t)
        e.tensor_tensor(out=u1, in0=t, in1=k0i, op=AluOpType.subtract)
        e.tensor_scalar(out=t, in0=u1, scalar1=0.5, scalar2=-1.0,
                        op0=AluOpType.is_ge, op1=AluOpType.add)
        e.tensor_tensor(out=t, in0=t, in1=k0i, op=AluOpType.add)
        e.scalar_tensor_tensor(out=t, in0=u1, scalar=-0.5, in1=t,
                               op0=AluOpType.is_ge, op1=AluOpType.add)
        return e.tensor_tensor(out=dst.rearrange("p (g q) -> p g q", q=GQ),
                               in0=t.rearrange("p (g q) -> p g q", q=GQ),
                               in1=sb_, op=AluOpType.mult)

    def init_(self, eng, e):
        if eng == "vector":
            e.memset(self.ones_bf[:], 1.0)
            e.memset(self.ones_f[:], 1.0)
            e.memset(self.zero_bf[:], 0.0)
            e.memset(self.s1acc[:], 0.0)
            e.memset(self.s2acc[:], 0.0)
            e.memset(self.s3acc[:], 0.0)
            e.memset(self.onesrow[:], 1.0)
            e.memset(self.cst_b[:], 1.0)
            e.memset(self.cst_f[:], 1.0)
            e.memset(self.cst_z[:], 0.0).then_inc(self.sIv, 1)
        elif eng == "gpsimd":
            e.wait_ge(self.sIv, 1)
            e.affine_select(out=self.ident[:], in_=self.cst_b[:],
                            pattern=[[-1, P]], compare_op=AluOpType.is_equal,
                            fill=0.0, base=0, channel_multiplier=1)
            e.affine_select(out=self.ident_f[:], in_=self.cst_f[:],
                            pattern=[[-1, P]], compare_op=AluOpType.is_equal,
                            fill=0.0, base=0, channel_multiplier=1)
            e.affine_select(out=self.mask_neg[:], in_=self.cst_z[:],
                            pattern=[[-1, P]], compare_op=AluOpType.is_ge,
                            fill=-3.0e38, base=0, channel_multiplier=1)
            e.memset(self.dum[:], 0.0).then_inc(self.sIv, 1)

    def p0(self, eng, e):
        if eng == "sync":
            def spill(m):
                e.wait_ge(self.s0ev, m + 1)
                e.dma_start(out=self.xt_h[m // NT, m % NT].rearrange("i p e -> p i e"),
                            in_=self.stage[m % 2][:].rearrange("p (i e) -> p i e", e=P)
                            ).then_inc(self.s0sp[m % 2], 16)
            for n in range(NTT):
                s, t = divmod(n, NT)
                if n >= 2:
                    spill(n - 2)
                    e.wait_ge(self.s0xb, n - 1)
                    e.wait_ge(self.s0sq, n - 1)
                e.dma_start(out=self.xin[n % 2][:],
                            in_=self.x_h[s, t * P:(t + 1) * P, :]
                            ).then_inc(self.s0x[n % 2], 16)
            spill(NTT - 2)
            spill(NTT - 1)
        elif eng == "scalar":
            e.wait_ge(self.sIv, 2)
            for n in range(NTT):
                for c in range(2 * n, 2 * n + 2):
                    if c >= 2:
                        e.wait_ge(self.s0w1r, c - 1)
                    e.dma_start(out=self.wstr[c % 2][:],
                                in_=self.w1t_h[(c % 16) * P:(c % 16 + 1) * P,
                                               (c // 16) * D:(c // 16 + 1) * D]
                                ).then_inc(self.s0w1[c % 2], 16)
                e.wait_ge(self.s0x[n % 2], 16 * (n // 2 + 1))
                if n >= 1:
                    e.wait_ge(self.s0sqr, n)
                e.activation(out=self.sqb[:], in_=self.xin[n % 2][:],
                             func=AF.Square).then_inc(self.s0sq, 1)
                e.wait_ge(self.s0g, n + 1)
                e.activation(out=self.r_c[:, n:n + 1], in_=self.st[:, 2:3],
                             func=AF.Sqrt).then_inc(self.s0r, 1)
                e.wait_ge(self.s0tp, n + 1)
                if n >= 2:
                    e.wait_ge(self.s0sp[n % 2], 16 * (n // 2))
                e.copy(out=self.stage[n % 2][:], in_=self.pT[:])
                e.copy(out=self.negm_row[0:1, n * P:(n + 1) * P],
                       in_=self.pAng[0:1, 0:P]).then_inc(self.s0ev, 1)
        elif eng == "gpsimd":
            pass
        elif eng == "vector":
            for n in range(NTT):
                e.wait_ge(self.s0x[n % 2], 16 * (n // 2 + 1))
                e.reduce_sum(out=self.st[:, 0:1], in_=self.xin[n % 2][:], axis=AX.X)
                if n >= 1:
                    e.wait_ge(self.s0tp, n)
                e.tensor_copy(out=self.xbf[:], in_=self.xin[n % 2][:]
                              ).then_inc(self.s0xb, 1)
                e.tensor_scalar(out=self.negm_c[:, n:n + 1], in0=self.st[:, 0:1],
                                scalar1=-1.0 / D, scalar2=None, op0=AluOpType.mult)
                e.wait_ge(self.s0sq, n + 1)
                e.reduce_sum(out=self.st[:, 1:2], in_=self.sqb[:],
                             axis=AX.X).then_inc(self.s0sqr, 1)
                e.tensor_tensor(out=self.st[:, 3:4], in0=self.negm_c[:, n:n + 1],
                                in1=self.negm_c[:, n:n + 1], op=AluOpType.mult)
                if n >= 1:
                    e.wait_ge(self.s0r, n)
                e.tensor_scalar(out=self.st[:, 2:3], in0=self.st[:, 1:2],
                                scalar1=1.0 / D, scalar2=None, op0=AluOpType.mult)
                e.tensor_tensor(out=self.st[:, 2:3], in0=self.st[:, 2:3],
                                in1=self.st[:, 3:4], op=AluOpType.subtract)
                e.tensor_scalar(out=self.st[:, 2:3], in0=self.st[:, 2:3],
                                scalar1=EPS, scalar2=None, op0=AluOpType.add)
                e.reciprocal(out=self.st[:, 2:3],
                             in_=self.st[:, 2:3]).then_inc(self.s0g, 1)
                for c in range(2 * n, 2 * n + 2):
                    e.wait_ge(self.s0w1[c % 2], 16 * (c // 2 + 1))
                    e.reduce_sum(out=self.st[:, 4:5], in_=self.wstr[c % 2][:],
                                 axis=AX.X, apply_absolute_value=True)
                    e.tensor_tensor(out=self.s1acc[:], in0=self.s1acc[:],
                                    in1=self.st[:, 4:5],
                                    op=AluOpType.add).then_inc(self.s0w1r, 1)
        elif eng == "tensor":
            e.wait_ge(self.sIv, 2)
            for n in range(NTT):
                e.wait_ge(self.s0xb, n + 1)
                if n >= 1:
                    e.wait_ge(self.s0ev, n)
                for c in range(16):
                    e.transpose(out=self.pT[:, c * P:(c + 1) * P],
                                in_=self.xbf[:, c * P:(c + 1) * P],
                                identity=self.ident[:])
                e.wait_ge(self.s0g, n + 1)
                e.transpose(out=self.pAng[0:1, 0:P],
                            in_=self.negm_c[:, n:n + 1],
                            identity=self.ident_f[:]).then_inc(self.s0tp, 1)

    def p05(self, eng, e):
        pos = self.st[:, 8:9]
        if eng == "gpsimd":
            e.wait_ge(self.s0w1r, 2 * NTT)
            e.iota(out=self.tA[:].rearrange("p (h q) -> p h q", q=GQ),
                   pattern=[[0, H], [1, GQ]], base=0, channel_multiplier=0,
                   allow_small_or_imprecise_dtypes=True).then_inc(self.sIv, 1)
            for t in range(NT):
                if t >= 1:
                    e.wait_ge(self.sTan, t)
                e.iota(out=pos, pattern=[[0, 1]], base=t * P,
                       channel_multiplier=1,
                       allow_small_or_imprecise_dtypes=True).then_inc(self.sTps, 1)
        elif eng == "scalar":
            e.wait_ge(self.sIv, 4)
            e.activation(out=self.invf_full[:], in_=self.tC[:],
                         func=AF.Exp).then_inc(self.sIv, 1)
            for t in range(NT):
                e.wait_ge(self.sTrr, 2 * t + 2)
                if t >= 1:
                    e.wait_ge(self.sTsp[0], 16 * t)
                    e.wait_ge(self.sTsp[1], 16 * t)
                e.activation(out=self.sinb[:], in_=self.tB[:], func=AF.Sin)
                e.activation(out=self.cosb[:], in_=self.tC[:],
                             func=AF.Sin).then_inc(self.sTsc, 1)
        elif eng == "sync":
            for t in range(NT):
                e.wait_ge(self.sTsc, t + 1)
                e.dma_start(out=self.tab_h[t, 0],
                            in_=self.cosb[:]).then_inc(self.sTsp[0], 16)
                e.dma_start(out=self.tab_h[t, 1],
                            in_=self.sinb[:]).then_inc(self.sTsp[1], 16)
        elif eng == "vector":
            e.wait_ge(self.sIv, 3)
            e.tensor_scalar(out=self.tB[:], in0=self.tA[:], scalar1=-LNHI,
                            scalar2=None, op0=AluOpType.mult)
            e.scalar_tensor_tensor(out=self.tC[:], in0=self.tA[:], scalar=-LNLO,
                                   in1=self.tB[:], op0=AluOpType.mult,
                                   op1=AluOpType.add).then_inc(self.sIv, 1)
            for t in range(NT):
                e.wait_ge(self.sTps, t + 1)
                if t == 0:
                    e.wait_ge(self.sIv, 5)
                if t >= 1:
                    e.wait_ge(self.sTsc, t)
                e.tensor_scalar(out=self.tA[:], in0=self.invf_full[:],
                                scalar1=pos, scalar2=None,
                                op0=AluOpType.mult).then_inc(self.sTan, 1)
                self._rred(e, shift=0.0, dst=self.tB)
                self._rred(e, shift=float(np.float32(math.pi / 2.0)), dst=self.tC)

    def _rred(self, e, shift, dst):
        a = self.tA[:]
        if shift != 0.0:
            e.tensor_scalar(out=self.re0[:], in0=a, scalar1=shift, scalar2=None,
                            op0=AluOpType.add)
            a = self.re0[:]
        e.tensor_scalar(out=self.re1[:], in0=a, scalar1=INV2PI, scalar2=None,
                        op0=AluOpType.mult)
        self.round_(e, self.re1[:], self.tki[:], self.re2[:], self.re3[:],
                    self.re1[:])
        e.scalar_tensor_tensor(out=self.re2[:], in0=self.re1[:], scalar=-C2HI,
                               in1=a, op0=AluOpType.mult, op1=AluOpType.add)
        e.scalar_tensor_tensor(out=self.re3[:], in0=self.re1[:], scalar=-C2LO,
                               in1=self.re2[:], op0=AluOpType.mult,
                               op1=AluOpType.add)
        e.tensor_scalar(out=dst[:], in0=self.re3[:], scalar1=-PI_CLAMP,
                        scalar2=PI_CLAMP, op0=AluOpType.max,
                        op1=AluOpType.min).then_inc(self.sTrr, 1)

    def p1(self, eng, e, which):
        w_h = {"q": self.wqt_h, "k": self.wkt_h, "v": self.wvt_h}[which]
        dst_h = {"q": self.q_h, "k": self.k_h, "v": self.v_h}[which]
        rope = which in ("q", "k")
        base = {"q": 0, "k": 1, "v": 2}[which]
        rbase = {"q": 0, "k": 1, "v": 0}[which]
        if eng == "gpsimd":
            if base >= 1:
                e.wait_ge(self.s1mm, base * NTT)
            e.dma_start(out=self.wA[:].rearrange("p (c o) -> p c o", o=D),
                        in_=w_h.rearrange("(c p) o -> p c o", p=P)
                        ).then_inc(self.s1w, 16)
        elif eng == "tensor":
            e.wait_ge(self.s1w, 16 * (base + 1))
            if base >= 1:
                e.wait_ge(self.s1cse, base)
            for oc in range(4):
                for i in range(16):
                    mm = e.matmul(out=self.pQ[0:1, oc * 512:(oc + 1) * 512],
                                  lhsT=self.ones_bf[:],
                                  rhs=self.wA[:, i * D + oc * 512:i * D + (oc + 1) * 512],
                                  start=(i == 0), stop=(i == 15))
            mm.then_inc(self.s1cs, 1)
            e.wait_ge(self.s1cse, base + 1)
            for n in range(NTT):
                idx = base * NTT + n
                e.wait_ge(self.s1xt[idx % 2], 16 * (idx // 2 + 1))
                if idx >= 1:
                    e.wait_ge(self.s1ev, idx)
                for i in range(16):
                    for oc in range(4):
                        e.matmul(out=self.pQ[:, oc * 512:(oc + 1) * 512],
                                 lhsT=self.xtt[n % 2][:, i * P:(i + 1) * P],
                                 rhs=self.wA[:, i * D + oc * 512:i * D + (oc + 1) * 512],
                                 start=(i == 0), stop=False)
                for oc in range(4):
                    mm = e.matmul(out=self.pQ[:, oc * 512:(oc + 1) * 512],
                                  lhsT=self.negm_row[0:1, n * P:(n + 1) * P],
                                  rhs=self.csw[0:1, oc * 512:(oc + 1) * 512],
                                  start=False, stop=True)
                mm.then_inc(self.s1mm, 1)
                if rope and n >= 1:
                    self._p1_trans(e, rbase * NTT + n - 1)
            if rope:
                self._p1_trans(e, rbase * NTT + NTT - 1)
        elif eng == "sync":
            if base == 0:
                e.wait_ge(self.s0sp[0], 16 * (NTT // 2))
                e.wait_ge(self.s0sp[1], 16 * (NTT // 2))
            for n in range(NTT):
                idx = base * NTT + n
                s, t = divmod(n, NT)
                if idx >= 2:
                    e.wait_ge(self.s1mm, idx - 1)
                e.dma_start(out=self.xtt[n % 2][:].rearrange("p (i q) -> p i q", q=P),
                            in_=self.xt_h[s, t].rearrange("i p q -> p i q")
                            ).then_inc(self.s1xt[idx % 2], 16)
                if rope:
                    if base == 0 and n == 0:
                        e.wait_ge(self.sTsp[0], 16 * 16)
                        e.wait_ge(self.sTsp[1], 16 * 16)
                    ridx = rbase * NTT + n
                    if ridx >= 2:
                        e.wait_ge(self.s1rp, ridx - 1)
                    e.dma_start(out=self.ctab[n % 2][:], in_=self.tab_h[t, 0]
                                ).then_inc(self.s1tb[ridx % 2], 16)
                    e.dma_start(out=self.stab[n % 2][:], in_=self.tab_h[t, 1]
                                ).then_inc(self.s1tb[ridx % 2], 16)
        elif eng == "scalar":
            e.wait_ge(self.s1cs, base + 1)
            e.copy(out=self.csw[0:1, :],
                   in_=self.pQ[0:1, 0:D]).then_inc(self.s1cse, 1)
            for n in range(NTT):
                idx = base * NTT + n
                e.wait_ge(self.s1mm, idx + 1)
                if rope:
                    ridx = rbase * NTT + n
                    if n >= 1:
                        self._p1_evac(e, which, ridx - 1, dst_h)
                    e.activation(out=self.ptok[:], in_=self.pQ[:, 0:D],
                                 func=AF.Copy,
                                 scale=self.r_c[:, n:n + 1]).then_inc(self.s1ev, 1)
                else:
                    if idx >= 2:
                        e.wait_ge(self.s1sp[idx % 2], 16 * (idx // 2))
                    e.activation(out=self.stage[n % 2][:], in_=self.pQ[:, 0:D],
                                 func=AF.Copy,
                                 scale=self.r_c[:, n:n + 1]).then_inc(self.s1ev, 1)
                    e.wait_ge(self.s1ev, base * NTT + n + 1)
                    e.dma_start(out=dst_h[n // NT, :, n % NT].rearrange("h p q -> p h q"),
                                in_=self.stage[n % 2][:].rearrange(
                                    "p (h q) -> p h q", q=P)
                                ).then_inc(self.s1sp[idx % 2], 16)
            if rope:
                self._p1_evac(e, which, rbase * NTT + NTT - 1, dst_h)
        elif eng == "vector":
            if not rope:
                return
            for n in range(NTT):
                idx = base * NTT + n
                ridx = rbase * NTT + n
                e.wait_ge(self.s1ev, idx + 1)
                e.wait_ge(self.s1tb[ridx % 2], 32 * (ridx // 2 + 1))
                if ridx >= 1:
                    e.wait_ge(self.s1tr, ridx)
                pe_ = self.ptok[:].rearrange("p (c two) -> p c two", two=2)
                qe, qo = pe_[:, :, 0:1], pe_[:, :, 1:2]
                ct = self.ctab[n % 2][:].unsqueeze(2)
                st_ = self.stab[n % 2][:].unsqueeze(2)
                r0 = self.re0[:].unsqueeze(2)
                r1 = self.re1[:].unsqueeze(2)
                r2 = self.re2[:].unsqueeze(2)
                r3 = self.re3[:].unsqueeze(2)
                e.tensor_tensor(out=r0, in0=qe, in1=ct, op=AluOpType.mult)
                e.tensor_tensor(out=r1, in0=qo, in1=st_, op=AluOpType.mult)
                e.tensor_tensor(out=r2, in0=qe, in1=st_, op=AluOpType.mult)
                e.tensor_tensor(out=r3, in0=qo, in1=ct, op=AluOpType.mult)
                ob = self.qbf[:].rearrange("p (c two) -> p c two", two=2)
                e.tensor_tensor(out=ob[:, :, 0:1], in0=r0, in1=r1,
                                op=AluOpType.subtract)
                e.tensor_tensor(out=ob[:, :, 1:2], in0=r2, in1=r3,
                                op=AluOpType.add).then_inc(self.s1rp, 1)

    def _p1_trans(self, e, ridx):
        e.wait_ge(self.s1rp, ridx + 1)
        if ridx >= 1:
            e.wait_ge(self.s1te, ridx)
        for c in range(16):
            tr = e.transpose(out=self.pT[:, c * P:(c + 1) * P],
                             in_=self.qbf[:, c * P:(c + 1) * P],
                             identity=self.ident[:])
        tr.then_inc(self.s1tr, 1)

    def _p1_evac(self, e, which, ridx, dst_h):
        base = {"q": 0, "k": 1}[which]
        n = ridx - base * NTT
        idx = base * NTT + n
        e.wait_ge(self.s1tr, ridx + 1)
        if idx >= 2:
            e.wait_ge(self.s1sp[idx % 2], 16 * (idx // 2))
        e.copy(out=self.stage[n % 2][:], in_=self.pT[:]).then_inc(self.s1te, 1)
        e.wait_ge(self.s1te, ridx + 1)
        e.dma_start(out=dst_h[n // NT, :, n % NT].rearrange("h p q -> p h q"),
                    in_=self.stage[n % 2][:].rearrange("p (h q) -> p h q", q=P)
                    ).then_inc(self.s1sp[idx % 2], 16)

    def p3(self, eng, e, s):
        def hh(h):
            return s * H + h

        def tauc(h, tau):
            return hh(h) * NT + tau

        def tpc(h, g, kt):
            c = hh(h) * 40
            for g2 in range(g):
                c += 4 * g2 + 4
            return c + kt

        def gc(h, g):
            return hh(h) * 4 + g

        if eng == "sync":
            if s == 0:
                for par in range(2):
                    e.wait_ge(self.s1sp[par], 16 * 3 * NTT // 2)
            else:
                for par in range(2):
                    e.wait_ge(self.s4spx[par], 16 * (NT // 2))
            for h in range(H):
                gh = hh(h)
                if gh >= 2:
                    e.wait_ge(self.s3dn, gh - 1)
                e.dma_start(out=self.Kb[h % 2][:].rearrange("p (t q) -> p t q", q=P),
                            in_=self.k_h[s, h].rearrange("t p q -> p t q")
                            ).then_inc(self.s3kv[h % 2], 16)
                e.dma_start(out=self.Qb[h % 2][:].rearrange("p (t q) -> p t q", q=P),
                            in_=self.q_h[s, h].rearrange("t p q -> p t q")
                            ).then_inc(self.s3kv[h % 2], 16)
                e.dma_start(out=self.Vb[h % 2][:].rearrange("p (t q) -> p t q", q=P),
                            in_=self.v_h[s, h].rearrange("t p q -> p t q")
                            ).then_inc(self.s3kv[h % 2], 16)
        elif eng == "tensor":
            for h in range(H):
                gh = hh(h)
                e.wait_ge(self.s3kv[h % 2], 48 * (gh // 2 + 1))
                for g in range(4):
                    for tg in range(4):
                        tau = 4 * g + tg
                        kw = (tau + 1) * P
                        if tauc(h, tau) >= 1:
                            e.wait_ge(self.s3ex, tauc(h, tau))
                        nk = (kw + 511) // 512
                        for kc in range(nk):
                            wdt = min(512, kw - kc * 512)
                            e.matmul(out=self.pQ[:, kc * 512:kc * 512 + wdt],
                                     lhsT=self.Qb[h % 2][:, tau * P:(tau + 1) * P],
                                     rhs=self.Kb[h % 2][:, kc * 512:kc * 512 + wdt],
                                     start=True, stop=True).then_inc(self.s3sc, 1) \
                                if kc == nk - 1 else \
                                e.matmul(out=self.pQ[:, kc * 512:kc * 512 + wdt],
                                         lhsT=self.Qb[h % 2][:, tau * P:(tau + 1) * P],
                                         rhs=self.Kb[h % 2][:, kc * 512:kc * 512 + wdt],
                                         start=True, stop=True)
                    e.wait_ge(self.s3pn, tauc(h, 4 * g + 3) + 1)
                    if gh * 4 + g >= 1:
                        e.wait_ge(self.s3ae, gc(h, g))
                    for kt in range(4 * g + 4):
                        par = kt % 2
                        c = tpc(h, g, kt)
                        if c >= 2:
                            e.wait_ge(self.s3tev, c - 1)
                        dst = self.pT[:, par * 1024:par * 1024 + 512]
                        for tg in range(4):
                            tau = 4 * g + tg
                            src = (self.probs[tg][:, kt * P:(kt + 1) * P]
                                   if kt <= tau else self.zero_bf[:])
                            tr = e.transpose(out=dst[:, tg * P:(tg + 1) * P]
                                             if False else
                                             self.pT[:, par * 1024 + tg * P:
                                                     par * 1024 + (tg + 1) * P],
                                             in_=src, identity=self.ident[:])
                        tr.then_inc(self.s3tp, 1)
                        e.wait_ge(self.s3tev, c + 1)
                        e.matmul(out=self.pAng[:, 0:512],
                                 lhsT=self.Vb[h % 2][:, kt * P:(kt + 1) * P],
                                 rhs=self.pTe[par][:],
                                 start=(kt == 0),
                                 stop=(kt == 4 * g + 3)).then_inc(
                                     self.s3av, 1) if kt == 4 * g + 3 else \
                            e.matmul(out=self.pAng[:, 0:512],
                                     lhsT=self.Vb[h % 2][:, kt * P:(kt + 1) * P],
                                     rhs=self.pTe[par][:],
                                     start=(kt == 0), stop=False)
        elif eng == "vector":
            for h in range(H):
                for g in range(4):
                    for tg in range(4):
                        tau = 4 * g + tg
                        kw = (tau + 1) * P
                        idx = tauc(h, tau)
                        e.wait_ge(self.s3sc, idx + 1)
                        e.tensor_tensor(out=self.pQ[:, tau * P:kw],
                                        in0=self.pQ[:, tau * P:kw],
                                        in1=self.mask_neg[:], op=AluOpType.add)
                        e.reduce_max(out=self.sbias[:, tg:tg + 1],
                                     in_=self.pQ[:, 0:kw], axis=AX.X, negate=True)
                        e.tensor_scalar(out=self.sbias[:, tg:tg + 1],
                                        in0=self.sbias[:, tg:tg + 1],
                                        scalar1=SC_ATT, scalar2=None,
                                        op0=AluOpType.mult).then_inc(self.s3mx, 1)
                        e.wait_ge(self.s3ex, idx + 1)
                        e.reciprocal(out=self.srs[:, tg:tg + 1],
                                     in_=self.ssum[:, tg:tg + 1])
                        e.tensor_scalar(out=self.probs[tg][:, 0:kw],
                                        in0=self.probs[tg][:, 0:kw],
                                        scalar1=self.srs[:, tg:tg + 1],
                                        scalar2=None,
                                        op0=AluOpType.mult).then_inc(self.s3pn, 1)
        elif eng == "scalar":
            for h in range(H):
                for g in range(4):
                    for tg in range(4):
                        tau = 4 * g + tg
                        kw = (tau + 1) * P
                        idx = tauc(h, tau)
                        e.wait_ge(self.s3mx, idx + 1)
                        if hh(h) * 4 + g >= 1:
                            e.wait_ge(self.s3tp, tpc(h, g, 0))
                        e.activation(out=self.probs[tg][:, 0:kw],
                                     in_=self.pQ[:, 0:kw], func=AF.Exp,
                                     bias=self.sbias[:, tg:tg + 1], scale=SC_ATT,
                                     accum_out=self.ssum[:, tg:tg + 1]
                                     ).then_inc(self.s3ex, 1)
                    for kt in range(4 * g + 4):
                        par = kt % 2
                        c = tpc(h, g, kt)
                        e.wait_ge(self.s3tp, c + 1)
                        e.copy(out=self.pTe[par][:],
                               in_=self.pT[:, par * 1024:par * 1024 + 512]
                               ).then_inc(self.s3tev, 1)
                    e.wait_ge(self.s3av, gc(h, g) + 1)
                    e.copy(out=self.wB[:, h * D + g * 512:h * D + (g + 1) * 512],
                           in_=self.pAng[:, 0:512]).then_inc(self.s3ae, 1)
        elif eng == "gpsimd":
            for h in range(H):
                e.wait_ge(self.s3ae, gc(h, 3) + 1)
                e.memset(self.dum[:], 0.0).then_inc(self.s3dn, 1)

    def p4(self, eng, e, s):
        def n_(t):
            return s * NT + t
        if eng == "gpsimd":
            if s == 0:
                e.dma_start(out=self.wA[:].rearrange("p (c o) -> p c o", o=D),
                            in_=self.wot_h.rearrange("(c p) o -> p c o", p=P)
                            ).then_inc(self.s4w, 16)
        elif eng == "sync":
            for t in range(NT):
                n = n_(t)
                if n >= 1:
                    e.wait_ge(self.s4x2, n)
                e.dma_start(out=self.xres[:],
                            in_=self.x_h[s, t * P:(t + 1) * P, :]
                            ).then_inc(self.s4x[n % 2], 16)
        elif eng == "tensor":
            e.wait_ge(self.s4w, 16)
            for t in range(NT):
                n = n_(t)
                if n >= 1:
                    e.wait_ge(self.s4x2, n)
                for i in range(16):
                    for oc in range(4):
                        mm = e.matmul(out=self.pQ[:, oc * 512:(oc + 1) * 512],
                                      lhsT=self.wB[:, i * D + t * P:
                                                   i * D + (t + 1) * P],
                                      rhs=self.wA[:, i * D + oc * 512:
                                                  i * D + (oc + 1) * 512],
                                      start=(i == 0), stop=(i == 15))
                mm.then_inc(self.s4mm, 1)
                e.wait_ge(self.s4q, n + 1)
                if n >= 1:
                    e.wait_ge(self.s4te, n)
                for c in range(16):
                    tr = e.transpose(out=self.pT[:, c * P:(c + 1) * P],
                                     in_=self.xqb[:, c * P:(c + 1) * P],
                                     identity=self.ident[:])
                tr.then_inc(self.s4tr, 1)
        elif eng == "vector":
            for t in range(NT):
                n = n_(t)
                e.wait_ge(self.s4mm, n + 1)
                e.wait_ge(self.s4x[n % 2], 16 * (n // 2 + 1))
                if n >= 2:
                    e.wait_ge(self.s4spx[n % 2], 16 * (n // 2))
                e.tensor_tensor(out=self.x2t[n % 2][:], in0=self.pQ[:, 0:D],
                                in1=self.xres[:], op=AluOpType.add
                                ).then_inc(self.s4x2, 1)
                e.reduce_sum(out=self.st[:, 0:1], in_=self.x2t[n % 2][:],
                             axis=AX.X)
                e.tensor_scalar(out=self.negm2_c[:, n:n + 1], in0=self.st[:, 0:1],
                                scalar1=-1.0 / D, scalar2=None, op0=AluOpType.mult)
                e.wait_ge(self.s4sq, n + 1)
                e.reduce_sum(out=self.st[:, 1:2], in_=self.xn2[:], axis=AX.X)
                e.tensor_tensor(out=self.st[:, 3:4], in0=self.negm2_c[:, n:n + 1],
                                in1=self.negm2_c[:, n:n + 1], op=AluOpType.mult)
                if n >= 1:
                    e.wait_ge(self.s4r, n)
                e.tensor_scalar(out=self.st[:, 2:3], in0=self.st[:, 1:2],
                                scalar1=1.0 / D, scalar2=None, op0=AluOpType.mult)
                e.tensor_tensor(out=self.st[:, 2:3], in0=self.st[:, 2:3],
                                in1=self.st[:, 3:4], op=AluOpType.subtract)
                e.tensor_scalar(out=self.st[:, 2:3], in0=self.st[:, 2:3],
                                scalar1=EPS, scalar2=None, op0=AluOpType.add)
                e.reciprocal(out=self.st[:, 2:3],
                             in_=self.st[:, 2:3]).then_inc(self.s4g, 1)
                e.wait_ge(self.s4r, n + 1)
                e.tensor_scalar(out=self.xn2[:], in0=self.x2t[n % 2][:],
                                scalar1=self.negm2_c[:, n:n + 1], scalar2=None,
                                op0=AluOpType.add)
                e.tensor_scalar(out=self.xn2[:], in0=self.xn2[:],
                                scalar1=self.r2_c[:, n:n + 1], scalar2=None,
                                op0=AluOpType.mult)
                for half in range(2):
                    qi = self.quant_(e, self.xn2[:, half * 1024:(half + 1) * 1024],
                                     self.xqb[:, half * 1024:(half + 1) * 1024],
                                     self.k0i4[:], self.rr4[:], self.u14[:])
                qi.then_inc(self.s4q, 1)
        elif eng == "scalar":
            for t in range(NT):
                n = n_(t)
                e.wait_ge(self.s4x2, n + 1)
                e.dma_start(out=self.x2_h[s, t * P:(t + 1) * P, :],
                            in_=self.x2t[n % 2][:]).then_inc(self.s4spx[n % 2], 16)
                e.activation(out=self.xn2[:], in_=self.x2t[n % 2][:],
                             func=AF.Square).then_inc(self.s4sq, 1)
                e.wait_ge(self.s4g, n + 1)
                e.activation(out=self.r2_c[:, n:n + 1], in_=self.st[:, 2:3],
                             func=AF.Sqrt).then_inc(self.s4r, 1)
                e.wait_ge(self.s4tr, n + 1)
                if n >= 2:
                    e.wait_ge(self.s4sp[n % 2], 16 * (n // 2))
                e.copy(out=self.st4[n % 2][:],
                       in_=self.pT[:]).then_inc(self.s4te, 1)
                e.wait_ge(self.s4te, n + 1)
                e.dma_start(out=self.xq_h[n].rearrange("i p q -> p i q"),
                            in_=self.st4[n % 2][:].rearrange("p (i q) -> p i q",
                                                             q=P)
                            ).then_inc(self.s4sp[n % 2], 16)

    def p5(self, eng, e):
        NF = 4
        NI = NF * NTT
        if eng == "sync":
            for par in range(2):
                e.wait_ge(self.s4sp[par], 16 * (NTT // 2))
            for fc in range(NF):
                for c in range(16):
                    for w_i, wh in ((0, self.w1t_h), (1, self.w2t_h)):
                        cc = (fc * 16 + c) * 2 + w_i
                        if cc >= 2:
                            e.wait_ge(self.s5sg, cc - 1)
                        if w_i == 1 and fc * 16 + c >= 1:
                            e.wait_ge(self.s5ac, fc * 16 + c)
                        e.dma_start(out=self.w5[cc % 2][:],
                                    in_=wh[c * P:(c + 1) * P,
                                           fc * D:(fc + 1) * D]
                                    ).then_inc(self.s5w[cc % 2], 16)
                for tau in range(NTT):
                    it = fc * NTT + tau
                    if it >= 2:
                        e.wait_ge(self.s5mm, it - 1)
                    e.dma_start(out=self.xqt[it % 2][:].rearrange(
                                    "p (i q) -> p i q", q=P),
                                in_=self.xq_h[tau].rearrange("i p q -> p i q")
                                ).then_inc(self.s5xq[it % 2], 16)
        elif eng == "scalar":
            for par in range(2):
                e.wait_ge(self.s4spx[par], 16 * (NTT // 2))
            e.wait_ge(self.s5s1c, 1)
            e.copy(out=self.st[0:1, 12:13],
                   in_=self.pAng[0:1, 0:1]).then_inc(self.s5s1c, 1)
            e.wait_ge(self.s5s1c, 3)
            e.activation(out=self.s1_col[:], in_=self.pAng[:, 2:3], func=AF.Copy,
                         scale=1.0 / (D * FF)).then_inc(self.s5s1c, 1)
            for fc in range(NF):
                if fc >= 1:
                    e.wait_ge(self.s5mm, fc * NTT)
                for c in range(16):
                    for w_i, dst in ((0, self.wA), (1, self.wB)):
                        cc = (fc * 16 + c) * 2 + w_i
                        e.wait_ge(self.s5w[cc % 2], 16 * (cc // 2 + 1))
                        e.sign(out=dst[:, c * D:(c + 1) * D],
                               in_=self.w5[cc % 2][:]).then_inc(self.s5sg, 1)
                for tau in range(NTT):
                    it = fc * NTT + tau
                    e.wait_ge(self.s5m0, it + 1)
                    if it >= 1:
                        e.wait_ge(self.s5h, 2 * it)
                    e.activation(out=self.z1s[:], in_=self.pQ[:, 0:1024],
                                 func=AF.Silu,
                                 scale=self.s1_col[:]).then_inc(self.s5e1, 1)
                    e.wait_ge(self.s5h, 2 * it + 1)
                    e.activation(out=self.z1s[:], in_=self.pQ[:, 1024:2048],
                                 func=AF.Silu,
                                 scale=self.s1_col[:]).then_inc(self.s5e1, 1)
                    if it >= 1:
                        self._p5_evac(e, it - 1)
                if fc == NF - 1:
                    self._p5_evac(e, NI - 1)
        elif eng == "tensor":
            e.matmul(out=self.pAng[0:1, 0:1], lhsT=self.s1acc[:],
                     rhs=self.ones_f[:], start=True,
                     stop=True).then_inc(self.s5s1c, 1)
            e.wait_ge(self.s5s1c, 2)
            e.matmul(out=self.pAng[:, 2:3], lhsT=self.onesrow[:],
                     rhs=self.st[0:1, 12:13], start=True,
                     stop=True).then_inc(self.s5s1c, 1)
            for fc in range(NF):
                for tau in range(NTT):
                    it = fc * NTT + tau
                    e.wait_ge(self.s5xq[it % 2], 16 * (it // 2 + 1))
                    e.wait_ge(self.s5sg, (fc * 16 + 16) * 2)
                    if it >= 1:
                        e.wait_ge(self.s5e1, 2 * it)
                    for i in range(16):
                        for oc in range(4):
                            mm = e.matmul(out=self.pQ[:, oc * 512:(oc + 1) * 512],
                                          lhsT=self.xqt[it % 2][:, i * P:(i + 1) * P],
                                          rhs=self.wA[:, i * D + oc * 512:
                                                      i * D + (oc + 1) * 512],
                                          start=(i == 0), stop=(i == 15))
                    mm.then_inc(self.s5m0, 1)
                    if it >= 1:
                        e.wait_ge(self.s5h, 2 * it)
                    for i in range(16):
                        for oc in range(2):
                            mm = e.matmul(out=self.pAng[:, oc * 512:(oc + 1) * 512],
                                          lhsT=self.xqt[it % 2][:, i * P:(i + 1) * P],
                                          rhs=self.wB[:, i * D + oc * 512:
                                                      i * D + (oc + 1) * 512],
                                          start=(i == 0), stop=(i == 15))
                    mm.then_inc(self.s5z0, 1)
                    if it >= 1:
                        self._p5_trans(e, it - 1)
                    e.wait_ge(self.s5h, 2 * it + 1)
                    for i in range(16):
                        for oc in range(2, 4):
                            mm = e.matmul(out=self.pAng[:, (oc - 2) * 512:
                                                        (oc - 1) * 512],
                                          lhsT=self.xqt[it % 2][:, i * P:(i + 1) * P],
                                          rhs=self.wB[:, i * D + oc * 512:
                                                      i * D + (oc + 1) * 512],
                                          start=(i == 0), stop=(i == 15))
                    mm.then_inc(self.s5mm, 1)
            self._p5_trans(e, NI - 1)
        elif eng == "vector":
            for par in range(2):
                e.wait_ge(self.s4spx[par], 16 * (NTT // 2))
            for fc in range(NF):
                for c in range(16):
                    j = fc * 16 + c
                    e.wait_ge(self.s5w[1], 16 * (j + 1))
                    e.reduce_sum(out=self.st[:, 4:5], in_=self.w5[1][:],
                                 axis=AX.X, apply_absolute_value=True)
                    e.tensor_tensor(out=self.s2acc[:], in0=self.s2acc[:],
                                    in1=self.st[:, 4:5],
                                    op=AluOpType.add).then_inc(self.s5ac, 1)
                for tau in range(NTT):
                    it = fc * NTT + tau
                    e.wait_ge(self.s5e1, 2 * it + 1)
                    e.wait_ge(self.s5z0, it + 1)
                    e.tensor_tensor(out=self.hh[:], in0=self.z1s[:],
                                    in1=self.pAng[:, 0:1024],
                                    op=AluOpType.mult).then_inc(self.s5h, 1)
                    if it >= 1:
                        e.wait_ge(self.s5tr, it)
                    self.quant_(e, self.hh[:], self.hqb[:, 0:1024],
                                self.k0i5[:], self.rr5[:], self.u15[:]
                                ).then_inc(self.s5q, 1)
                    e.wait_ge(self.s5e1, 2 * it + 2)
                    e.wait_ge(self.s5mm, it + 1)
                    e.tensor_tensor(out=self.hh[:], in0=self.z1s[:],
                                    in1=self.pAng[:, 0:1024],
                                    op=AluOpType.mult).then_inc(self.s5h, 1)
                    self.quant_(e, self.hh[:], self.hqb[:, 1024:2048],
                                self.k0i5[:], self.rr5[:], self.u15[:]
                                ).then_inc(self.s5q, 1)

    def _p5_trans(self, e, it):
        e.wait_ge(self.s5q, 2 * it + 2)
        if it >= 1:
            e.wait_ge(self.s5te, it)
        for c in range(16):
            tr = e.transpose(out=self.pT[:, c * P:(c + 1) * P],
                             in_=self.hqb[:, c * P:(c + 1) * P],
                             identity=self.ident[:])
        tr.then_inc(self.s5tr, 1)

    def _p5_evac(self, e, it):
        fc, tau = divmod(it, NTT)
        e.wait_ge(self.s5tr, it + 1)
        if it >= 2:
            e.wait_ge(self.s5sp[it % 2], 16 * (it // 2))
        e.copy(out=self.st5[it % 2][:], in_=self.pT[:]).then_inc(self.s5te, 1)
        e.wait_ge(self.s5te, it + 1)
        e.dma_start(out=self.hq_h[fc * 16:(fc + 1) * 16, tau].rearrange(
                        "i p q -> p i q"),
                    in_=self.st5[it % 2][:].rearrange("p (i q) -> p i q", q=P)
                    ).then_inc(self.s5sp[it % 2], 16)

    def p6(self, eng, e):
        if eng == "sync":
            for c in range(64):
                if c >= 2:
                    e.wait_ge(self.s6sg, c - 1)
                    e.wait_ge(self.s6ac, c - 1)
                e.dma_start(out=self.w6[c % 2][:],
                            in_=self.w3t_h[(c % 64) * P:(c % 64 + 1) * P, :]
                            ).then_inc(self.s6w[c % 2], 16)
        elif eng == "scalar":
            for c in range(64):
                e.wait_ge(self.s6w[c % 2], 16 * (c // 2 + 1))
                if c >= 2:
                    e.wait_ge(self.s6sp[c % 2], 16 * (c // 2))
                e.sign(out=self.w3sb[c % 2][:],
                       in_=self.w6[c % 2][:]).then_inc(self.s6sg, 1)
                e.wait_ge(self.s6sg, c + 1)
                e.dma_start(out=self.w3s_h[c],
                            in_=self.w3sb[c % 2][:]).then_inc(self.s6sp[c % 2], 16)
            e.wait_ge(self.s6sc, 1)
            e.copy(out=self.st[0:1, 10:11], in_=self.pAng[0:1, 0:1])
            e.copy(out=self.st[0:1, 11:12],
                   in_=self.pAng[0:1, 1:2]).then_inc(self.s6sc, 1)
            e.wait_ge(self.s6sc, 4)
            e.copy(out=self.s23_col[:],
                   in_=self.pAng[:, 4:5]).then_inc(self.s6sc, 1)
        elif eng == "vector":
            for c in range(64):
                e.wait_ge(self.s6w[c % 2], 16 * (c // 2 + 1))
                e.reduce_sum(out=self.st[:, 4:5], in_=self.w6[c % 2][:],
                             axis=AX.X, apply_absolute_value=True)
                e.tensor_tensor(out=self.s3acc[:], in0=self.s3acc[:],
                                in1=self.st[:, 4:5],
                                op=AluOpType.add).then_inc(self.s6ac, 1)
            e.wait_ge(self.s6sc, 2)
            e.tensor_tensor(out=self.st[0:1, 10:11], in0=self.st[0:1, 10:11],
                            in1=self.st[0:1, 11:12], op=AluOpType.mult)
            e.tensor_scalar(out=self.st[0:1, 10:11], in0=self.st[0:1, 10:11],
                            scalar1=1.0 / (D * FF), scalar2=1.0 / (D * FF),
                            op0=AluOpType.mult,
                            op1=AluOpType.mult).then_inc(self.s6sc, 1)
        elif eng == "tensor":
            e.wait_ge(self.s6ac, 64)
            e.matmul(out=self.pAng[0:1, 0:1], lhsT=self.s2acc[:],
                     rhs=self.ones_f[:], start=True, stop=True)
            e.matmul(out=self.pAng[0:1, 1:2], lhsT=self.s3acc[:],
                     rhs=self.ones_f[:], start=True,
                     stop=True).then_inc(self.s6sc, 1)
            e.wait_ge(self.s6sc, 3)
            e.matmul(out=self.pAng[:, 4:5], lhsT=self.onesrow[:],
                     rhs=self.st[0:1, 10:11], start=True,
                     stop=True).then_inc(self.s6sc, 1)

    def p7(self, eng, e):
        NSUP = 4
        NFC = 4
        if eng == "sync":
            for it in range(NSUP * NFC):
                sup, fc = divmod(it, NFC)
                if it >= 1:
                    e.wait_ge(self.s7mm, it * 8)
                e.dma_start(out=self.wA[:].rearrange("p (j o) -> p j o", o=D),
                            in_=self.w3s_h[fc * 16:(fc + 1) * 16].rearrange(
                                "j p o -> p j o")).then_inc(self.s7w[0], 16)
        elif eng == "gpsimd":
            for par in range(2):
                e.wait_ge(self.s6sp[par], 16 * 32)
                e.wait_ge(self.s5sp[par], 16 * (4 * NTT // 2))
            for it in range(NSUP * NFC):
                sup, fc = divmod(it, NFC)
                for tl in range(8):
                    tau = sup * 8 + tl
                    k = it * 8 + tl
                    if k >= 2:
                        e.wait_ge(self.s7mm, k - 1)
                    e.dma_start(out=self.hqt[k % 2][:].rearrange(
                                    "p (j q) -> p j q", q=P),
                                in_=self.hq_h[fc * 16:(fc + 1) * 16, tau].rearrange(
                                    "j p q -> p j q")
                                ).then_inc(self.s7h[k % 2], 16)
        elif eng == "tensor":
            for it in range(NSUP * NFC):
                sup, fc = divmod(it, NFC)
                e.wait_ge(self.s7w[0], 16 * (it + 1))
                for tl in range(8):
                    tau = sup * 8 + tl
                    k = it * 8 + tl
                    e.wait_ge(self.s7h[k % 2], 16 * (k // 2 + 1))
                    if k >= 2:
                        e.wait_ge(self.s7ac, k - 1)
                    par = k % 2
                    for j in range(16):
                        for oc in range(4):
                            if par == 0:
                                dst = self.pQ[:, oc * 512:(oc + 1) * 512]
                            else:
                                dst = (self.pAng[:, (oc % 2) * 512:(oc % 2 + 1) * 512]
                                       if oc < 2 else
                                       self.pTf[:, (oc % 2) * 512:(oc % 2 + 1) * 512])
                            mm = e.matmul(out=dst,
                                          lhsT=self.hqt[k % 2][:, j * P:(j + 1) * P],
                                          rhs=self.wA[:, j * D + oc * 512:
                                                      j * D + (oc + 1) * 512],
                                          start=(j == 0), stop=(j == 15))
                    mm.then_inc(self.s7mm, 1)
        elif eng == "vector":
            for it in range(NSUP * NFC):
                sup, fc = divmod(it, NFC)
                for tl in range(8):
                    tau = sup * 8 + tl
                    k = it * 8 + tl
                    e.wait_ge(self.s7mm, k + 1)
                    par = k % 2
                    aslc = self.acc[:, tl * D:(tl + 1) * D]
                    if par == 0:
                        srcs = [self.pQ[:, 0:1024], self.pQ[:, 1024:2048]]
                    else:
                        srcs = [self.pAng[:, 0:1024], self.pTf[:, 0:1024]]
                    for half in range(2):
                        a2 = aslc[:, half * 1024:(half + 1) * 1024]
                        if fc == 0:
                            ai = e.tensor_copy(out=a2, in_=srcs[half])
                        else:
                            ai = e.tensor_tensor(out=a2, in0=a2, in1=srcs[half],
                                                 op=AluOpType.add)
                    ai.then_inc(self.s7ac, 1)
                if fc == NFC - 1:
                    for tl in range(8):
                        tau = sup * 8 + tl
                        k2 = sup * 8 + tl
                        e.wait_ge(self.s7x2[k2 % 2], 16 * (k2 // 2 + 1))
                        if k2 >= 2:
                            e.wait_ge(self.s7out[k2 % 2], 16 * (k2 // 2))
                        e.scalar_tensor_tensor(out=self.outb[k2 % 2][:],
                                               in0=aslc_get(self, tl),
                                               scalar=self.s23_col[:],
                                               in1=self.x2r[k2 % 2][:],
                                               op0=AluOpType.mult,
                                               op1=AluOpType.add
                                               ).then_inc(self.s7ev, 1)
        elif eng == "scalar":
            def outd(k2):
                s_, t_ = divmod(k2, NT)
                e.wait_ge(self.s7ev, k2 + 1)
                e.dma_start(out=self.out_h[s_, t_ * P:(t_ + 1) * P, :],
                            in_=self.outb[k2 % 2][:]
                            ).then_inc(self.s7out[k2 % 2], 16)
            for par in range(2):
                e.wait_ge(self.s4spx[par], 16 * (NTT // 2))
                e.wait_ge(self.s6sp[par], 16 * 32)
                e.wait_ge(self.s5sp[par], 16 * (4 * NTT // 2))
            for k2 in range(NTT):
                if k2 >= 2:
                    outd(k2 - 2)
                    e.wait_ge(self.s7ev, k2 - 1)
                s_, t_ = divmod(k2, NT)
                e.dma_start(out=self.x2r[k2 % 2][:],
                            in_=self.x2_h[s_, t_ * P:(t_ + 1) * P, :]
                            ).then_inc(self.s7x2[k2 % 2], 16)
            outd(NTT - 2)
            outd(NTT - 1)
            e.wait_ge(self.s7out[0], 16 * 16)
            e.wait_ge(self.s7out[1], 16 * 16)


def aslc_get(b, tl):
    return b.acc[:, tl * D:(tl + 1) * D]


def build():
    nc = bass.Bass("TRN2", target_bir_lowering=False)
    b = B(nc)
    ENG = ["sync", "gpsimd", "vector", "scalar", "tensor"]

    def phase(fn):
        with nc.Block() as block:
            for eng in ENG:
                dec = getattr(block, eng)

                def mk(eng=eng):
                    def _(e):
                        if eng in ("vector", "scalar", "gpsimd"):
                            e = EP(e)
                        fn(eng, e)
                    return _
                dec(mk())

    def g_a(eng, e):
        if eng in ("vector", "gpsimd"):
            b.init_(eng, e)
        b.p0(eng, e)
        b.p05(eng, e)
        for w in ("q", "k", "v"):
            b.p1(eng, e, w)
    phase(g_a)
    phase(lambda eng, e: b.p3(eng, e, 0))
    phase(lambda eng, e: b.p4(eng, e, 0))
    phase(lambda eng, e: b.p3(eng, e, 1))
    phase(lambda eng, e: b.p4(eng, e, 1))
    phase(lambda eng, e: b.p5(eng, e))
    phase(lambda eng, e: b.p6(eng, e))
    phase(lambda eng, e: b.p7(eng, e))
    return nc, b


def kernel(L0_attn_x, L0_attn_Wq, L0_attn_Wk, L0_attn_Wv, L0_attn_Wo,
           L0_ffn_w1, L0_ffn_w2, L0_ffn_w3):
    x = np.ascontiguousarray(np.asarray(L0_attn_x, dtype=np.float32))
    wqt = np.ascontiguousarray(np.asarray(L0_attn_Wq, dtype=np.float32).T)
    wkt = np.ascontiguousarray(np.asarray(L0_attn_Wk, dtype=np.float32).T)
    wvt = np.ascontiguousarray(np.asarray(L0_attn_Wv, dtype=np.float32).T)
    wot = np.ascontiguousarray(np.asarray(L0_attn_Wo, dtype=np.float32).T)
    w1t = np.ascontiguousarray(np.asarray(L0_ffn_w1, dtype=np.float32).T)
    w2t = np.ascontiguousarray(np.asarray(L0_ffn_w2, dtype=np.float32).T)
    w3t = np.ascontiguousarray(np.asarray(L0_ffn_w3, dtype=np.float32).T)
    nc, _ = build()
    in_maps = [{"x": x[2 * c:2 * c + 2], "wqt": wqt, "wkt": wkt, "wvt": wvt,
                "wot": wot, "w1t": w1t, "w2t": w2t, "w3t": w3t}
               for c in range(8)]
    res = run_bass_kernel_spmd(nc, in_maps, core_ids=list(range(8)))
    return np.concatenate([np.asarray(r["out"]) for r in res.results], axis=0)


# version 05 (final)
# speedup vs baseline: 1.1587x; 1.1587x over previous
import math
import numpy as np
import concourse.bass as bass
import concourse.mybir as mybir
from concourse.alu_op_type import AluOpType
from concourse.bass_utils import run_bass_kernel_spmd

F32 = mybir.dt.float32
BF16 = mybir.dt.bfloat16
I32 = mybir.dt.int32
AF = mybir.ActivationFunctionType
AX = mybir.AxisListType

P = 128
D = 2048
S = 2048
NT = 16
NS = 2
H = 16
FF = 8192
NTT = NS * NT
GQ = 64

SC_ATT = 1.0 / math.sqrt(128.0)
EPS = 1e-5
C2HI = float(np.float32(2.0 * math.pi))
C2LO = float(np.float32(2.0 * math.pi - np.float64(np.float32(2.0 * math.pi))))
INV2PI = float(np.float32(1.0 / (2.0 * math.pi)))
PI_CLAMP = 3.1415925
LNTH = math.log(10000.0) / 64.0
LNHI = float(np.float32(LNTH))
LNLO = float(np.float32(LNTH - np.float64(np.float32(LNTH))))

_DBG = False


def _span(ap):
    try:
        t = ap.tensor
        tn = type(t).__name__
        if "SB" not in tn and "PSum" not in tn:
            return None
        isz = mybir.dt.size(ap.dtype)
        lo = ap.offset * isz
        ext = 1
        for step, cnt in ap.ap[1:]:
            ext += abs(step) * (cnt - 1)
        return (t.name, lo, lo + ext * isz)
    except Exception:
        return None


class EP:

    def __init__(self, e):
        self.e = e
        self.w = []
        self.r = []

    @staticmethod
    def _hits(span, lst):
        if span is None:
            return False
        n, lo, hi = span
        for (n2, lo2, hi2) in lst:
            if n == n2 and lo < hi2 and lo2 < hi:
                return True
        return False

    def _go(self, outs, ins, f, *a, **k):
        on = [s for s in (_span(o) for o in outs) if s]
        inn = [s for s in (_span(i) for i in ins
                           if not isinstance(i, (int, float, type(None)))) if s]
        if any(self._hits(s, self.w) for s in inn) or \
           any(self._hits(s, self.w) or self._hits(s, self.r) for s in on):
            self.e.drain()
            self.w, self.r = [], []
        self.w.extend(on)
        self.r.extend(inn)
        return f(*a, **k)

    def wait_ge(self, sem, v):
        return self.e.wait_ge(sem, v)

    def drain(self):
        self.w, self.r = [], []
        return self.e.drain()

    def nop(self):
        return self.e.nop()

    def dma_start(self, **k):
        return self.e.dma_start(**k)

    def memset(self, ap, c):
        return self._go([ap], [], self.e.memset, ap, c)

    def tensor_copy(self, out, in_):
        return self._go([out], [in_], self.e.tensor_copy, out=out, in_=in_)

    def tensor_tensor(self, out, in0, in1, op):
        return self._go([out], [in0, in1], self.e.tensor_tensor, out=out,
                        in0=in0, in1=in1, op=op)

    def tensor_scalar(self, out, in0, scalar1, scalar2, op0, op1=None,
                      accum_out=None):
        kw = dict(out=out, in0=in0, scalar1=scalar1, scalar2=scalar2, op0=op0)
        if op1 is not None:
            kw["op1"] = op1
        if accum_out is not None:
            kw["accum_out"] = accum_out
        return self._go([out, accum_out], [in0, scalar1, scalar2],
                        self.e.tensor_scalar, **kw)

    def scalar_tensor_tensor(self, out, in0, scalar, in1, op0, op1):
        return self._go([out], [in0, scalar, in1], self.e.scalar_tensor_tensor,
                        out=out, in0=in0, scalar=scalar, in1=in1, op0=op0,
                        op1=op1)

    def reduce_sum(self, out, in_, axis, apply_absolute_value=None):
        kw = dict(out=out, in_=in_, axis=axis)
        if apply_absolute_value is not None:
            kw["apply_absolute_value"] = apply_absolute_value
        return self._go([out], [in_], self.e.reduce_sum, **kw)

    def reduce_max(self, out, in_, axis, negate=None):
        kw = dict(out=out, in_=in_, axis=axis)
        if negate is not None:
            kw["negate"] = negate
        return self._go([out], [in_], self.e.reduce_max, **kw)

    def tensor_reduce(self, out, in_, axis, op, apply_absolute_value=None):
        kw = dict(out=out, in_=in_, axis=axis, op=op)
        if apply_absolute_value is not None:
            kw["apply_absolute_value"] = apply_absolute_value
        return self._go([out], [in_], self.e.tensor_reduce, **kw)

    def reciprocal(self, out, in_):
        return self._go([out], [in_], self.e.reciprocal, out=out, in_=in_)

    def activation(self, out, in_, func, bias=None, scale=None, accum_out=None):
        kw = dict(out=out, in_=in_, func=func)
        if bias is not None:
            kw["bias"] = bias
        if scale is not None:
            kw["scale"] = scale
        if accum_out is not None:
            kw["accum_out"] = accum_out
        return self._go([out, accum_out], [in_, bias, scale],
                        self.e.activation, **kw)

    def copy(self, out, in_):
        return self._go([out], [in_], self.e.copy, out=out, in_=in_)

    def sign(self, out, in_):
        return self._go([out], [in_], self.e.sign, out=out, in_=in_)

    def iota(self, out, pattern, base, channel_multiplier,
             allow_small_or_imprecise_dtypes=True):
        return self._go([out], [], self.e.iota, out=out, pattern=pattern,
                        base=base, channel_multiplier=channel_multiplier,
                        allow_small_or_imprecise_dtypes=allow_small_or_imprecise_dtypes)

    def affine_select(self, out, in_, pattern, compare_op, fill, base,
                      channel_multiplier):
        return self._go([out], [in_], self.e.affine_select, out=out, in_=in_,
                        pattern=pattern, compare_op=compare_op, fill=fill,
                        base=base, channel_multiplier=channel_multiplier)


ARENA_COLS = 30720


class B:
    def __init__(self, nc):
        self.nc = nc
        d = nc.dram_tensor
        self.x_h = d("x", [NS, S, D], F32, kind="ExternalInput").ap()
        self.wqt_h = d("wqt", [D, D], F32, kind="ExternalInput").ap()
        self.wkt_h = d("wkt", [D, D], F32, kind="ExternalInput").ap()
        self.wvt_h = d("wvt", [D, D], F32, kind="ExternalInput").ap()
        self.wot_h = d("wot", [D, D], F32, kind="ExternalInput").ap()
        self.w1t_h = d("w1t", [D, FF], F32, kind="ExternalInput").ap()
        self.w2t_h = d("w2t", [D, FF], F32, kind="ExternalInput").ap()
        self.w3t_h = d("w3t", [FF, D], F32, kind="ExternalInput").ap()
        self.out_h = d("out", [NS, S, D], F32, kind="ExternalOutput").ap()
        ks = "ExternalOutput" if _DBG else "Internal"
        self.xt_h = d("xt_s", [NS, NT, 16, P, P], BF16, kind=ks).ap()
        self.tab_h = d("tab_s", [NT, 2, P, H * GQ], F32, kind=ks).ap()
        self.q_h = d("q_s", [NS, H, NT, P, P], BF16, kind=ks).ap()
        self.k_h = d("k_s", [NS, H, NT, P, P], BF16, kind=ks).ap()
        self.v_h = d("v_s", [NS, H, NT, P, P], BF16, kind=ks).ap()
        self.x2_h = d("x2_s", [NS, S, D], F32, kind=ks).ap()
        self.xq_h = d("xq_s", [NTT, 16, P, P], BF16, kind=ks).ap()
        self.hq_h = d("hq_s", [64, NTT, P, P], BF16, kind=ks).ap()
        self.w3s_h = d("w3s_s", [64, P, D], BF16, kind=ks).ap()

        sb = nc.alloc_sbuf_tensor
        self.wA = sb("wA", [P, 16 * D], BF16)
        self.arena = sb("arena", [P, ARENA_COLS], F32)
        self.ident = sb("ident", [P, P], BF16)
        self.ident_f = sb("ident_f", [P, P], F32)
        self.zero_bf = sb("zero_bf", [P, P], BF16)
        self.mask_neg = sb("mask_neg", [P, P], F32)
        self.mask_bf = sb("mask_bf", [P, P], BF16)
        self.ones_bf = sb("ones_bf", [P, 1], BF16)
        self.ones_f = sb("ones_f", [P, 1], F32)
        self.onesrow = sb("onesrow", [1, P], F32)
        self.negm_row = sb("negm_row", [1, P * NTT], BF16)
        self.negm_c = sb("negm_c", [P, NTT], F32)
        self.r_c = sb("r_c", [P, NTT], F32)
        self.negm2_c = sb("negm2_c", [P, NTT], F32)
        self.r2_c = sb("r2_c", [P, NTT], F32)
        self.s1acc = sb("s1acc", [P, 1], F32)
        self.s2acc = sb("s2acc", [P, 1], F32)
        self.s3acc = sb("s3acc", [P, 1], F32)
        self.s1_col = sb("s1_col", [P, 1], F32)
        self.s23_col = sb("s23_col", [P, 1], F32)
        self.csw = sb("csw", [1, D], BF16)
        self.st = sb("st", [P, 16], F32)
        self.qs = sb("qs", [P, 4 * GQ], F32)
        self.ssum = sb("ssum", [P, 4], F32)
        self.sbias = sb("sbias", [P, 4], F32)
        self.srs = sb("srs", [P, 4], F32)
        self.posr = sb("posr", [1, P], F32)
        self.dum = sb("dum", [1, 2], F32)
        self.cst_b = sb("cst_b", [P, P], BF16)
        self.cst_f = sb("cst_f", [P, P], F32)
        self.cst_z = sb("cst_z", [P, P], F32)

        def cv(off, cols, dt=F32):
            a = self.arena[:, off:off + cols]
            if dt is F32:
                return a
            return a.bitcast(dt)
        self.xin = [cv(0, 2048), cv(2048, 2048)]
        self.ptok = cv(4096, 2048)
        self.sqb = self.ptok
        self.wstr = [cv(6144, 2048), cv(8192, 2048)]
        self.invf_full = cv(6144, 1024)
        self.stage = [cv(10240, 1024, BF16), cv(11264, 1024, BF16)]
        self.xbf = cv(12288, 1024, BF16)
        self.xtt = [cv(13312, 1024, BF16), cv(14336, 1024, BF16)]
        self.qbf = cv(15360, 1024, BF16)
        self.tA = cv(16384, 1024)
        self.tB = cv(17408, 1024)
        self.tC = cv(18432, 1024)
        self.tki = cv(19456, 1024, I32)
        self.cosb = cv(20480, 1024)
        self.sinb = cv(21504, 1024)
        self.ctab = [cv(22528, 1024), cv(23552, 1024)]
        self.stab = [cv(24576, 1024), cv(25600, 1024)]
        self.re0 = cv(26624, 1024)
        self.re1 = cv(27648, 1024)
        self.re2 = cv(28672, 1024)
        self.re3 = cv(29696, 1024)
        self.wB = cv(0, 16384, BF16)
        self.Kb = [cv(16384, 1024, BF16), cv(17408, 1024, BF16)]
        self.Qb = [cv(18432, 1024, BF16), cv(19456, 1024, BF16)]
        self.Vb = [cv(20480, 1024, BF16), cv(21504, 1024, BF16)]
        self.probsA = [cv(22528 + 1024 * i, 1024, BF16) for i in range(4)]
        self.probsB = [cv(26624 + 768 * i, 768, BF16) for i in range(4)]
        self.pTe = [cv(29696, 512, BF16), cv(30208, 512, BF16)]
        self.xres = cv(16384, 2048)
        self.x2t = [cv(18432, 2048), cv(20480, 2048)]
        self.xn2 = cv(22528, 2048)
        self.k0i4 = cv(24576, 1024, I32)
        self.rr4 = cv(25600, 1024)
        self.u14 = cv(26624, 1024)
        self.xqb = cv(27648, 1024, BF16)
        self.st4 = [cv(28672, 1024, BF16), cv(29696, 1024, BF16)]
        self.xqt = [cv(16384, 1024, BF16), cv(17408, 1024, BF16)]
        self.z1s = cv(18432, 1024)
        self.hh = cv(19456, 1024)
        self.k0i5 = cv(20480, 1024, I32)
        self.rr5 = cv(21504, 1024)
        self.u15 = cv(22528, 1024)
        self.hqb = cv(23552, 1024, BF16)
        self.st5 = [cv(24576, 1024, BF16), cv(25600, 1024, BF16)]
        self.w5 = [cv(26624, 2048), cv(28672, 2048)]
        self.w6 = [cv(16384, 2048), cv(18432, 2048)]
        self.w3sb = [cv(20480, 1024, BF16), cv(21504, 1024, BF16)]
        self.acc = cv(0, 16384)
        self.hqt = [cv(16384, 1024, BF16), cv(17408, 1024, BF16)]
        self.x2r = [cv(18432, 2048), cv(20480, 2048)]
        self.outb = [cv(22528, 2048), cv(24576, 2048)]

        ps = nc.alloc_psum_tensor
        self.pQ = ps("pQ", [P, D], F32)
        self.pT = ps("pT", [P, D], BF16)
        self.pAng = ps("pAng", [P, 1024], F32)
        self.pQb = self.pQ[:].bitcast(BF16)
        self.pTf = self.pT[:].bitcast(F32)

        for name in ["s0sq", "s0sqr", "s0g", "s0r", "s0xb", "s0tp",
                     "s0ev", "s0w1r", "sIv", "sTps", "sTan", "sTev",
                     "sTrr", "sTsc", "s1w", "s1cs", "s1cse",
                     "s1mm", "s1ev", "s1rp", "s1tr", "s1te",
                     "s3dn", "s3sc", "s3mx", "s3ex", "s3pn", "s3mg", "s3me", "s3tp", "s3tev",
                     "s3av", "s3ae",
                     "s4w", "s4mm", "s4x2", "s4sq", "s4g", "s4r", "s4q",
                     "s4tr", "s4te",
                     "s5sg", "s5ac", "s5mm", "s5m0", "s5z0", "s5z1", "s5e1", "s5h", "s5q", "s5tr", "s5te",
                     "s5s1c",
                     "s6sg", "s6ac", "s6sc",
                     "s7mm", "s7ac", "s7ev"]:
            setattr(self, name, nc.alloc_semaphore(name))
        for name in ["s0x", "s0sp", "s0w1", "sTsp", "s1xt", "s1tb", "s1sp",
                     "s3kv", "s4x", "s4spx", "s4sp", "s5w", "s5xq", "s5sp",
                     "s6w", "s6sp", "s7w", "s7h", "s7x2", "s7out"]:
            setattr(self, name, [nc.alloc_semaphore(name + "A"),
                                 nc.alloc_semaphore(name + "B")])

    def round_(self, e, t, k0i, r, u1, out):
        e.tensor_copy(out=k0i, in_=t)
        e.tensor_tensor(out=r, in0=t, in1=k0i, op=AluOpType.subtract)
        e.tensor_scalar(out=u1, in0=r, scalar1=0.5, scalar2=-1.0,
                        op0=AluOpType.is_ge, op1=AluOpType.add)
        e.tensor_tensor(out=u1, in0=u1, in1=k0i, op=AluOpType.add)
        e.scalar_tensor_tensor(out=out, in0=r, scalar=-0.5, in1=u1,
                               op0=AluOpType.is_ge, op1=AluOpType.add)

    def quant_(self, e, src, dst, k0i, r, u1):
        g = 16
        sg = self.qs[:, 0:g]
        inv = self.qs[:, g:2 * g]
        sinv = self.qs[:, 2 * g:3 * g]
        e.tensor_reduce(out=sg, in_=src.rearrange("p (g q) -> p g q", q=GQ),
                        axis=AX.X, op=AluOpType.max, apply_absolute_value=True)
        e.tensor_scalar(out=sg, in0=sg, scalar1=EPS, scalar2=None,
                        op0=AluOpType.max)
        e.reciprocal(out=inv, in_=sg)
        e.tensor_scalar(out=inv, in0=inv, scalar1=127.0, scalar2=None,
                        op0=AluOpType.mult)
        e.tensor_scalar(out=sinv, in0=sg, scalar1=1.0 / 127.0, scalar2=None,
                        op0=AluOpType.mult)
        ib = inv.unsqueeze(2).broadcast_to([P, g, GQ])
        sb_ = sinv.unsqueeze(2).broadcast_to([P, g, GQ])
        s3 = src.rearrange("p (g q) -> p g q", q=GQ)
        t = r
        e.tensor_tensor(out=t.rearrange("p (g q) -> p g q", q=GQ), in0=s3,
                        in1=ib, op=AluOpType.mult)
        e.tensor_copy(out=k0i, in_=t)
        e.tensor_tensor(out=u1, in0=t, in1=k0i, op=AluOpType.subtract)
        e.tensor_scalar(out=t, in0=u1, scalar1=0.5, scalar2=-1.0,
                        op0=AluOpType.is_ge, op1=AluOpType.add)
        e.tensor_tensor(out=t, in0=t, in1=k0i, op=AluOpType.add)
        e.scalar_tensor_tensor(out=t, in0=u1, scalar=-0.5, in1=t,
                               op0=AluOpType.is_ge, op1=AluOpType.add)
        return e.tensor_tensor(out=dst.rearrange("p (g q) -> p g q", q=GQ),
                               in0=t.rearrange("p (g q) -> p g q", q=GQ),
                               in1=sb_, op=AluOpType.mult)

    def init_(self, eng, e):
        if eng == "vector":
            e.memset(self.ones_bf[:], 1.0)
            e.memset(self.ones_f[:], 1.0)
            e.memset(self.zero_bf[:], 0.0)
            e.memset(self.s1acc[:], 0.0)
            e.memset(self.s2acc[:], 0.0)
            e.memset(self.s3acc[:], 0.0)
            e.memset(self.onesrow[:], 1.0)
            e.memset(self.cst_b[:], 1.0)
            e.memset(self.cst_f[:], 1.0)
            e.memset(self.cst_z[:], 0.0).then_inc(self.sIv, 1)
        elif eng == "gpsimd":
            e.wait_ge(self.sIv, 1)
            e.affine_select(out=self.ident[:], in_=self.cst_b[:],
                            pattern=[[-1, P]], compare_op=AluOpType.is_equal,
                            fill=0.0, base=0, channel_multiplier=1)
            e.affine_select(out=self.ident_f[:], in_=self.cst_f[:],
                            pattern=[[-1, P]], compare_op=AluOpType.is_equal,
                            fill=0.0, base=0, channel_multiplier=1)
            e.affine_select(out=self.mask_neg[:], in_=self.cst_z[:],
                            pattern=[[-1, P]], compare_op=AluOpType.is_ge,
                            fill=-3.0e38, base=0, channel_multiplier=1)
            e.affine_select(out=self.mask_bf[:], in_=self.zero_bf[:],
                            pattern=[[-1, P]], compare_op=AluOpType.is_ge,
                            fill=-3.0e38, base=0, channel_multiplier=1)
            e.memset(self.dum[:], 0.0).then_inc(self.sIv, 1)

    def p0(self, eng, e):
        if eng == "sync":
            def spill(m):
                e.wait_ge(self.s0ev, m + 1)
                e.dma_start(out=self.xt_h[m // NT, m % NT].rearrange("i p e -> p i e"),
                            in_=self.stage[m % 2][:].rearrange("p (i e) -> p i e", e=P)
                            ).then_inc(self.s0sp[m % 2], 16)
            for n in range(NTT):
                s, t = divmod(n, NT)
                if n >= 2:
                    spill(n - 2)
                    e.wait_ge(self.s0xb, n - 1)
                    e.wait_ge(self.s0sq, n - 1)
                e.dma_start(out=self.xin[n % 2][:],
                            in_=self.x_h[s, t * P:(t + 1) * P, :]
                            ).then_inc(self.s0x[n % 2], 16)
            spill(NTT - 2)
            spill(NTT - 1)
        elif eng == "scalar":
            e.wait_ge(self.sIv, 2)
            for n in range(NTT):
                for c in range(2 * n, 2 * n + 2):
                    if c >= 2:
                        e.wait_ge(self.s0w1r, c - 1)
                    e.dma_start(out=self.wstr[c % 2][:],
                                in_=self.w1t_h[(c % 16) * P:(c % 16 + 1) * P,
                                               (c // 16) * D:(c // 16 + 1) * D]
                                ).then_inc(self.s0w1[c % 2], 16)
                e.wait_ge(self.s0x[n % 2], 16 * (n // 2 + 1))
                if n >= 1:
                    e.wait_ge(self.s0sqr, n)
                e.activation(out=self.sqb[:], in_=self.xin[n % 2][:],
                             func=AF.Square).then_inc(self.s0sq, 1)
                e.wait_ge(self.s0g, n + 1)
                e.activation(out=self.r_c[:, n:n + 1], in_=self.st[:, 2:3],
                             func=AF.Sqrt).then_inc(self.s0r, 1)
                e.wait_ge(self.s0tp, n + 1)
                if n >= 2:
                    e.wait_ge(self.s0sp[n % 2], 16 * (n // 2))
                e.copy(out=self.stage[n % 2][:], in_=self.pT[:])
                e.copy(out=self.negm_row[0:1, n * P:(n + 1) * P],
                       in_=self.pAng[0:1, 0:P]).then_inc(self.s0ev, 1)
        elif eng == "gpsimd":
            pass
        elif eng == "vector":
            for n in range(NTT):
                e.wait_ge(self.s0x[n % 2], 16 * (n // 2 + 1))
                e.reduce_sum(out=self.st[:, 0:1], in_=self.xin[n % 2][:], axis=AX.X)
                if n >= 1:
                    e.wait_ge(self.s0tp, n)
                e.tensor_copy(out=self.xbf[:], in_=self.xin[n % 2][:]
                              ).then_inc(self.s0xb, 1)
                e.tensor_scalar(out=self.negm_c[:, n:n + 1], in0=self.st[:, 0:1],
                                scalar1=-1.0 / D, scalar2=None, op0=AluOpType.mult)
                e.wait_ge(self.s0sq, n + 1)
                e.reduce_sum(out=self.st[:, 1:2], in_=self.sqb[:],
                             axis=AX.X).then_inc(self.s0sqr, 1)
                e.tensor_tensor(out=self.st[:, 3:4], in0=self.negm_c[:, n:n + 1],
                                in1=self.negm_c[:, n:n + 1], op=AluOpType.mult)
                if n >= 1:
                    e.wait_ge(self.s0r, n)
                e.tensor_scalar(out=self.st[:, 2:3], in0=self.st[:, 1:2],
                                scalar1=1.0 / D, scalar2=None, op0=AluOpType.mult)
                e.tensor_tensor(out=self.st[:, 2:3], in0=self.st[:, 2:3],
                                in1=self.st[:, 3:4], op=AluOpType.subtract)
                e.tensor_scalar(out=self.st[:, 2:3], in0=self.st[:, 2:3],
                                scalar1=EPS, scalar2=None, op0=AluOpType.add)
                e.reciprocal(out=self.st[:, 2:3],
                             in_=self.st[:, 2:3]).then_inc(self.s0g, 1)
                for c in range(2 * n, 2 * n + 2):
                    e.wait_ge(self.s0w1[c % 2], 16 * (c // 2 + 1))
                    e.reduce_sum(out=self.st[:, 4:5], in_=self.wstr[c % 2][:],
                                 axis=AX.X, apply_absolute_value=True)
                    e.tensor_tensor(out=self.s1acc[:], in0=self.s1acc[:],
                                    in1=self.st[:, 4:5],
                                    op=AluOpType.add).then_inc(self.s0w1r, 1)
        elif eng == "tensor":
            e.wait_ge(self.sIv, 2)
            for n in range(NTT):
                e.wait_ge(self.s0xb, n + 1)
                if n >= 1:
                    e.wait_ge(self.s0ev, n)
                for c in range(16):
                    e.transpose(out=self.pT[:, c * P:(c + 1) * P],
                                in_=self.xbf[:, c * P:(c + 1) * P],
                                identity=self.ident[:])
                e.wait_ge(self.s0g, n + 1)
                e.transpose(out=self.pAng[0:1, 0:P],
                            in_=self.negm_c[:, n:n + 1],
                            identity=self.ident_f[:]).then_inc(self.s0tp, 1)

    def p05(self, eng, e):
        pos = self.st[:, 8:9]
        if eng == "gpsimd":
            e.wait_ge(self.s0w1r, 2 * NTT)
            e.iota(out=self.tA[:].rearrange("p (h q) -> p h q", q=GQ),
                   pattern=[[0, H], [1, GQ]], base=0, channel_multiplier=0,
                   allow_small_or_imprecise_dtypes=True).then_inc(self.sIv, 1)
            for t in range(NT):
                if t >= 1:
                    e.wait_ge(self.sTan, t)
                e.iota(out=pos, pattern=[[0, 1]], base=t * P,
                       channel_multiplier=1,
                       allow_small_or_imprecise_dtypes=True).then_inc(self.sTps, 1)
        elif eng == "scalar":
            e.wait_ge(self.sIv, 4)
            e.activation(out=self.invf_full[:], in_=self.tC[:],
                         func=AF.Exp).then_inc(self.sIv, 1)
            for t in range(NT):
                e.wait_ge(self.sTrr, 2 * t + 2)
                if t >= 1:
                    e.wait_ge(self.sTsp[0], 16 * t)
                    e.wait_ge(self.sTsp[1], 16 * t)
                e.activation(out=self.sinb[:], in_=self.tB[:], func=AF.Sin)
                e.activation(out=self.cosb[:], in_=self.tC[:],
                             func=AF.Sin).then_inc(self.sTsc, 1)
        elif eng == "sync":
            for t in range(NT):
                e.wait_ge(self.sTsc, t + 1)
                e.dma_start(out=self.tab_h[t, 0],
                            in_=self.cosb[:]).then_inc(self.sTsp[0], 16)
                e.dma_start(out=self.tab_h[t, 1],
                            in_=self.sinb[:]).then_inc(self.sTsp[1], 16)
        elif eng == "vector":
            e.wait_ge(self.sIv, 3)
            e.tensor_scalar(out=self.tB[:], in0=self.tA[:], scalar1=-LNHI,
                            scalar2=None, op0=AluOpType.mult)
            e.scalar_tensor_tensor(out=self.tC[:], in0=self.tA[:], scalar=-LNLO,
                                   in1=self.tB[:], op0=AluOpType.mult,
                                   op1=AluOpType.add).then_inc(self.sIv, 1)
            for t in range(NT):
                e.wait_ge(self.sTps, t + 1)
                if t == 0:
                    e.wait_ge(self.sIv, 5)
                if t >= 1:
                    e.wait_ge(self.sTsc, t)
                e.tensor_scalar(out=self.tA[:], in0=self.invf_full[:],
                                scalar1=pos, scalar2=None,
                                op0=AluOpType.mult).then_inc(self.sTan, 1)
                self._rred(e, shift=0.0, dst=self.tB)
                self._rred(e, shift=float(np.float32(math.pi / 2.0)), dst=self.tC)

    def _rred(self, e, shift, dst):
        a = self.tA[:]
        if shift != 0.0:
            e.tensor_scalar(out=self.re0[:], in0=a, scalar1=shift, scalar2=None,
                            op0=AluOpType.add)
            a = self.re0[:]
        e.tensor_scalar(out=self.re1[:], in0=a, scalar1=INV2PI, scalar2=None,
                        op0=AluOpType.mult)
        self.round_(e, self.re1[:], self.tki[:], self.re2[:], self.re3[:],
                    self.re1[:])
        e.scalar_tensor_tensor(out=self.re2[:], in0=self.re1[:], scalar=-C2HI,
                               in1=a, op0=AluOpType.mult, op1=AluOpType.add)
        e.scalar_tensor_tensor(out=self.re3[:], in0=self.re1[:], scalar=-C2LO,
                               in1=self.re2[:], op0=AluOpType.mult,
                               op1=AluOpType.add)
        e.tensor_scalar(out=dst[:], in0=self.re3[:], scalar1=-PI_CLAMP,
                        scalar2=PI_CLAMP, op0=AluOpType.max,
                        op1=AluOpType.min).then_inc(self.sTrr, 1)

    def p1(self, eng, e, which):
        w_h = {"q": self.wqt_h, "k": self.wkt_h, "v": self.wvt_h}[which]
        dst_h = {"q": self.q_h, "k": self.k_h, "v": self.v_h}[which]
        rope = which in ("q", "k")
        base = {"q": 0, "k": 1, "v": 2}[which]
        rbase = {"q": 0, "k": 1, "v": 0}[which]
        if eng == "gpsimd":
            if base >= 1:
                e.wait_ge(self.s1mm, base * NTT)
            e.dma_start(out=self.wA[:].rearrange("p (c o) -> p c o", o=D),
                        in_=w_h.rearrange("(c p) o -> p c o", p=P)
                        ).then_inc(self.s1w, 16)
        elif eng == "tensor":
            e.wait_ge(self.s1w, 16 * (base + 1))
            if base >= 1:
                e.wait_ge(self.s1cse, base)
            for oc in range(4):
                for i in range(16):
                    mm = e.matmul(out=self.pQ[0:1, oc * 512:(oc + 1) * 512],
                                  lhsT=self.ones_bf[:],
                                  rhs=self.wA[:, i * D + oc * 512:i * D + (oc + 1) * 512],
                                  start=(i == 0), stop=(i == 15))
            mm.then_inc(self.s1cs, 1)
            e.wait_ge(self.s1cse, base + 1)
            for n in range(NTT):
                idx = base * NTT + n
                e.wait_ge(self.s1xt[idx % 2], 16 * (idx // 2 + 1))
                if idx >= 1:
                    e.wait_ge(self.s1ev, idx)
                for i in range(16):
                    for oc in range(4):
                        e.matmul(out=self.pQ[:, oc * 512:(oc + 1) * 512],
                                 lhsT=self.xtt[n % 2][:, i * P:(i + 1) * P],
                                 rhs=self.wA[:, i * D + oc * 512:i * D + (oc + 1) * 512],
                                 start=(i == 0), stop=False)
                for oc in range(4):
                    mm = e.matmul(out=self.pQ[:, oc * 512:(oc + 1) * 512],
                                  lhsT=self.negm_row[0:1, n * P:(n + 1) * P],
                                  rhs=self.csw[0:1, oc * 512:(oc + 1) * 512],
                                  start=False, stop=True)
                mm.then_inc(self.s1mm, 1)
                if rope and n >= 1:
                    self._p1_trans(e, rbase * NTT + n - 1)
            if rope:
                self._p1_trans(e, rbase * NTT + NTT - 1)
        elif eng == "sync":
            if base == 0:
                e.wait_ge(self.s0sp[0], 16 * (NTT // 2))
                e.wait_ge(self.s0sp[1], 16 * (NTT // 2))
            for n in range(NTT):
                idx = base * NTT + n
                s, t = divmod(n, NT)
                if idx >= 2:
                    e.wait_ge(self.s1mm, idx - 1)
                e.dma_start(out=self.xtt[n % 2][:].rearrange("p (i q) -> p i q", q=P),
                            in_=self.xt_h[s, t].rearrange("i p q -> p i q")
                            ).then_inc(self.s1xt[idx % 2], 16)
                if rope:
                    if base == 0 and n == 0:
                        e.wait_ge(self.sTsp[0], 16 * 16)
                        e.wait_ge(self.sTsp[1], 16 * 16)
                    ridx = rbase * NTT + n
                    if ridx >= 2:
                        e.wait_ge(self.s1rp, ridx - 1)
                    e.dma_start(out=self.ctab[n % 2][:], in_=self.tab_h[t, 0]
                                ).then_inc(self.s1tb[ridx % 2], 16)
                    e.dma_start(out=self.stab[n % 2][:], in_=self.tab_h[t, 1]
                                ).then_inc(self.s1tb[ridx % 2], 16)
        elif eng == "scalar":
            e.wait_ge(self.s1cs, base + 1)
            e.copy(out=self.csw[0:1, :],
                   in_=self.pQ[0:1, 0:D]).then_inc(self.s1cse, 1)
            for n in range(NTT):
                idx = base * NTT + n
                e.wait_ge(self.s1mm, idx + 1)
                if rope:
                    ridx = rbase * NTT + n
                    if n >= 1:
                        self._p1_evac(e, which, ridx - 1, dst_h)
                    e.activation(out=self.ptok[:], in_=self.pQ[:, 0:D],
                                 func=AF.Copy,
                                 scale=self.r_c[:, n:n + 1]).then_inc(self.s1ev, 1)
                else:
                    if idx >= 2:
                        e.wait_ge(self.s1sp[idx % 2], 16 * (idx // 2))
                    e.activation(out=self.stage[n % 2][:], in_=self.pQ[:, 0:D],
                                 func=AF.Copy,
                                 scale=self.r_c[:, n:n + 1]).then_inc(self.s1ev, 1)
                    e.wait_ge(self.s1ev, base * NTT + n + 1)
                    e.dma_start(out=dst_h[n // NT, :, n % NT].rearrange("h p q -> p h q"),
                                in_=self.stage[n % 2][:].rearrange(
                                    "p (h q) -> p h q", q=P)
                                ).then_inc(self.s1sp[idx % 2], 16)
            if rope:
                self._p1_evac(e, which, rbase * NTT + NTT - 1, dst_h)
        elif eng == "vector":
            if not rope:
                return
            for n in range(NTT):
                idx = base * NTT + n
                ridx = rbase * NTT + n
                e.wait_ge(self.s1ev, idx + 1)
                e.wait_ge(self.s1tb[ridx % 2], 32 * (ridx // 2 + 1))
                if ridx >= 1:
                    e.wait_ge(self.s1tr, ridx)
                pe_ = self.ptok[:].rearrange("p (c two) -> p c two", two=2)
                qe, qo = pe_[:, :, 0:1], pe_[:, :, 1:2]
                ct = self.ctab[n % 2][:].unsqueeze(2)
                st_ = self.stab[n % 2][:].unsqueeze(2)
                r0 = self.re0[:].unsqueeze(2)
                r1 = self.re1[:].unsqueeze(2)
                r2 = self.re2[:].unsqueeze(2)
                r3 = self.re3[:].unsqueeze(2)
                e.tensor_tensor(out=r0, in0=qe, in1=ct, op=AluOpType.mult)
                e.tensor_tensor(out=r1, in0=qo, in1=st_, op=AluOpType.mult)
                e.tensor_tensor(out=r2, in0=qe, in1=st_, op=AluOpType.mult)
                e.tensor_tensor(out=r3, in0=qo, in1=ct, op=AluOpType.mult)
                ob = self.qbf[:].rearrange("p (c two) -> p c two", two=2)
                e.tensor_tensor(out=ob[:, :, 0:1], in0=r0, in1=r1,
                                op=AluOpType.subtract)
                e.tensor_tensor(out=ob[:, :, 1:2], in0=r2, in1=r3,
                                op=AluOpType.add).then_inc(self.s1rp, 1)

    def _p1_trans(self, e, ridx):
        e.wait_ge(self.s1rp, ridx + 1)
        if ridx >= 1:
            e.wait_ge(self.s1te, ridx)
        for c in range(16):
            tr = e.transpose(out=self.pT[:, c * P:(c + 1) * P],
                             in_=self.qbf[:, c * P:(c + 1) * P],
                             identity=self.ident[:])
        tr.then_inc(self.s1tr, 1)

    def _p1_evac(self, e, which, ridx, dst_h):
        base = {"q": 0, "k": 1}[which]
        n = ridx - base * NTT
        idx = base * NTT + n
        e.wait_ge(self.s1tr, ridx + 1)
        if idx >= 2:
            e.wait_ge(self.s1sp[idx % 2], 16 * (idx // 2))
        e.copy(out=self.stage[n % 2][:], in_=self.pT[:]).then_inc(self.s1te, 1)
        e.wait_ge(self.s1te, ridx + 1)
        e.dma_start(out=dst_h[n // NT, :, n % NT].rearrange("h p q -> p h q"),
                    in_=self.stage[n % 2][:].rearrange("p (h q) -> p h q", q=P)
                    ).then_inc(self.s1sp[idx % 2], 16)

    def probs_ap(self, g, tg):
        return (self.probsA[tg] if g % 2 == 1 else self.probsB[tg])

    @staticmethod
    def _p3_nch(tau):
        return 1 if tau < 8 else 2

    @classmethod
    def _p3_chc(cls, gh, tau, k):
        per_h = sum(cls._p3_nch(t) for t in range(NT))
        c = gh * per_h
        for t in range(tau):
            c += cls._p3_nch(t)
        return c + k

    @staticmethod
    def _p3_nbat(g):
        return 2 * g + 2

    @classmethod
    def _p3_tpc(cls, gh, g, b):
        c = gh * 20
        for g2 in range(g):
            c += cls._p3_nbat(g2)
        return c + b

    def _p3_batch_slices(self, g, tg):
        if g == 0:
            return []
        nb = self._p3_nbat(g - 1)
        lo = (nb * tg) // 4
        hi = (nb * (tg + 1)) // 4
        return list(range(lo, hi))

    def p3(self, eng, e, s):
        def hh(h):
            return s * H + h

        def tauc(h, tau):
            return hh(h) * NT + tau

        def gc(h, g):
            return hh(h) * 4 + g

        pS = [self.pQ[:, 0:1024], self.pQ[:, 1024:2048]]
        pTb = [self.pT[:, 0:1024], self.pT[:, 1024:2048]]

        if eng == "sync":
            if s == 0:
                for par in range(2):
                    e.wait_ge(self.s1sp[par], 16 * 3 * NTT // 2)
            else:
                for par in range(2):
                    e.wait_ge(self.s4spx[par], 16 * (NT // 2))
            for h in range(H):
                gh = hh(h)
                if gh >= 2:
                    e.wait_ge(self.s3dn, gh - 1)
                e.dma_start(out=self.Kb[h % 2][:].rearrange("p (t q) -> p t q", q=P),
                            in_=self.k_h[s, h].rearrange("t p q -> p t q")
                            ).then_inc(self.s3kv[h % 2], 16)
                e.dma_start(out=self.Qb[h % 2][:].rearrange("p (t q) -> p t q", q=P),
                            in_=self.q_h[s, h].rearrange("t p q -> p t q")
                            ).then_inc(self.s3kv[h % 2], 16)
                e.dma_start(out=self.Vb[h % 2][:].rearrange("p (t q) -> p t q", q=P),
                            in_=self.v_h[s, h].rearrange("t p q -> p t q")
                            ).then_inc(self.s3kv[h % 2], 16)
        elif eng == "tensor":
            for h in range(H):
                gh = hh(h)
                e.wait_ge(self.s3kv[h % 2], 48 * (gh // 2 + 1))
                for g in range(4):
                    for tg in range(4):
                        tau = 4 * g + tg
                        kw = (tau + 1) * P
                        nch = self._p3_nch(tau)
                        for k in range(nch):
                            cg = self._p3_chc(gh, tau, k)
                            par = cg % 2
                            if cg >= 2:
                                e.wait_ge(self.s3ex, cg - 1)
                            cw = min(1024, kw - 1024 * k)
                            dst = pS[par]
                            nkc = (cw + 511) // 512
                            for kc in range(nkc):
                                w = min(512, cw - kc * 512)
                                last = (kc == nkc - 1)
                                mm = e.matmul(
                                    out=dst[:, kc * 512:kc * 512 + w],
                                    lhsT=self.Qb[h % 2][:, tau * P:(tau + 1) * P],
                                    rhs=self.Kb[h % 2][:, 1024 * k + kc * 512:
                                                       1024 * k + kc * 512 + w],
                                    start=True, stop=not (last and k == nch - 1))
                            if k == nch - 1:
                                off = kw - 1024 * k - P
                                mm = e.matmul(out=dst[:, off:off + P],
                                              lhsT=self.ident[:],
                                              rhs=self.mask_bf[:],
                                              start=False, stop=True)
                            mm.then_inc(self.s3sc, 1)
                        for b in self._p3_batch_slices(g, tg):
                            self._p3_ktbatch(e, s, h, g - 1, b, pTb)
                for b in range(self._p3_nbat(3)):
                    self._p3_ktbatch(e, s, h, 3, b, pTb)
        elif eng == "vector":
            for h in range(H):
                gh = hh(h)
                for g in range(4):
                    for tg in range(4):
                        tau = 4 * g + tg
                        kw = (tau + 1) * P
                        nch = self._p3_nch(tau)
                        for k in range(nch):
                            cg = self._p3_chc(gh, tau, k)
                            par = cg % 2
                            e.wait_ge(self.s3sc, cg + 1)
                            cw = min(1024, kw - 1024 * k)
                            e.reduce_max(out=self.sbias[:, k:k + 1],
                                         in_=pS[par][:, 0:cw], axis=AX.X,
                                         negate=True)
                            e.tensor_scalar(out=self.sbias[:, k:k + 1],
                                            in0=self.sbias[:, k:k + 1],
                                            scalar1=SC_ATT, scalar2=None,
                                            op0=AluOpType.mult
                                            ).then_inc(self.s3mx, 1)
                        cg1 = self._p3_chc(gh, tau, nch - 1)
                        e.wait_ge(self.s3ex, cg1 + 1)
                        if nch == 1:
                            e.reciprocal(out=self.srs[:, 0:1],
                                         in_=self.ssum[:, 0:1])
                            pr = self.probs_ap(g, tg)
                            e.tensor_scalar(out=pr[:, 0:kw],
                                            in0=pr[:, 0:kw],
                                            scalar1=self.srs[:, 0:1],
                                            scalar2=None,
                                            op0=AluOpType.mult
                                            ).then_inc(self.s3pn, 1)
                        else:
                            e.tensor_tensor(out=self.sbias[:, 2:3],
                                            in0=self.sbias[:, 0:1],
                                            in1=self.sbias[:, 1:2],
                                            op=AluOpType.min)
                            e.tensor_tensor(out=self.sbias[:, 0:1],
                                            in0=self.sbias[:, 2:3],
                                            in1=self.sbias[:, 0:1],
                                            op=AluOpType.subtract)
                            e.tensor_tensor(out=self.sbias[:, 1:2],
                                            in0=self.sbias[:, 2:3],
                                            in1=self.sbias[:, 1:2],
                                            op=AluOpType.subtract
                                            ).then_inc(self.s3mg, 1)
                            e.wait_ge(self.s3me, gh * 8 + tau - 7)
                            e.tensor_tensor(out=self.srs[:, 0:1],
                                            in0=self.ssum[:, 0:1],
                                            in1=self.srs[:, 2:3],
                                            op=AluOpType.mult)
                            e.tensor_tensor(out=self.srs[:, 1:2],
                                            in0=self.ssum[:, 1:2],
                                            in1=self.srs[:, 3:4],
                                            op=AluOpType.mult)
                            e.tensor_tensor(out=self.srs[:, 0:1],
                                            in0=self.srs[:, 0:1],
                                            in1=self.srs[:, 1:2],
                                            op=AluOpType.add)
                            e.reciprocal(out=self.srs[:, 0:1],
                                         in_=self.srs[:, 0:1])
                            e.tensor_tensor(out=self.srs[:, 2:3],
                                            in0=self.srs[:, 2:3],
                                            in1=self.srs[:, 0:1],
                                            op=AluOpType.mult)
                            e.tensor_tensor(out=self.srs[:, 3:4],
                                            in0=self.srs[:, 3:4],
                                            in1=self.srs[:, 0:1],
                                            op=AluOpType.mult)
                            pr = self.probs_ap(g, tg)
                            e.tensor_scalar(out=pr[:, 0:1024],
                                            in0=pr[:, 0:1024],
                                            scalar1=self.srs[:, 2:3],
                                            scalar2=None, op0=AluOpType.mult)
                            e.tensor_scalar(out=pr[:, 1024:kw],
                                            in0=pr[:, 1024:kw],
                                            scalar1=self.srs[:, 3:4],
                                            scalar2=None, op0=AluOpType.mult
                                            ).then_inc(self.s3pn, 1)
        elif eng == "scalar":
            if s == 1:
                for par in range(2):
                    e.wait_ge(self.s4sp[par], 16 * (NT // 2))
            for h in range(H):
                gh = hh(h)
                for g in range(4):
                    for tg in range(4):
                        tau = 4 * g + tg
                        kw = (tau + 1) * P
                        nch = self._p3_nch(tau)
                        for k in range(nch):
                            cg = self._p3_chc(gh, tau, k)
                            par = cg % 2
                            e.wait_ge(self.s3mx, cg + 1)
                            cw = min(1024, kw - 1024 * k)
                            e.activation(out=self.probs_ap(g, tg)[:, 1024 * k:
                                                                  1024 * k + cw],
                                         in_=pS[par][:, 0:cw], func=AF.Exp,
                                         bias=self.sbias[:, k:k + 1],
                                         scale=SC_ATT,
                                         accum_out=self.ssum[:, k:k + 1]
                                         ).then_inc(self.s3ex, 1)
                        if nch == 2:
                            e.wait_ge(self.s3mg, gh * 8 + tau - 7)
                            e.activation(out=self.srs[:, 2:4],
                                         in_=self.sbias[:, 0:2],
                                         func=AF.Exp).then_inc(self.s3me, 1)
                        for b in self._p3_batch_slices(g, tg):
                            c = self._p3_tpc(gh, g - 1, b)
                            e.wait_ge(self.s3tp, c + 1)
                            e.copy(out=self.pTe[c % 2][:],
                                   in_=pTb[c % 2][:]).then_inc(self.s3tev, 1)
                    if g >= 1 and tg == 3:
                        e.wait_ge(self.s3av, gh * 4 + (g - 1) + 1)
                        e.copy(out=self.wB[:, h * D + (g - 1) * 512:
                                           h * D + g * 512],
                               in_=self.pAng[:, 0:512]).then_inc(self.s3ae, 1)
                for b in range(self._p3_nbat(3)):
                    c = self._p3_tpc(gh, 3, b)
                    e.wait_ge(self.s3tp, c + 1)
                    e.copy(out=self.pTe[c % 2][:],
                           in_=pTb[c % 2][:]).then_inc(self.s3tev, 1)
                e.wait_ge(self.s3av, gh * 4 + 4)
                e.copy(out=self.wB[:, h * D + 3 * 512:h * D + 4 * 512],
                       in_=self.pAng[:, 0:512]).then_inc(self.s3ae, 1)
        elif eng == "gpsimd":
            for h in range(H):
                e.wait_ge(self.s3ae, gc(h, 3) + 1)
                e.memset(self.dum[:], 0.0).then_inc(self.s3dn, 1)

    def _p3_ktbatch(self, e, s, h, g, b, pTb):
        gh = s * H + h
        c = self._p3_tpc(gh, g, b)
        bpar = c % 2
        e.wait_ge(self.s3pn, gh * NT + 4 * g + 4)
        if c >= 2:
            e.wait_ge(self.s3tev, c - 1)
        for sub in range(2):
            kt = 2 * b + sub
            for tg in range(4):
                tau = 4 * g + tg
                src = (self.probs_ap(g, tg)[:, kt * P:(kt + 1) * P]
                       if kt <= tau else self.zero_bf[:])
                tr = e.transpose(out=pTb[bpar][:, sub * 512 + tg * P:
                                               sub * 512 + (tg + 1) * P],
                                 in_=src, identity=self.ident[:])
        tr.then_inc(self.s3tp, 1)
        e.wait_ge(self.s3tev, c + 1)
        if b == 0:
            e.wait_ge(self.s3ae, gh * 4 + g)
        for sub in range(2):
            kt = 2 * b + sub
            mm = e.matmul(out=self.pAng[:, 0:512],
                          lhsT=self.Vb[h % 2][:, kt * P:(kt + 1) * P],
                          rhs=self.pTe[c % 2][:, sub * 512:(sub + 1) * 512],
                          start=(b == 0 and sub == 0),
                          stop=(b == self._p3_nbat(g) - 1 and sub == 1))
        if b == self._p3_nbat(g) - 1:
            mm.then_inc(self.s3av, 1)

    def p4(self, eng, e, s):
        def n_(t):
            return s * NT + t
        if eng == "gpsimd":
            if s == 0:
                e.dma_start(out=self.wA[:].rearrange("p (c o) -> p c o", o=D),
                            in_=self.wot_h.rearrange("(c p) o -> p c o", p=P)
                            ).then_inc(self.s4w, 16)
        elif eng == "sync":
            for t in range(NT):
                n = n_(t)
                if n >= 1:
                    e.wait_ge(self.s4x2, n)
                e.dma_start(out=self.xres[:],
                            in_=self.x_h[s, t * P:(t + 1) * P, :]
                            ).then_inc(self.s4x[n % 2], 16)
        elif eng == "tensor":
            e.wait_ge(self.s4w, 16)
            for t in range(NT):
                n = n_(t)
                if n >= 1:
                    e.wait_ge(self.s4x2, n)
                for i in range(16):
                    for oc in range(4):
                        mm = e.matmul(out=self.pQ[:, oc * 512:(oc + 1) * 512],
                                      lhsT=self.wB[:, i * D + t * P:
                                                   i * D + (t + 1) * P],
                                      rhs=self.wA[:, i * D + oc * 512:
                                                  i * D + (oc + 1) * 512],
                                      start=(i == 0), stop=(i == 15))
                mm.then_inc(self.s4mm, 1)
                if n >= s * NT + 1:
                    self._p4_trans(e, n - 1)
            self._p4_trans(e, s * NT + NT - 1)
        elif eng == "vector":
            for t in range(NT):
                n = n_(t)
                e.wait_ge(self.s4mm, n + 1)
                e.wait_ge(self.s4x[n % 2], 16 * (n // 2 + 1))
                if n >= 2:
                    e.wait_ge(self.s4spx[n % 2], 16 * (n // 2))
                e.tensor_tensor(out=self.x2t[n % 2][:], in0=self.pQ[:, 0:D],
                                in1=self.xres[:], op=AluOpType.add
                                ).then_inc(self.s4x2, 1)
                e.reduce_sum(out=self.st[:, 0:1], in_=self.x2t[n % 2][:],
                             axis=AX.X)
                e.tensor_scalar(out=self.negm2_c[:, n:n + 1], in0=self.st[:, 0:1],
                                scalar1=-1.0 / D, scalar2=None, op0=AluOpType.mult)
                e.wait_ge(self.s4sq, n + 1)
                e.reduce_sum(out=self.st[:, 1:2], in_=self.xn2[:], axis=AX.X)
                e.tensor_tensor(out=self.st[:, 3:4], in0=self.negm2_c[:, n:n + 1],
                                in1=self.negm2_c[:, n:n + 1], op=AluOpType.mult)
                if n >= 1:
                    e.wait_ge(self.s4r, n)
                e.tensor_scalar(out=self.st[:, 2:3], in0=self.st[:, 1:2],
                                scalar1=1.0 / D, scalar2=None, op0=AluOpType.mult)
                e.tensor_tensor(out=self.st[:, 2:3], in0=self.st[:, 2:3],
                                in1=self.st[:, 3:4], op=AluOpType.subtract)
                e.tensor_scalar(out=self.st[:, 2:3], in0=self.st[:, 2:3],
                                scalar1=EPS, scalar2=None, op0=AluOpType.add)
                e.reciprocal(out=self.st[:, 2:3],
                             in_=self.st[:, 2:3]).then_inc(self.s4g, 1)
                e.wait_ge(self.s4r, n + 1)
                e.tensor_scalar(out=self.xn2[:], in0=self.x2t[n % 2][:],
                                scalar1=self.negm2_c[:, n:n + 1], scalar2=None,
                                op0=AluOpType.add)
                e.tensor_scalar(out=self.xn2[:], in0=self.xn2[:],
                                scalar1=self.r2_c[:, n:n + 1], scalar2=None,
                                op0=AluOpType.mult)
                if n >= 1:
                    e.wait_ge(self.s4tr, n)
                for half in range(2):
                    qi = self.quant_(e, self.xn2[:, half * 1024:(half + 1) * 1024],
                                     self.xqb[:, half * 1024:(half + 1) * 1024],
                                     self.k0i4[:], self.rr4[:], self.u14[:])
                qi.then_inc(self.s4q, 1)
        elif eng == "scalar":
            for t in range(NT):
                n = n_(t)
                e.wait_ge(self.s4x2, n + 1)
                e.dma_start(out=self.x2_h[s, t * P:(t + 1) * P, :],
                            in_=self.x2t[n % 2][:]).then_inc(self.s4spx[n % 2], 16)
                e.activation(out=self.xn2[:], in_=self.x2t[n % 2][:],
                             func=AF.Square).then_inc(self.s4sq, 1)
                e.wait_ge(self.s4g, n + 1)
                e.activation(out=self.r2_c[:, n:n + 1], in_=self.st[:, 2:3],
                             func=AF.Sqrt).then_inc(self.s4r, 1)
                if n >= s * NT + 1:
                    self._p4_evac(e, n - 1)
            self._p4_evac(e, s * NT + NT - 1)

    def _p4_trans(self, e, n):
        e.wait_ge(self.s4q, n + 1)
        if n >= 1:
            e.wait_ge(self.s4te, n)
        for c in range(16):
            tr = e.transpose(out=self.pT[:, c * P:(c + 1) * P],
                             in_=self.xqb[:, c * P:(c + 1) * P],
                             identity=self.ident[:])
        tr.then_inc(self.s4tr, 1)

    def _p4_evac(self, e, n):
        e.wait_ge(self.s4tr, n + 1)
        if n >= 2:
            e.wait_ge(self.s4sp[n % 2], 16 * (n // 2))
        e.copy(out=self.st4[n % 2][:], in_=self.pT[:]).then_inc(self.s4te, 1)
        e.wait_ge(self.s4te, n + 1)
        e.dma_start(out=self.xq_h[n].rearrange("i p q -> p i q"),
                    in_=self.st4[n % 2][:].rearrange("p (i q) -> p i q", q=P)
                    ).then_inc(self.s4sp[n % 2], 16)

    def p5(self, eng, e):
        NF = 4
        NI = NF * NTT
        if eng == "sync":
            for par in range(2):
                e.wait_ge(self.s4sp[par], 16 * (NTT // 2))
            for fc in range(NF):
                for c in range(16):
                    for w_i, wh in ((0, self.w1t_h), (1, self.w2t_h)):
                        cc = (fc * 16 + c) * 2 + w_i
                        if cc >= 2:
                            e.wait_ge(self.s5sg, cc - 1)
                        if w_i == 1 and fc * 16 + c >= 1:
                            e.wait_ge(self.s5ac, fc * 16 + c)
                        e.dma_start(out=self.w5[cc % 2][:],
                                    in_=wh[c * P:(c + 1) * P,
                                           fc * D:(fc + 1) * D]
                                    ).then_inc(self.s5w[cc % 2], 16)
                for tau in range(NTT):
                    it = fc * NTT + tau
                    if it >= 2:
                        e.wait_ge(self.s5mm, it - 1)
                    e.dma_start(out=self.xqt[it % 2][:].rearrange(
                                    "p (i q) -> p i q", q=P),
                                in_=self.xq_h[tau].rearrange("i p q -> p i q")
                                ).then_inc(self.s5xq[it % 2], 16)
        elif eng == "scalar":
            for par in range(2):
                e.wait_ge(self.s4spx[par], 16 * (NTT // 2))
            e.wait_ge(self.s5s1c, 1)
            e.copy(out=self.st[0:1, 12:13],
                   in_=self.pAng[0:1, 0:1]).then_inc(self.s5s1c, 1)
            e.wait_ge(self.s5s1c, 3)
            e.activation(out=self.s1_col[:], in_=self.pAng[:, 2:3], func=AF.Copy,
                         scale=1.0 / (D * FF)).then_inc(self.s5s1c, 1)
            for fc in range(NF):
                if fc >= 1:
                    e.wait_ge(self.s5mm, fc * NTT)
                for c in range(16):
                    for w_i, dst in ((0, self.wA), (1, self.wB)):
                        cc = (fc * 16 + c) * 2 + w_i
                        e.wait_ge(self.s5w[cc % 2], 16 * (cc // 2 + 1))
                        e.sign(out=dst[:, c * D:(c + 1) * D],
                               in_=self.w5[cc % 2][:]).then_inc(self.s5sg, 1)
                for tau in range(NTT):
                    it = fc * NTT + tau
                    e.wait_ge(self.s5m0, it + 1)
                    if it >= 1:
                        e.wait_ge(self.s5h, 2 * it)
                    e.activation(out=self.z1s[:], in_=self.pQ[:, 0:1024],
                                 func=AF.Silu,
                                 scale=self.s1_col[:]).then_inc(self.s5e1, 1)
                    e.wait_ge(self.s5h, 2 * it + 1)
                    e.activation(out=self.z1s[:], in_=self.pQ[:, 1024:2048],
                                 func=AF.Silu,
                                 scale=self.s1_col[:]).then_inc(self.s5e1, 1)
                    if it >= 1:
                        self._p5_evac(e, it - 1)
                if fc == NF - 1:
                    self._p5_evac(e, NI - 1)
        elif eng == "tensor":
            e.matmul(out=self.pAng[0:1, 0:1], lhsT=self.s1acc[:],
                     rhs=self.ones_f[:], start=True,
                     stop=True).then_inc(self.s5s1c, 1)
            e.wait_ge(self.s5s1c, 2)
            e.matmul(out=self.pAng[:, 2:3], lhsT=self.onesrow[:],
                     rhs=self.st[0:1, 12:13], start=True,
                     stop=True).then_inc(self.s5s1c, 1)
            for fc in range(NF):
                for tau in range(NTT):
                    it = fc * NTT + tau
                    e.wait_ge(self.s5xq[it % 2], 16 * (it // 2 + 1))
                    e.wait_ge(self.s5sg, (fc * 16 + 16) * 2)
                    if it >= 1:
                        e.wait_ge(self.s5e1, 2 * it)
                    for i in range(16):
                        for oc in range(4):
                            mm = e.matmul(out=self.pQ[:, oc * 512:(oc + 1) * 512],
                                          lhsT=self.xqt[it % 2][:, i * P:(i + 1) * P],
                                          rhs=self.wA[:, i * D + oc * 512:
                                                      i * D + (oc + 1) * 512],
                                          start=(i == 0), stop=(i == 15))
                    mm.then_inc(self.s5m0, 1)
                    if it >= 1:
                        e.wait_ge(self.s5h, 2 * it)
                    for i in range(16):
                        for oc in range(2):
                            mm = e.matmul(out=self.pAng[:, oc * 512:(oc + 1) * 512],
                                          lhsT=self.xqt[it % 2][:, i * P:(i + 1) * P],
                                          rhs=self.wB[:, i * D + oc * 512:
                                                      i * D + (oc + 1) * 512],
                                          start=(i == 0), stop=(i == 15))
                    mm.then_inc(self.s5z0, 1)
                    if it >= 1:
                        self._p5_trans(e, it - 1)
                    e.wait_ge(self.s5h, 2 * it + 1)
                    for i in range(16):
                        for oc in range(2, 4):
                            mm = e.matmul(out=self.pAng[:, (oc - 2) * 512:
                                                        (oc - 1) * 512],
                                          lhsT=self.xqt[it % 2][:, i * P:(i + 1) * P],
                                          rhs=self.wB[:, i * D + oc * 512:
                                                      i * D + (oc + 1) * 512],
                                          start=(i == 0), stop=(i == 15))
                    mm.then_inc(self.s5mm, 1)
            self._p5_trans(e, NI - 1)
        elif eng == "vector":
            for par in range(2):
                e.wait_ge(self.s4spx[par], 16 * (NTT // 2))
            for fc in range(NF):
                for c in range(16):
                    j = fc * 16 + c
                    e.wait_ge(self.s5w[1], 16 * (j + 1))
                    e.reduce_sum(out=self.st[:, 4:5], in_=self.w5[1][:],
                                 axis=AX.X, apply_absolute_value=True)
                    e.tensor_tensor(out=self.s2acc[:], in0=self.s2acc[:],
                                    in1=self.st[:, 4:5],
                                    op=AluOpType.add).then_inc(self.s5ac, 1)
                for tau in range(NTT):
                    it = fc * NTT + tau
                    e.wait_ge(self.s5e1, 2 * it + 1)
                    e.wait_ge(self.s5z0, it + 1)
                    e.tensor_tensor(out=self.hh[:], in0=self.z1s[:],
                                    in1=self.pAng[:, 0:1024],
                                    op=AluOpType.mult).then_inc(self.s5h, 1)
                    if it >= 1:
                        e.wait_ge(self.s5tr, it)
                    self.quant_(e, self.hh[:], self.hqb[:, 0:1024],
                                self.k0i5[:], self.rr5[:], self.u15[:]
                                ).then_inc(self.s5q, 1)
                    e.wait_ge(self.s5e1, 2 * it + 2)
                    e.wait_ge(self.s5mm, it + 1)
                    e.tensor_tensor(out=self.hh[:], in0=self.z1s[:],
                                    in1=self.pAng[:, 0:1024],
                                    op=AluOpType.mult).then_inc(self.s5h, 1)
                    self.quant_(e, self.hh[:], self.hqb[:, 1024:2048],
                                self.k0i5[:], self.rr5[:], self.u15[:]
                                ).then_inc(self.s5q, 1)

    def _p5_trans(self, e, it):
        e.wait_ge(self.s5q, 2 * it + 2)
        if it >= 1:
            e.wait_ge(self.s5te, it)
        for c in range(16):
            tr = e.transpose(out=self.pT[:, c * P:(c + 1) * P],
                             in_=self.hqb[:, c * P:(c + 1) * P],
                             identity=self.ident[:])
        tr.then_inc(self.s5tr, 1)

    def _p5_evac(self, e, it):
        fc, tau = divmod(it, NTT)
        e.wait_ge(self.s5tr, it + 1)
        if it >= 2:
            e.wait_ge(self.s5sp[it % 2], 16 * (it // 2))
        e.copy(out=self.st5[it % 2][:], in_=self.pT[:]).then_inc(self.s5te, 1)
        e.wait_ge(self.s5te, it + 1)
        e.dma_start(out=self.hq_h[fc * 16:(fc + 1) * 16, tau].rearrange(
                        "i p q -> p i q"),
                    in_=self.st5[it % 2][:].rearrange("p (i q) -> p i q", q=P)
                    ).then_inc(self.s5sp[it % 2], 16)

    def p6(self, eng, e):
        if eng == "sync":
            for c in range(64):
                if c >= 2:
                    e.wait_ge(self.s6sg, c - 1)
                    e.wait_ge(self.s6ac, c - 1)
                e.dma_start(out=self.w6[c % 2][:],
                            in_=self.w3t_h[(c % 64) * P:(c % 64 + 1) * P, :]
                            ).then_inc(self.s6w[c % 2], 16)
        elif eng == "scalar":
            for c in range(64):
                e.wait_ge(self.s6w[c % 2], 16 * (c // 2 + 1))
                if c >= 2:
                    e.wait_ge(self.s6sp[c % 2], 16 * (c // 2))
                e.sign(out=self.w3sb[c % 2][:],
                       in_=self.w6[c % 2][:]).then_inc(self.s6sg, 1)
                e.wait_ge(self.s6sg, c + 1)
                e.dma_start(out=self.w3s_h[c],
                            in_=self.w3sb[c % 2][:]).then_inc(self.s6sp[c % 2], 16)
            e.wait_ge(self.s6sc, 1)
            e.copy(out=self.st[0:1, 10:11], in_=self.pAng[0:1, 0:1])
            e.copy(out=self.st[0:1, 11:12],
                   in_=self.pAng[0:1, 1:2]).then_inc(self.s6sc, 1)
            e.wait_ge(self.s6sc, 4)
            e.copy(out=self.s23_col[:],
                   in_=self.pAng[:, 4:5]).then_inc(self.s6sc, 1)
        elif eng == "vector":
            for c in range(64):
                e.wait_ge(self.s6w[c % 2], 16 * (c // 2 + 1))
                e.reduce_sum(out=self.st[:, 4:5], in_=self.w6[c % 2][:],
                             axis=AX.X, apply_absolute_value=True)
                e.tensor_tensor(out=self.s3acc[:], in0=self.s3acc[:],
                                in1=self.st[:, 4:5],
                                op=AluOpType.add).then_inc(self.s6ac, 1)
            e.wait_ge(self.s6sc, 2)
            e.tensor_tensor(out=self.st[0:1, 10:11], in0=self.st[0:1, 10:11],
                            in1=self.st[0:1, 11:12], op=AluOpType.mult)
            e.tensor_scalar(out=self.st[0:1, 10:11], in0=self.st[0:1, 10:11],
                            scalar1=1.0 / (D * FF), scalar2=1.0 / (D * FF),
                            op0=AluOpType.mult,
                            op1=AluOpType.mult).then_inc(self.s6sc, 1)
        elif eng == "tensor":
            e.wait_ge(self.s6ac, 64)
            e.matmul(out=self.pAng[0:1, 0:1], lhsT=self.s2acc[:],
                     rhs=self.ones_f[:], start=True, stop=True)
            e.matmul(out=self.pAng[0:1, 1:2], lhsT=self.s3acc[:],
                     rhs=self.ones_f[:], start=True,
                     stop=True).then_inc(self.s6sc, 1)
            e.wait_ge(self.s6sc, 3)
            e.matmul(out=self.pAng[:, 4:5], lhsT=self.onesrow[:],
                     rhs=self.st[0:1, 10:11], start=True,
                     stop=True).then_inc(self.s6sc, 1)

    def p7(self, eng, e):
        NSUP = 4
        NFC = 4
        if eng == "sync":
            for it in range(NSUP * NFC):
                sup, fc = divmod(it, NFC)
                if it >= 1:
                    e.wait_ge(self.s7mm, it * 8)
                e.dma_start(out=self.wA[:].rearrange("p (j o) -> p j o", o=D),
                            in_=self.w3s_h[fc * 16:(fc + 1) * 16].rearrange(
                                "j p o -> p j o")).then_inc(self.s7w[0], 16)
        elif eng == "gpsimd":
            for par in range(2):
                e.wait_ge(self.s6sp[par], 16 * 32)
                e.wait_ge(self.s5sp[par], 16 * (4 * NTT // 2))
            for it in range(NSUP * NFC):
                sup, fc = divmod(it, NFC)
                for tl in range(8):
                    tau = sup * 8 + tl
                    k = it * 8 + tl
                    if k >= 2:
                        e.wait_ge(self.s7mm, k - 1)
                    e.dma_start(out=self.hqt[k % 2][:].rearrange(
                                    "p (j q) -> p j q", q=P),
                                in_=self.hq_h[fc * 16:(fc + 1) * 16, tau].rearrange(
                                    "j p q -> p j q")
                                ).then_inc(self.s7h[k % 2], 16)
        elif eng == "tensor":
            for it in range(NSUP * NFC):
                sup, fc = divmod(it, NFC)
                e.wait_ge(self.s7w[0], 16 * (it + 1))
                for tl in range(8):
                    tau = sup * 8 + tl
                    k = it * 8 + tl
                    e.wait_ge(self.s7h[k % 2], 16 * (k // 2 + 1))
                    if k >= 2:
                        e.wait_ge(self.s7ac, k - 1)
                    par = k % 2
                    for j in range(16):
                        for oc in range(4):
                            if par == 0:
                                dst = self.pQ[:, oc * 512:(oc + 1) * 512]
                            else:
                                dst = (self.pAng[:, (oc % 2) * 512:(oc % 2 + 1) * 512]
                                       if oc < 2 else
                                       self.pTf[:, (oc % 2) * 512:(oc % 2 + 1) * 512])
                            mm = e.matmul(out=dst,
                                          lhsT=self.hqt[k % 2][:, j * P:(j + 1) * P],
                                          rhs=self.wA[:, j * D + oc * 512:
                                                      j * D + (oc + 1) * 512],
                                          start=(j == 0), stop=(j == 15))
                    mm.then_inc(self.s7mm, 1)
        elif eng == "vector":
            for it in range(NSUP * NFC):
                sup, fc = divmod(it, NFC)
                for tl in range(8):
                    tau = sup * 8 + tl
                    k = it * 8 + tl
                    e.wait_ge(self.s7mm, k + 1)
                    par = k % 2
                    aslc = self.acc[:, tl * D:(tl + 1) * D]
                    if par == 0:
                        srcs = [self.pQ[:, 0:1024], self.pQ[:, 1024:2048]]
                    else:
                        srcs = [self.pAng[:, 0:1024], self.pTf[:, 0:1024]]
                    for half in range(2):
                        a2 = aslc[:, half * 1024:(half + 1) * 1024]
                        if fc == 0:
                            ai = e.tensor_copy(out=a2, in_=srcs[half])
                        else:
                            ai = e.tensor_tensor(out=a2, in0=a2, in1=srcs[half],
                                                 op=AluOpType.add)
                    ai.then_inc(self.s7ac, 1)
                if fc == NFC - 1:
                    for tl in range(8):
                        tau = sup * 8 + tl
                        k2 = sup * 8 + tl
                        e.wait_ge(self.s7x2[k2 % 2], 16 * (k2 // 2 + 1))
                        if k2 >= 2:
                            e.wait_ge(self.s7out[k2 % 2], 16 * (k2 // 2))
                        e.scalar_tensor_tensor(out=self.outb[k2 % 2][:],
                                               in0=aslc_get(self, tl),
                                               scalar=self.s23_col[:],
                                               in1=self.x2r[k2 % 2][:],
                                               op0=AluOpType.mult,
                                               op1=AluOpType.add
                                               ).then_inc(self.s7ev, 1)
        elif eng == "scalar":
            def outd(k2):
                s_, t_ = divmod(k2, NT)
                e.wait_ge(self.s7ev, k2 + 1)
                e.dma_start(out=self.out_h[s_, t_ * P:(t_ + 1) * P, :],
                            in_=self.outb[k2 % 2][:]
                            ).then_inc(self.s7out[k2 % 2], 16)
            for par in range(2):
                e.wait_ge(self.s4spx[par], 16 * (NTT // 2))
                e.wait_ge(self.s6sp[par], 16 * 32)
                e.wait_ge(self.s5sp[par], 16 * (4 * NTT // 2))
            for k2 in range(NTT):
                if k2 >= 2:
                    outd(k2 - 2)
                    e.wait_ge(self.s7ev, k2 - 1)
                s_, t_ = divmod(k2, NT)
                e.dma_start(out=self.x2r[k2 % 2][:],
                            in_=self.x2_h[s_, t_ * P:(t_ + 1) * P, :]
                            ).then_inc(self.s7x2[k2 % 2], 16)
            outd(NTT - 2)
            outd(NTT - 1)
            e.wait_ge(self.s7out[0], 16 * 16)
            e.wait_ge(self.s7out[1], 16 * 16)


def aslc_get(b, tl):
    return b.acc[:, tl * D:(tl + 1) * D]


def build():
    nc = bass.Bass("TRN2", target_bir_lowering=False)
    b = B(nc)
    ENG = ["sync", "gpsimd", "vector", "scalar", "tensor"]

    def phase(fn):
        with nc.Block() as block:
            for eng in ENG:
                dec = getattr(block, eng)

                def mk(eng=eng):
                    def _(e):
                        if eng in ("vector", "scalar", "gpsimd"):
                            e = EP(e)
                        fn(eng, e)
                    return _
                dec(mk())

    def g_a(eng, e):
        if eng in ("vector", "gpsimd"):
            b.init_(eng, e)
        b.p0(eng, e)
        b.p05(eng, e)
        for w in ("q", "k", "v"):
            b.p1(eng, e, w)
    phase(g_a)
    phase(lambda eng, e: b.p3(eng, e, 0))
    phase(lambda eng, e: b.p4(eng, e, 0))
    phase(lambda eng, e: b.p3(eng, e, 1))
    phase(lambda eng, e: b.p4(eng, e, 1))
    phase(lambda eng, e: b.p5(eng, e))
    phase(lambda eng, e: b.p6(eng, e))
    phase(lambda eng, e: b.p7(eng, e))
    return nc, b


def kernel(L0_attn_x, L0_attn_Wq, L0_attn_Wk, L0_attn_Wv, L0_attn_Wo,
           L0_ffn_w1, L0_ffn_w2, L0_ffn_w3):
    x = np.ascontiguousarray(np.asarray(L0_attn_x, dtype=np.float32))
    wqt = np.ascontiguousarray(np.asarray(L0_attn_Wq, dtype=np.float32).T)
    wkt = np.ascontiguousarray(np.asarray(L0_attn_Wk, dtype=np.float32).T)
    wvt = np.ascontiguousarray(np.asarray(L0_attn_Wv, dtype=np.float32).T)
    wot = np.ascontiguousarray(np.asarray(L0_attn_Wo, dtype=np.float32).T)
    w1t = np.ascontiguousarray(np.asarray(L0_ffn_w1, dtype=np.float32).T)
    w2t = np.ascontiguousarray(np.asarray(L0_ffn_w2, dtype=np.float32).T)
    w3t = np.ascontiguousarray(np.asarray(L0_ffn_w3, dtype=np.float32).T)
    nc, _ = build()
    in_maps = [{"x": x[2 * c:2 * c + 2], "wqt": wqt, "wkt": wkt, "wvt": wvt,
                "wot": wot, "w1t": w1t, "w2t": w2t, "w3t": w3t}
               for c in range(8)]
    res = run_bass_kernel_spmd(nc, in_maps, core_ids=list(range(8)))
    return np.concatenate([np.asarray(r["out"]) for r in res.results], axis=0)


# version 06
# speedup vs baseline: 1.2477x; 1.0768x over previous
import math
import numpy as np
import concourse.bass as bass
import concourse.mybir as mybir
from concourse.alu_op_type import AluOpType
from concourse.bass_utils import run_bass_kernel_spmd

F32 = mybir.dt.float32
BF16 = mybir.dt.bfloat16
I32 = mybir.dt.int32
AF = mybir.ActivationFunctionType
AX = mybir.AxisListType

P = 128
D = 2048
S = 2048
NT = 16
NS = 2
H = 16
FF = 8192
NTT = NS * NT
GQ = 64

SC_ATT = 1.0 / math.sqrt(128.0)
EPS = 1e-5
C2HI = float(np.float32(2.0 * math.pi))
C2LO = float(np.float32(2.0 * math.pi - np.float64(np.float32(2.0 * math.pi))))
INV2PI = float(np.float32(1.0 / (2.0 * math.pi)))
PI_CLAMP = 3.1415925
LNTH = math.log(10000.0) / 64.0
LNHI = float(np.float32(LNTH))
LNLO = float(np.float32(LNTH - np.float64(np.float32(LNTH))))

_DBG = False


def _span(ap):
    try:
        t = ap.tensor
        tn = type(t).__name__
        if "SB" not in tn and "PSum" not in tn:
            return None
        isz = mybir.dt.size(ap.dtype)
        lo = ap.offset * isz
        ext = 1
        for step, cnt in ap.ap[1:]:
            ext += abs(step) * (cnt - 1)
        return (t.name, lo, lo + ext * isz)
    except Exception:
        return None


class EP:

    def __init__(self, e):
        self.e = e
        self.w = []
        self.r = []

    @staticmethod
    def _hits(span, lst):
        if span is None:
            return False
        n, lo, hi = span
        for (n2, lo2, hi2) in lst:
            if n == n2 and lo < hi2 and lo2 < hi:
                return True
        return False

    def _go(self, outs, ins, f, *a, **k):
        on = [s for s in (_span(o) for o in outs) if s]
        inn = [s for s in (_span(i) for i in ins
                           if not isinstance(i, (int, float, type(None)))) if s]
        if any(self._hits(s, self.w) for s in inn) or \
           any(self._hits(s, self.w) or self._hits(s, self.r) for s in on):
            self.e.drain()
            self.w, self.r = [], []
        self.w.extend(on)
        self.r.extend(inn)
        return f(*a, **k)

    def wait_ge(self, sem, v):
        return self.e.wait_ge(sem, v)

    def drain(self):
        self.w, self.r = [], []
        return self.e.drain()

    def nop(self):
        return self.e.nop()

    def dma_start(self, **k):
        return self.e.dma_start(**k)

    def memset(self, ap, c):
        return self._go([ap], [], self.e.memset, ap, c)

    def tensor_copy(self, out, in_):
        return self._go([out], [in_], self.e.tensor_copy, out=out, in_=in_)

    def tensor_tensor(self, out, in0, in1, op):
        return self._go([out], [in0, in1], self.e.tensor_tensor, out=out,
                        in0=in0, in1=in1, op=op)

    def tensor_scalar(self, out, in0, scalar1, scalar2, op0, op1=None,
                      accum_out=None):
        kw = dict(out=out, in0=in0, scalar1=scalar1, scalar2=scalar2, op0=op0)
        if op1 is not None:
            kw["op1"] = op1
        if accum_out is not None:
            kw["accum_out"] = accum_out
        return self._go([out, accum_out], [in0, scalar1, scalar2],
                        self.e.tensor_scalar, **kw)

    def scalar_tensor_tensor(self, out, in0, scalar, in1, op0, op1):
        return self._go([out], [in0, scalar, in1], self.e.scalar_tensor_tensor,
                        out=out, in0=in0, scalar=scalar, in1=in1, op0=op0,
                        op1=op1)

    def reduce_sum(self, out, in_, axis, apply_absolute_value=None):
        kw = dict(out=out, in_=in_, axis=axis)
        if apply_absolute_value is not None:
            kw["apply_absolute_value"] = apply_absolute_value
        return self._go([out], [in_], self.e.reduce_sum, **kw)

    def reduce_max(self, out, in_, axis, negate=None):
        kw = dict(out=out, in_=in_, axis=axis)
        if negate is not None:
            kw["negate"] = negate
        return self._go([out], [in_], self.e.reduce_max, **kw)

    def tensor_reduce(self, out, in_, axis, op, apply_absolute_value=None):
        kw = dict(out=out, in_=in_, axis=axis, op=op)
        if apply_absolute_value is not None:
            kw["apply_absolute_value"] = apply_absolute_value
        return self._go([out], [in_], self.e.tensor_reduce, **kw)

    def reciprocal(self, out, in_):
        return self._go([out], [in_], self.e.reciprocal, out=out, in_=in_)

    def activation(self, out, in_, func, bias=None, scale=None, accum_out=None):
        kw = dict(out=out, in_=in_, func=func)
        if bias is not None:
            kw["bias"] = bias
        if scale is not None:
            kw["scale"] = scale
        if accum_out is not None:
            kw["accum_out"] = accum_out
        return self._go([out, accum_out], [in_, bias, scale],
                        self.e.activation, **kw)

    def copy(self, out, in_):
        return self._go([out], [in_], self.e.copy, out=out, in_=in_)

    def sign(self, out, in_):
        return self._go([out], [in_], self.e.sign, out=out, in_=in_)

    def iota(self, out, pattern, base, channel_multiplier,
             allow_small_or_imprecise_dtypes=True):
        return self._go([out], [], self.e.iota, out=out, pattern=pattern,
                        base=base, channel_multiplier=channel_multiplier,
                        allow_small_or_imprecise_dtypes=allow_small_or_imprecise_dtypes)

    def affine_select(self, out, in_, pattern, compare_op, fill, base,
                      channel_multiplier):
        return self._go([out], [in_], self.e.affine_select, out=out, in_=in_,
                        pattern=pattern, compare_op=compare_op, fill=fill,
                        base=base, channel_multiplier=channel_multiplier)


ARENA_COLS = 30720


class B:
    def __init__(self, nc):
        self.nc = nc
        d = nc.dram_tensor
        self.x_h = d("x", [NS, S, D], F32, kind="ExternalInput").ap()
        self.wqt_h = d("wqt", [D, D], F32, kind="ExternalInput").ap()
        self.wkt_h = d("wkt", [D, D], F32, kind="ExternalInput").ap()
        self.wvt_h = d("wvt", [D, D], F32, kind="ExternalInput").ap()
        self.wot_h = d("wot", [D, D], F32, kind="ExternalInput").ap()
        self.w1t_h = d("w1t", [D, FF], F32, kind="ExternalInput").ap()
        self.w2t_h = d("w2t", [D, FF], F32, kind="ExternalInput").ap()
        self.w3t_h = d("w3t", [FF, D], F32, kind="ExternalInput").ap()
        self.out_h = d("out", [NS, S, D], F32, kind="ExternalOutput").ap()
        ks = "ExternalOutput" if _DBG else "Internal"
        self.xt_h = d("xt_s", [NS, NT, 16, P, P], BF16, kind=ks).ap()
        self.tab_h = d("tab_s", [NT, 2, P, H * GQ], F32, kind=ks).ap()
        self.q_h = d("q_s", [NS, H, NT, P, P], BF16, kind=ks).ap()
        self.k_h = d("k_s", [NS, H, NT, P, P], BF16, kind=ks).ap()
        self.v_h = d("v_s", [NS, H, NT, P, P], BF16, kind=ks).ap()
        self.x2_h = d("x2_s", [NS, S, D], F32, kind=ks).ap()
        self.xq_h = d("xq_s", [NTT, 16, P, P], BF16, kind=ks).ap()
        self.hq_h = d("hq_s", [64, NTT, P, P], BF16, kind=ks).ap()
        self.w3s_h = d("w3s_s", [64, P, D], BF16, kind=ks).ap()

        sb = nc.alloc_sbuf_tensor
        self.wA = sb("wA", [P, 16 * D], BF16)
        self.arena = sb("arena", [P, ARENA_COLS], F32)
        self.ident = sb("ident", [P, P], BF16)
        self.ident_f = sb("ident_f", [P, P], F32)
        self.zero_bf = sb("zero_bf", [P, P], BF16)
        self.mask_neg = sb("mask_neg", [P, P], F32)
        self.mask_bf = sb("mask_bf", [P, P], BF16)
        self.ones_bf = sb("ones_bf", [P, 1], BF16)
        self.ones_f = sb("ones_f", [P, 1], F32)
        self.onesrow = sb("onesrow", [1, P], F32)
        self.negm_row = sb("negm_row", [1, P * NTT], BF16)
        self.negm_c = sb("negm_c", [P, NTT], F32)
        self.r_c = sb("r_c", [P, NTT], F32)
        self.negm2_c = sb("negm2_c", [P, NTT], F32)
        self.r2_c = sb("r2_c", [P, NTT], F32)
        self.s1acc = sb("s1acc", [P, 1], F32)
        self.s2acc = sb("s2acc", [P, 1], F32)
        self.s3acc = sb("s3acc", [P, 1], F32)
        self.s1_col = sb("s1_col", [P, 1], F32)
        self.s23_col = sb("s23_col", [P, 1], F32)
        self.csw = sb("csw", [1, D], BF16)
        self.st = sb("st", [P, 16], F32)
        self.qs = sb("qs", [P, 4 * GQ], F32)
        self.ssum = sb("ssum", [P, 4], F32)
        self.sbias = sb("sbias", [P, 4], F32)
        self.srs = sb("srs", [P, 4], F32)
        self.posr = sb("posr", [1, P], F32)
        self.dum = sb("dum", [1, 2], F32)
        self.cst_b = sb("cst_b", [P, P], BF16)
        self.cst_f = sb("cst_f", [P, P], F32)
        self.cst_z = sb("cst_z", [P, P], F32)

        def cv(off, cols, dt=F32):
            a = self.arena[:, off:off + cols]
            if dt is F32:
                return a
            return a.bitcast(dt)
        self.xin = [cv(0, 2048), cv(2048, 2048)]
        self.ptok = cv(4096, 2048)
        self.sqb = self.ptok
        self.wstr = [cv(6144, 2048), cv(8192, 2048)]
        self.invf_full = cv(6144, 1024)
        self.stage = [cv(10240, 1024, BF16), cv(11264, 1024, BF16)]
        self.xbf = cv(12288, 1024, BF16)
        self.xtt = [cv(13312, 1024, BF16), cv(14336, 1024, BF16)]
        self.qbf = cv(15360, 1024, BF16)
        self.tA = cv(16384, 1024)
        self.tB = cv(17408, 1024)
        self.tC = cv(18432, 1024)
        self.tki = cv(19456, 1024, I32)
        self.cosb = cv(20480, 1024)
        self.sinb = cv(21504, 1024)
        self.ctab = [cv(22528, 1024), cv(23552, 1024)]
        self.stab = [cv(24576, 1024), cv(25600, 1024)]
        self.re0 = cv(26624, 1024)
        self.re1 = cv(27648, 1024)
        self.re2 = cv(28672, 1024)
        self.re3 = cv(29696, 1024)
        self.wB = cv(0, 16384, BF16)
        self.Kb = [cv(16384, 1024, BF16), cv(17408, 1024, BF16)]
        self.Qb = [cv(18432, 1024, BF16), cv(19456, 1024, BF16)]
        self.Vb = [cv(20480, 1024, BF16), cv(21504, 1024, BF16)]
        self.probsA = [cv(22528 + 1024 * i, 1024, BF16) for i in range(4)]
        self.probsB = [cv(26624 + 768 * i, 768, BF16) for i in range(4)]
        self.pTe = [cv(29696, 512, BF16), cv(30208, 512, BF16)]
        self.xres = cv(16384, 2048)
        self.x2t = [cv(18432, 2048), cv(20480, 2048)]
        self.xn2 = cv(22528, 2048)
        self.k0i4 = cv(24576, 1024, I32)
        self.rr4 = cv(25600, 1024)
        self.u14 = cv(26624, 1024)
        self.xqb = cv(27648, 1024, BF16)
        self.st4 = [cv(28672, 1024, BF16), cv(29696, 1024, BF16)]
        self.xqt = [cv(16384, 1024, BF16), cv(17408, 1024, BF16)]
        self.z1s = cv(18432, 1024)
        self.hh = cv(19456, 1024)
        self.k0i5 = cv(20480, 1024, I32)
        self.rr5 = cv(21504, 1024)
        self.u15 = cv(22528, 1024)
        self.hqb = cv(23552, 1024, BF16)
        self.st5 = [cv(24576, 1024, BF16), cv(25600, 1024, BF16)]
        self.w5 = [cv(26624, 2048), cv(28672, 2048)]
        self.w6 = [cv(16384, 2048), cv(18432, 2048)]
        self.w3sb = [cv(20480, 1024, BF16), cv(21504, 1024, BF16)]
        self.acc = cv(0, 16384)
        self.hqt = [cv(16384, 1024, BF16), cv(17408, 1024, BF16)]
        self.x2r = [cv(18432, 2048), cv(20480, 2048)]
        self.outb = [cv(22528, 2048), cv(24576, 2048)]

        ps = nc.alloc_psum_tensor
        self.pQ = ps("pQ", [P, D], F32)
        self.pT = ps("pT", [P, D], BF16)
        self.pAng = ps("pAng", [P, 1024], F32)
        self.pQb = self.pQ[:].bitcast(BF16)
        self.pTf = self.pT[:].bitcast(F32)

        for name in ["s0sq", "s0sqr", "s0g", "s0r", "s0xb", "s0tp",
                     "s0ev", "s0w1r", "sIv", "sTps", "sTan", "sTev",
                     "sTrr", "sTsc", "s1w", "s1cs", "s1cse",
                     "s1mm", "s1ev", "s1rp", "s1tr", "s1te",
                     "s3dn", "s3sc", "s3mx", "s3ex", "s3pn", "s3mg", "s3me", "s3tp", "s3tev",
                     "s3av", "s3ae",
                     "s4w", "s4mm", "s4x2", "s4sq", "s4g", "s4r", "s4q",
                     "s4tr", "s4te",
                     "s5sg", "s5ac", "s5mm", "s5m0", "s5z0", "s5z1", "s5e1", "s5h", "s5q", "s5tr", "s5te",
                     "s5s1c",
                     "s6sg", "s6ac", "s6sc",
                     "s7mm", "s7ac", "s7ev"]:
            setattr(self, name, nc.alloc_semaphore(name))
        for name in ["s0x", "s0sp", "s0w1", "sTsp", "s1xt", "s1tb", "s1sp",
                     "s3kv", "s4x", "s4spx", "s4sp", "s5w", "s5xq", "s5sp",
                     "s6w", "s6sp", "s7w", "s7h", "s7x2", "s7out"]:
            setattr(self, name, [nc.alloc_semaphore(name + "A"),
                                 nc.alloc_semaphore(name + "B")])

    def round_(self, e, t, k0i, r, u1, out):
        e.tensor_copy(out=k0i, in_=t)
        e.tensor_tensor(out=r, in0=t, in1=k0i, op=AluOpType.subtract)
        e.tensor_scalar(out=u1, in0=r, scalar1=0.5, scalar2=-1.0,
                        op0=AluOpType.is_ge, op1=AluOpType.add)
        e.tensor_tensor(out=u1, in0=u1, in1=k0i, op=AluOpType.add)
        e.scalar_tensor_tensor(out=out, in0=r, scalar=-0.5, in1=u1,
                               op0=AluOpType.is_ge, op1=AluOpType.add)

    def quant_(self, e, src, dst, k0i, r, u1):
        g = 16
        sg = self.qs[:, 0:g]
        inv = self.qs[:, g:2 * g]
        sinv = self.qs[:, 2 * g:3 * g]
        e.tensor_reduce(out=sg, in_=src.rearrange("p (g q) -> p g q", q=GQ),
                        axis=AX.X, op=AluOpType.max, apply_absolute_value=True)
        e.tensor_scalar(out=sg, in0=sg, scalar1=EPS, scalar2=None,
                        op0=AluOpType.max)
        e.reciprocal(out=inv, in_=sg)
        e.tensor_scalar(out=inv, in0=inv, scalar1=127.0, scalar2=None,
                        op0=AluOpType.mult)
        e.tensor_scalar(out=sinv, in0=sg, scalar1=1.0 / 127.0, scalar2=None,
                        op0=AluOpType.mult)
        ib = inv.unsqueeze(2).broadcast_to([P, g, GQ])
        sb_ = sinv.unsqueeze(2).broadcast_to([P, g, GQ])
        s3 = src.rearrange("p (g q) -> p g q", q=GQ)
        t = r
        e.tensor_tensor(out=t.rearrange("p (g q) -> p g q", q=GQ), in0=s3,
                        in1=ib, op=AluOpType.mult)
        e.tensor_copy(out=k0i, in_=t)
        e.tensor_tensor(out=u1, in0=t, in1=k0i, op=AluOpType.subtract)
        e.tensor_scalar(out=t, in0=u1, scalar1=0.5, scalar2=-1.0,
                        op0=AluOpType.is_ge, op1=AluOpType.add)
        e.tensor_tensor(out=t, in0=t, in1=k0i, op=AluOpType.add)
        e.scalar_tensor_tensor(out=t, in0=u1, scalar=-0.5, in1=t,
                               op0=AluOpType.is_ge, op1=AluOpType.add)
        return e.tensor_tensor(out=dst.rearrange("p (g q) -> p g q", q=GQ),
                               in0=t.rearrange("p (g q) -> p g q", q=GQ),
                               in1=sb_, op=AluOpType.mult)

    def init_(self, eng, e):
        if eng == "vector":
            e.memset(self.ones_bf[:], 1.0)
            e.memset(self.ones_f[:], 1.0)
            e.memset(self.zero_bf[:], 0.0)
            e.memset(self.s1acc[:], 0.0)
            e.memset(self.s2acc[:], 0.0)
            e.memset(self.s3acc[:], 0.0)
            e.memset(self.onesrow[:], 1.0)
            e.memset(self.cst_b[:], 1.0)
            e.memset(self.cst_f[:], 1.0)
            e.memset(self.cst_z[:], 0.0).then_inc(self.sIv, 1)
        elif eng == "gpsimd":
            e.wait_ge(self.sIv, 1)
            e.affine_select(out=self.ident[:], in_=self.cst_b[:],
                            pattern=[[-1, P]], compare_op=AluOpType.is_equal,
                            fill=0.0, base=0, channel_multiplier=1)
            e.affine_select(out=self.ident_f[:], in_=self.cst_f[:],
                            pattern=[[-1, P]], compare_op=AluOpType.is_equal,
                            fill=0.0, base=0, channel_multiplier=1)
            e.affine_select(out=self.mask_neg[:], in_=self.cst_z[:],
                            pattern=[[-1, P]], compare_op=AluOpType.is_ge,
                            fill=-3.0e38, base=0, channel_multiplier=1)
            e.affine_select(out=self.mask_bf[:], in_=self.zero_bf[:],
                            pattern=[[-1, P]], compare_op=AluOpType.is_ge,
                            fill=-3.0e38, base=0, channel_multiplier=1)
            e.memset(self.dum[:], 0.0).then_inc(self.sIv, 1)

    def p0(self, eng, e):
        if eng == "sync":
            def spill(m):
                e.wait_ge(self.s0ev, m + 1)
                e.dma_start(out=self.xt_h[m // NT, m % NT].rearrange("i p e -> p i e"),
                            in_=self.stage[m % 2][:].rearrange("p (i e) -> p i e", e=P)
                            ).then_inc(self.s0sp[m % 2], 16)
            for n in range(NTT):
                s, t = divmod(n, NT)
                if n >= 2:
                    spill(n - 2)
                    e.wait_ge(self.s0xb, n - 1)
                    e.wait_ge(self.s0sq, n - 1)
                e.dma_start(out=self.xin[n % 2][:],
                            in_=self.x_h[s, t * P:(t + 1) * P, :]
                            ).then_inc(self.s0x[n % 2], 16)
            spill(NTT - 2)
            spill(NTT - 1)
        elif eng == "scalar":
            e.wait_ge(self.sIv, 2)
            for n in range(NTT):
                for c in range(2 * n, 2 * n + 2):
                    if c >= 2:
                        e.wait_ge(self.s0w1r, c - 1)
                    e.dma_start(out=self.wstr[c % 2][:],
                                in_=self.w1t_h[(c % 16) * P:(c % 16 + 1) * P,
                                               (c // 16) * D:(c // 16 + 1) * D]
                                ).then_inc(self.s0w1[c % 2], 16)
                e.wait_ge(self.s0x[n % 2], 16 * (n // 2 + 1))
                if n >= 1:
                    e.wait_ge(self.s0sqr, n)
                e.activation(out=self.sqb[:], in_=self.xin[n % 2][:],
                             func=AF.Square).then_inc(self.s0sq, 1)
                e.wait_ge(self.s0g, n + 1)
                e.activation(out=self.r_c[:, n:n + 1], in_=self.st[:, 2:3],
                             func=AF.Sqrt).then_inc(self.s0r, 1)
                e.wait_ge(self.s0tp, n + 1)
                if n >= 2:
                    e.wait_ge(self.s0sp[n % 2], 16 * (n // 2))
                e.copy(out=self.stage[n % 2][:], in_=self.pT[:])
                e.copy(out=self.negm_row[0:1, n * P:(n + 1) * P],
                       in_=self.pAng[0:1, 0:P]).then_inc(self.s0ev, 1)
        elif eng == "gpsimd":
            pass
        elif eng == "vector":
            for n in range(NTT):
                e.wait_ge(self.s0x[n % 2], 16 * (n // 2 + 1))
                e.reduce_sum(out=self.st[:, 0:1], in_=self.xin[n % 2][:], axis=AX.X)
                if n >= 1:
                    e.wait_ge(self.s0tp, n)
                e.tensor_copy(out=self.xbf[:], in_=self.xin[n % 2][:]
                              ).then_inc(self.s0xb, 1)
                e.tensor_scalar(out=self.negm_c[:, n:n + 1], in0=self.st[:, 0:1],
                                scalar1=-1.0 / D, scalar2=None, op0=AluOpType.mult)
                e.wait_ge(self.s0sq, n + 1)
                e.reduce_sum(out=self.st[:, 1:2], in_=self.sqb[:],
                             axis=AX.X).then_inc(self.s0sqr, 1)
                e.tensor_tensor(out=self.st[:, 3:4], in0=self.negm_c[:, n:n + 1],
                                in1=self.negm_c[:, n:n + 1], op=AluOpType.mult)
                if n >= 1:
                    e.wait_ge(self.s0r, n)
                e.tensor_scalar(out=self.st[:, 2:3], in0=self.st[:, 1:2],
                                scalar1=1.0 / D, scalar2=None, op0=AluOpType.mult)
                e.tensor_tensor(out=self.st[:, 2:3], in0=self.st[:, 2:3],
                                in1=self.st[:, 3:4], op=AluOpType.subtract)
                e.tensor_scalar(out=self.st[:, 2:3], in0=self.st[:, 2:3],
                                scalar1=EPS, scalar2=None, op0=AluOpType.add)
                e.reciprocal(out=self.st[:, 2:3],
                             in_=self.st[:, 2:3]).then_inc(self.s0g, 1)
                for c in range(2 * n, 2 * n + 2):
                    e.wait_ge(self.s0w1[c % 2], 16 * (c // 2 + 1))
                    e.reduce_sum(out=self.st[:, 4:5], in_=self.wstr[c % 2][:],
                                 axis=AX.X, apply_absolute_value=True)
                    e.tensor_tensor(out=self.s1acc[:], in0=self.s1acc[:],
                                    in1=self.st[:, 4:5],
                                    op=AluOpType.add).then_inc(self.s0w1r, 1)
        elif eng == "tensor":
            e.wait_ge(self.sIv, 2)
            for n in range(NTT):
                e.wait_ge(self.s0xb, n + 1)
                if n >= 1:
                    e.wait_ge(self.s0ev, n)
                for c in range(16):
                    e.transpose(out=self.pT[:, c * P:(c + 1) * P],
                                in_=self.xbf[:, c * P:(c + 1) * P],
                                identity=self.ident[:])
                e.wait_ge(self.s0g, n + 1)
                e.transpose(out=self.pAng[0:1, 0:P],
                            in_=self.negm_c[:, n:n + 1],
                            identity=self.ident_f[:]).then_inc(self.s0tp, 1)

    def p05(self, eng, e):
        pos = self.st[:, 8:9]
        if eng == "gpsimd":
            e.wait_ge(self.s0w1r, 2 * NTT)
            e.iota(out=self.tA[:].rearrange("p (h q) -> p h q", q=GQ),
                   pattern=[[0, H], [1, GQ]], base=0, channel_multiplier=0,
                   allow_small_or_imprecise_dtypes=True).then_inc(self.sIv, 1)
            for t in range(NT):
                if t >= 1:
                    e.wait_ge(self.sTan, t)
                e.iota(out=pos, pattern=[[0, 1]], base=t * P,
                       channel_multiplier=1,
                       allow_small_or_imprecise_dtypes=True).then_inc(self.sTps, 1)
        elif eng == "scalar":
            e.wait_ge(self.sIv, 4)
            e.activation(out=self.invf_full[:], in_=self.tC[:],
                         func=AF.Exp).then_inc(self.sIv, 1)
            for t in range(NT):
                e.wait_ge(self.sTrr, 2 * t + 2)
                if t >= 1:
                    e.wait_ge(self.sTsp[0], 16 * t)
                    e.wait_ge(self.sTsp[1], 16 * t)
                e.activation(out=self.sinb[:], in_=self.tB[:], func=AF.Sin)
                e.activation(out=self.cosb[:], in_=self.tC[:],
                             func=AF.Sin).then_inc(self.sTsc, 1)
        elif eng == "sync":
            for t in range(NT):
                e.wait_ge(self.sTsc, t + 1)
                e.dma_start(out=self.tab_h[t, 0],
                            in_=self.cosb[:]).then_inc(self.sTsp[0], 16)
                e.dma_start(out=self.tab_h[t, 1],
                            in_=self.sinb[:]).then_inc(self.sTsp[1], 16)
        elif eng == "vector":
            e.wait_ge(self.sIv, 3)
            e.tensor_scalar(out=self.tB[:], in0=self.tA[:], scalar1=-LNHI,
                            scalar2=None, op0=AluOpType.mult)
            e.scalar_tensor_tensor(out=self.tC[:], in0=self.tA[:], scalar=-LNLO,
                                   in1=self.tB[:], op0=AluOpType.mult,
                                   op1=AluOpType.add).then_inc(self.sIv, 1)
            for t in range(NT):
                e.wait_ge(self.sTps, t + 1)
                if t == 0:
                    e.wait_ge(self.sIv, 5)
                if t >= 1:
                    e.wait_ge(self.sTsc, t)
                e.tensor_scalar(out=self.tA[:], in0=self.invf_full[:],
                                scalar1=pos, scalar2=None,
                                op0=AluOpType.mult).then_inc(self.sTan, 1)
                self._rred(e, shift=0.0, dst=self.tB)
                self._rred(e, shift=float(np.float32(math.pi / 2.0)), dst=self.tC)

    def _rred(self, e, shift, dst):
        a = self.tA[:]
        if shift != 0.0:
            e.tensor_scalar(out=self.re0[:], in0=a, scalar1=shift, scalar2=None,
                            op0=AluOpType.add)
            a = self.re0[:]
        e.tensor_scalar(out=self.re1[:], in0=a, scalar1=INV2PI, scalar2=None,
                        op0=AluOpType.mult)
        self.round_(e, self.re1[:], self.tki[:], self.re2[:], self.re3[:],
                    self.re1[:])
        e.scalar_tensor_tensor(out=self.re2[:], in0=self.re1[:], scalar=-C2HI,
                               in1=a, op0=AluOpType.mult, op1=AluOpType.add)
        e.scalar_tensor_tensor(out=self.re3[:], in0=self.re1[:], scalar=-C2LO,
                               in1=self.re2[:], op0=AluOpType.mult,
                               op1=AluOpType.add)
        e.tensor_scalar(out=dst[:], in0=self.re3[:], scalar1=-PI_CLAMP,
                        scalar2=PI_CLAMP, op0=AluOpType.max,
                        op1=AluOpType.min).then_inc(self.sTrr, 1)

    def p1(self, eng, e, which):
        w_h = {"q": self.wqt_h, "k": self.wkt_h, "v": self.wvt_h}[which]
        dst_h = {"q": self.q_h, "k": self.k_h, "v": self.v_h}[which]
        rope = which in ("q", "k")
        base = {"q": 0, "k": 1, "v": 2}[which]
        rbase = {"q": 0, "k": 1, "v": 0}[which]
        if eng == "gpsimd":
            if base >= 1:
                e.wait_ge(self.s1mm, base * NTT)
            e.dma_start(out=self.wA[:].rearrange("p (c o) -> p c o", o=D),
                        in_=w_h.rearrange("(c p) o -> p c o", p=P)
                        ).then_inc(self.s1w, 16)
        elif eng == "tensor":
            e.wait_ge(self.s1w, 16 * (base + 1))
            if base >= 1:
                e.wait_ge(self.s1cse, base)
            for oc in range(4):
                for i in range(16):
                    mm = e.matmul(out=self.pQ[0:1, oc * 512:(oc + 1) * 512],
                                  lhsT=self.ones_bf[:],
                                  rhs=self.wA[:, i * D + oc * 512:i * D + (oc + 1) * 512],
                                  start=(i == 0), stop=(i == 15))
            mm.then_inc(self.s1cs, 1)
            e.wait_ge(self.s1cse, base + 1)
            for n in range(NTT):
                idx = base * NTT + n
                e.wait_ge(self.s1xt[idx % 2], 16 * (idx // 2 + 1))
                if idx >= 1:
                    e.wait_ge(self.s1ev, idx)
                for i in range(16):
                    for oc in range(4):
                        e.matmul(out=self.pQ[:, oc * 512:(oc + 1) * 512],
                                 lhsT=self.xtt[n % 2][:, i * P:(i + 1) * P],
                                 rhs=self.wA[:, i * D + oc * 512:i * D + (oc + 1) * 512],
                                 start=(i == 0), stop=False)
                for oc in range(4):
                    mm = e.matmul(out=self.pQ[:, oc * 512:(oc + 1) * 512],
                                  lhsT=self.negm_row[0:1, n * P:(n + 1) * P],
                                  rhs=self.csw[0:1, oc * 512:(oc + 1) * 512],
                                  start=False, stop=True)
                mm.then_inc(self.s1mm, 1)
                if rope and n >= 1:
                    self._p1_trans(e, rbase * NTT + n - 1)
            if rope:
                self._p1_trans(e, rbase * NTT + NTT - 1)
        elif eng == "sync":
            if base == 0:
                e.wait_ge(self.s0sp[0], 16 * (NTT // 2))
                e.wait_ge(self.s0sp[1], 16 * (NTT // 2))
            for n in range(NTT):
                idx = base * NTT + n
                s, t = divmod(n, NT)
                if idx >= 2:
                    e.wait_ge(self.s1mm, idx - 1)
                e.dma_start(out=self.xtt[n % 2][:].rearrange("p (i q) -> p i q", q=P),
                            in_=self.xt_h[s, t].rearrange("i p q -> p i q")
                            ).then_inc(self.s1xt[idx % 2], 16)
                if rope:
                    if base == 0 and n == 0:
                        e.wait_ge(self.sTsp[0], 16 * 16)
                        e.wait_ge(self.sTsp[1], 16 * 16)
                    ridx = rbase * NTT + n
                    if ridx >= 2:
                        e.wait_ge(self.s1rp, ridx - 1)
                    e.dma_start(out=self.ctab[n % 2][:], in_=self.tab_h[t, 0]
                                ).then_inc(self.s1tb[ridx % 2], 16)
                    e.dma_start(out=self.stab[n % 2][:], in_=self.tab_h[t, 1]
                                ).then_inc(self.s1tb[ridx % 2], 16)
        elif eng == "scalar":
            e.wait_ge(self.s1cs, base + 1)
            e.copy(out=self.csw[0:1, :],
                   in_=self.pQ[0:1, 0:D]).then_inc(self.s1cse, 1)
            for n in range(NTT):
                idx = base * NTT + n
                e.wait_ge(self.s1mm, idx + 1)
                if rope:
                    ridx = rbase * NTT + n
                    if n >= 1:
                        self._p1_evac(e, which, ridx - 1, dst_h)
                    e.activation(out=self.ptok[:], in_=self.pQ[:, 0:D],
                                 func=AF.Copy,
                                 scale=self.r_c[:, n:n + 1]).then_inc(self.s1ev, 1)
                else:
                    if idx >= 2:
                        e.wait_ge(self.s1sp[idx % 2], 16 * (idx // 2))
                    e.activation(out=self.stage[n % 2][:], in_=self.pQ[:, 0:D],
                                 func=AF.Copy,
                                 scale=self.r_c[:, n:n + 1]).then_inc(self.s1ev, 1)
                    e.wait_ge(self.s1ev, base * NTT + n + 1)
                    e.dma_start(out=dst_h[n // NT, :, n % NT].rearrange("h p q -> p h q"),
                                in_=self.stage[n % 2][:].rearrange(
                                    "p (h q) -> p h q", q=P)
                                ).then_inc(self.s1sp[idx % 2], 16)
            if rope:
                self._p1_evac(e, which, rbase * NTT + NTT - 1, dst_h)
        elif eng == "vector":
            if not rope:
                return
            for n in range(NTT):
                idx = base * NTT + n
                ridx = rbase * NTT + n
                e.wait_ge(self.s1ev, idx + 1)
                e.wait_ge(self.s1tb[ridx % 2], 32 * (ridx // 2 + 1))
                if ridx >= 1:
                    e.wait_ge(self.s1tr, ridx)
                pe_ = self.ptok[:].rearrange("p (c two) -> p c two", two=2)
                qe, qo = pe_[:, :, 0:1], pe_[:, :, 1:2]
                ct = self.ctab[n % 2][:].unsqueeze(2)
                st_ = self.stab[n % 2][:].unsqueeze(2)
                r0 = self.re0[:].unsqueeze(2)
                r1 = self.re1[:].unsqueeze(2)
                r2 = self.re2[:].unsqueeze(2)
                r3 = self.re3[:].unsqueeze(2)
                e.tensor_tensor(out=r0, in0=qe, in1=ct, op=AluOpType.mult)
                e.tensor_tensor(out=r1, in0=qo, in1=st_, op=AluOpType.mult)
                e.tensor_tensor(out=r2, in0=qe, in1=st_, op=AluOpType.mult)
                e.tensor_tensor(out=r3, in0=qo, in1=ct, op=AluOpType.mult)
                ob = self.qbf[:].rearrange("p (c two) -> p c two", two=2)
                e.tensor_tensor(out=ob[:, :, 0:1], in0=r0, in1=r1,
                                op=AluOpType.subtract)
                e.tensor_tensor(out=ob[:, :, 1:2], in0=r2, in1=r3,
                                op=AluOpType.add).then_inc(self.s1rp, 1)

    def _p1_trans(self, e, ridx):
        e.wait_ge(self.s1rp, ridx + 1)
        if ridx >= 1:
            e.wait_ge(self.s1te, ridx)
        for c in range(16):
            tr = e.transpose(out=self.pT[:, c * P:(c + 1) * P],
                             in_=self.qbf[:, c * P:(c + 1) * P],
                             identity=self.ident[:])
        tr.then_inc(self.s1tr, 1)

    def _p1_evac(self, e, which, ridx, dst_h):
        base = {"q": 0, "k": 1}[which]
        n = ridx - base * NTT
        idx = base * NTT + n
        e.wait_ge(self.s1tr, ridx + 1)
        if idx >= 2:
            e.wait_ge(self.s1sp[idx % 2], 16 * (idx // 2))
        e.copy(out=self.stage[n % 2][:], in_=self.pT[:]).then_inc(self.s1te, 1)
        e.wait_ge(self.s1te, ridx + 1)
        e.dma_start(out=dst_h[n // NT, :, n % NT].rearrange("h p q -> p h q"),
                    in_=self.stage[n % 2][:].rearrange("p (h q) -> p h q", q=P)
                    ).then_inc(self.s1sp[idx % 2], 16)

    def probs_ap(self, g, tg):
        return (self.probsA[tg] if g % 2 == 1 else self.probsB[tg])

    @staticmethod
    def _p3_nch(tau):
        return 1 if tau < 8 else 2

    @classmethod
    def _p3_chc(cls, gh, tau, k):
        per_h = sum(cls._p3_nch(t) for t in range(NT))
        c = gh * per_h
        for t in range(tau):
            c += cls._p3_nch(t)
        return c + k

    @staticmethod
    def _p3_nbat(g):
        return 2 * g + 2

    @classmethod
    def _p3_tpc(cls, gh, g, b):
        c = gh * 20
        for g2 in range(g):
            c += cls._p3_nbat(g2)
        return c + b

    def _p3_batch_slices(self, g, tg):
        if g == 0:
            return []
        nb = self._p3_nbat(g - 1)
        lo = (nb * tg) // 4
        hi = (nb * (tg + 1)) // 4
        return list(range(lo, hi))

    def p3(self, eng, e, s):
        def hh(h):
            return s * H + h

        def tauc(h, tau):
            return hh(h) * NT + tau

        def gc(h, g):
            return hh(h) * 4 + g

        pS = [self.pQ[:, 0:1024], self.pQ[:, 1024:2048]]
        pTb = [self.pT[:, 0:1024], self.pT[:, 1024:2048]]

        if eng == "sync":
            if s == 0:
                for par in range(2):
                    e.wait_ge(self.s1sp[par], 16 * 3 * NTT // 2)
            else:
                for par in range(2):
                    e.wait_ge(self.s4spx[par], 16 * (NT // 2))
            for h in range(H):
                gh = hh(h)
                if gh >= 2:
                    e.wait_ge(self.s3dn, gh - 1)
                e.dma_start(out=self.Kb[h % 2][:].rearrange("p (t q) -> p t q", q=P),
                            in_=self.k_h[s, h].rearrange("t p q -> p t q")
                            ).then_inc(self.s3kv[h % 2], 16)
                e.dma_start(out=self.Qb[h % 2][:].rearrange("p (t q) -> p t q", q=P),
                            in_=self.q_h[s, h].rearrange("t p q -> p t q")
                            ).then_inc(self.s3kv[h % 2], 16)
                e.dma_start(out=self.Vb[h % 2][:].rearrange("p (t q) -> p t q", q=P),
                            in_=self.v_h[s, h].rearrange("t p q -> p t q")
                            ).then_inc(self.s3kv[h % 2], 16)
        elif eng == "tensor":
            for h in range(H):
                gh = hh(h)
                e.wait_ge(self.s3kv[h % 2], 48 * (gh // 2 + 1))
                for g in range(4):
                    for tg in range(4):
                        tau = 4 * g + tg
                        kw = (tau + 1) * P
                        nch = self._p3_nch(tau)
                        for k in range(nch):
                            cg = self._p3_chc(gh, tau, k)
                            par = cg % 2
                            if cg >= 2:
                                e.wait_ge(self.s3ex, cg - 1)
                            cw = min(1024, kw - 1024 * k)
                            dst = pS[par]
                            nkc = (cw + 511) // 512
                            for kc in range(nkc):
                                w = min(512, cw - kc * 512)
                                last = (kc == nkc - 1)
                                mm = e.matmul(
                                    out=dst[:, kc * 512:kc * 512 + w],
                                    lhsT=self.Qb[h % 2][:, tau * P:(tau + 1) * P],
                                    rhs=self.Kb[h % 2][:, 1024 * k + kc * 512:
                                                       1024 * k + kc * 512 + w],
                                    start=True, stop=not (last and k == nch - 1))
                            if k == nch - 1:
                                off = kw - 1024 * k - P
                                mm = e.matmul(out=dst[:, off:off + P],
                                              lhsT=self.ident[:],
                                              rhs=self.mask_bf[:],
                                              start=False, stop=True)
                            mm.then_inc(self.s3sc, 1)
                        for b in self._p3_batch_slices(g, tg):
                            self._p3_ktbatch(e, s, h, g - 1, b, pTb)
                for b in range(self._p3_nbat(3)):
                    self._p3_ktbatch(e, s, h, 3, b, pTb)
        elif eng == "vector":
            for h in range(H):
                gh = hh(h)
                for g in range(4):
                    for tg in range(4):
                        tau = 4 * g + tg
                        kw = (tau + 1) * P
                        nch = self._p3_nch(tau)
                        par2 = tau % 2
                        cg1 = self._p3_chc(gh, tau, nch - 1)
                        e.wait_ge(self.s3ex, cg1 + 1)
                        if nch == 2:
                            e.tensor_tensor(
                                out=self.srs[:, par2:par2 + 1],
                                in0=self.ssum[:, 2 * par2:2 * par2 + 1],
                                in1=self.ssum[:, 2 * par2 + 1:2 * par2 + 2],
                                op=AluOpType.add)
                            e.reciprocal(out=self.srs[:, par2:par2 + 1],
                                         in_=self.srs[:, par2:par2 + 1]
                                         ).then_inc(self.s3mx, 1)
                        else:
                            e.reciprocal(out=self.srs[:, par2:par2 + 1],
                                         in_=self.ssum[:, 2 * par2:2 * par2 + 1]
                                         ).then_inc(self.s3mx, 1)
                        pr = self.probs_ap(g, tg)
                        e.tensor_scalar(out=pr[:, 0:kw], in0=pr[:, 0:kw],
                                        scalar1=self.srs[:, par2:par2 + 1],
                                        scalar2=None, op0=AluOpType.mult
                                        ).then_inc(self.s3pn, 1)
        elif eng == "scalar":
            if s == 1:
                for par in range(2):
                    e.wait_ge(self.s4sp[par], 16 * (NT // 2))
            for h in range(H):
                gh = hh(h)
                for g in range(4):
                    for tg in range(4):
                        tau = 4 * g + tg
                        kw = (tau + 1) * P
                        nch = self._p3_nch(tau)
                        tidx = gh * NT + tau
                        par2 = tau % 2
                        for k in range(nch):
                            cg = self._p3_chc(gh, tau, k)
                            par = cg % 2
                            e.wait_ge(self.s3sc, cg + 1)
                            if k == 0 and tidx >= 2:
                                e.wait_ge(self.s3mx, tidx - 1)
                            cw = min(1024, kw - 1024 * k)
                            e.activation(out=self.probs_ap(g, tg)[:, 1024 * k:
                                                                  1024 * k + cw],
                                         in_=pS[par][:, 0:cw], func=AF.Exp,
                                         scale=SC_ATT,
                                         accum_out=self.ssum[:, 2 * par2 + k:
                                                             2 * par2 + k + 1]
                                         ).then_inc(self.s3ex, 1)
                        for b in self._p3_batch_slices(g, tg):
                            c = self._p3_tpc(gh, g - 1, b)
                            e.wait_ge(self.s3tp, c + 1)
                            e.copy(out=self.pTe[c % 2][:],
                                   in_=pTb[c % 2][:]).then_inc(self.s3tev, 1)
                    if g >= 1 and tg == 3:
                        e.wait_ge(self.s3av, gh * 4 + (g - 1) + 1)
                        e.copy(out=self.wB[:, h * D + (g - 1) * 512:
                                           h * D + g * 512],
                               in_=self.pAng[:, 0:512]).then_inc(self.s3ae, 1)
                for b in range(self._p3_nbat(3)):
                    c = self._p3_tpc(gh, 3, b)
                    e.wait_ge(self.s3tp, c + 1)
                    e.copy(out=self.pTe[c % 2][:],
                           in_=pTb[c % 2][:]).then_inc(self.s3tev, 1)
                e.wait_ge(self.s3av, gh * 4 + 4)
                e.copy(out=self.wB[:, h * D + 3 * 512:h * D + 4 * 512],
                       in_=self.pAng[:, 0:512]).then_inc(self.s3ae, 1)
        elif eng == "gpsimd":
            for h in range(H):
                e.wait_ge(self.s3ae, gc(h, 3) + 1)
                e.memset(self.dum[:], 0.0).then_inc(self.s3dn, 1)

    def _p3_ktbatch(self, e, s, h, g, b, pTb):
        gh = s * H + h
        c = self._p3_tpc(gh, g, b)
        bpar = c % 2
        e.wait_ge(self.s3pn, gh * NT + 4 * g + 4)
        if c >= 2:
            e.wait_ge(self.s3tev, c - 1)
        for sub in range(2):
            kt = 2 * b + sub
            for tg in range(4):
                tau = 4 * g + tg
                src = (self.probs_ap(g, tg)[:, kt * P:(kt + 1) * P]
                       if kt <= tau else self.zero_bf[:])
                tr = e.transpose(out=pTb[bpar][:, sub * 512 + tg * P:
                                               sub * 512 + (tg + 1) * P],
                                 in_=src, identity=self.ident[:])
        tr.then_inc(self.s3tp, 1)
        e.wait_ge(self.s3tev, c + 1)
        if b == 0:
            e.wait_ge(self.s3ae, gh * 4 + g)
        for sub in range(2):
            kt = 2 * b + sub
            mm = e.matmul(out=self.pAng[:, 0:512],
                          lhsT=self.Vb[h % 2][:, kt * P:(kt + 1) * P],
                          rhs=self.pTe[c % 2][:, sub * 512:(sub + 1) * 512],
                          start=(b == 0 and sub == 0),
                          stop=(b == self._p3_nbat(g) - 1 and sub == 1))
        if b == self._p3_nbat(g) - 1:
            mm.then_inc(self.s3av, 1)

    def p4(self, eng, e, s):
        def n_(t):
            return s * NT + t
        if eng == "gpsimd":
            if s == 0:
                e.dma_start(out=self.wA[:].rearrange("p (c o) -> p c o", o=D),
                            in_=self.wot_h.rearrange("(c p) o -> p c o", p=P)
                            ).then_inc(self.s4w, 16)
        elif eng == "sync":
            for t in range(NT):
                n = n_(t)
                if n >= 1:
                    e.wait_ge(self.s4x2, n)
                e.dma_start(out=self.xres[:],
                            in_=self.x_h[s, t * P:(t + 1) * P, :]
                            ).then_inc(self.s4x[n % 2], 16)
        elif eng == "tensor":
            e.wait_ge(self.s4w, 16)
            for t in range(NT):
                n = n_(t)
                if n >= 1:
                    e.wait_ge(self.s4x2, n)
                for i in range(16):
                    for oc in range(4):
                        mm = e.matmul(out=self.pQ[:, oc * 512:(oc + 1) * 512],
                                      lhsT=self.wB[:, i * D + t * P:
                                                   i * D + (t + 1) * P],
                                      rhs=self.wA[:, i * D + oc * 512:
                                                  i * D + (oc + 1) * 512],
                                      start=(i == 0), stop=(i == 15))
                mm.then_inc(self.s4mm, 1)
                if n >= s * NT + 1:
                    self._p4_trans(e, n - 1)
            self._p4_trans(e, s * NT + NT - 1)
        elif eng == "vector":
            for t in range(NT):
                n = n_(t)
                e.wait_ge(self.s4mm, n + 1)
                e.wait_ge(self.s4x[n % 2], 16 * (n // 2 + 1))
                if n >= 2:
                    e.wait_ge(self.s4spx[n % 2], 16 * (n // 2))
                e.tensor_tensor(out=self.x2t[n % 2][:], in0=self.pQ[:, 0:D],
                                in1=self.xres[:], op=AluOpType.add
                                ).then_inc(self.s4x2, 1)
                e.reduce_sum(out=self.st[:, 0:1], in_=self.x2t[n % 2][:],
                             axis=AX.X)
                e.tensor_scalar(out=self.negm2_c[:, n:n + 1], in0=self.st[:, 0:1],
                                scalar1=-1.0 / D, scalar2=None, op0=AluOpType.mult)
                e.wait_ge(self.s4sq, n + 1)
                e.reduce_sum(out=self.st[:, 1:2], in_=self.xn2[:], axis=AX.X)
                e.tensor_tensor(out=self.st[:, 3:4], in0=self.negm2_c[:, n:n + 1],
                                in1=self.negm2_c[:, n:n + 1], op=AluOpType.mult)
                if n >= 1:
                    e.wait_ge(self.s4r, n)
                e.tensor_scalar(out=self.st[:, 2:3], in0=self.st[:, 1:2],
                                scalar1=1.0 / D, scalar2=None, op0=AluOpType.mult)
                e.tensor_tensor(out=self.st[:, 2:3], in0=self.st[:, 2:3],
                                in1=self.st[:, 3:4], op=AluOpType.subtract)
                e.tensor_scalar(out=self.st[:, 2:3], in0=self.st[:, 2:3],
                                scalar1=EPS, scalar2=None, op0=AluOpType.add)
                e.reciprocal(out=self.st[:, 2:3],
                             in_=self.st[:, 2:3]).then_inc(self.s4g, 1)
                e.wait_ge(self.s4r, n + 1)
                e.tensor_scalar(out=self.xn2[:], in0=self.x2t[n % 2][:],
                                scalar1=self.negm2_c[:, n:n + 1], scalar2=None,
                                op0=AluOpType.add)
                e.tensor_scalar(out=self.xn2[:], in0=self.xn2[:],
                                scalar1=self.r2_c[:, n:n + 1], scalar2=None,
                                op0=AluOpType.mult)
                if n >= 1:
                    e.wait_ge(self.s4tr, n)
                for half in range(2):
                    qi = self.quant_(e, self.xn2[:, half * 1024:(half + 1) * 1024],
                                     self.xqb[:, half * 1024:(half + 1) * 1024],
                                     self.k0i4[:], self.rr4[:], self.u14[:])
                qi.then_inc(self.s4q, 1)
        elif eng == "scalar":
            for t in range(NT):
                n = n_(t)
                e.wait_ge(self.s4x2, n + 1)
                e.dma_start(out=self.x2_h[s, t * P:(t + 1) * P, :],
                            in_=self.x2t[n % 2][:]).then_inc(self.s4spx[n % 2], 16)
                e.activation(out=self.xn2[:], in_=self.x2t[n % 2][:],
                             func=AF.Square).then_inc(self.s4sq, 1)
                e.wait_ge(self.s4g, n + 1)
                e.activation(out=self.r2_c[:, n:n + 1], in_=self.st[:, 2:3],
                             func=AF.Sqrt).then_inc(self.s4r, 1)
                if n >= s * NT + 1:
                    self._p4_evac(e, n - 1)
            self._p4_evac(e, s * NT + NT - 1)

    def _p4_trans(self, e, n):
        e.wait_ge(self.s4q, n + 1)
        if n >= 1:
            e.wait_ge(self.s4te, n)
        for c in range(16):
            tr = e.transpose(out=self.pT[:, c * P:(c + 1) * P],
                             in_=self.xqb[:, c * P:(c + 1) * P],
                             identity=self.ident[:])
        tr.then_inc(self.s4tr, 1)

    def _p4_evac(self, e, n):
        e.wait_ge(self.s4tr, n + 1)
        if n >= 2:
            e.wait_ge(self.s4sp[n % 2], 16 * (n // 2))
        e.copy(out=self.st4[n % 2][:], in_=self.pT[:]).then_inc(self.s4te, 1)
        e.wait_ge(self.s4te, n + 1)
        e.dma_start(out=self.xq_h[n].rearrange("i p q -> p i q"),
                    in_=self.st4[n % 2][:].rearrange("p (i q) -> p i q", q=P)
                    ).then_inc(self.s4sp[n % 2], 16)

    def p5(self, eng, e):
        NF = 4
        NI = NF * NTT
        if eng == "sync":
            for par in range(2):
                e.wait_ge(self.s4sp[par], 16 * (NTT // 2))
            for fc in range(NF):
                for c in range(16):
                    for w_i, wh in ((0, self.w1t_h), (1, self.w2t_h)):
                        cc = (fc * 16 + c) * 2 + w_i
                        if cc >= 2:
                            e.wait_ge(self.s5sg, cc - 1)
                        if w_i == 1 and fc * 16 + c >= 1:
                            e.wait_ge(self.s5ac, fc * 16 + c)
                        e.dma_start(out=self.w5[cc % 2][:],
                                    in_=wh[c * P:(c + 1) * P,
                                           fc * D:(fc + 1) * D]
                                    ).then_inc(self.s5w[cc % 2], 16)
                for tau in range(NTT):
                    it = fc * NTT + tau
                    if it >= 2:
                        e.wait_ge(self.s5mm, it - 1)
                    e.dma_start(out=self.xqt[it % 2][:].rearrange(
                                    "p (i q) -> p i q", q=P),
                                in_=self.xq_h[tau].rearrange("i p q -> p i q")
                                ).then_inc(self.s5xq[it % 2], 16)
        elif eng == "scalar":
            for par in range(2):
                e.wait_ge(self.s4spx[par], 16 * (NTT // 2))
            e.wait_ge(self.s5s1c, 1)
            e.copy(out=self.st[0:1, 12:13],
                   in_=self.pAng[0:1, 0:1]).then_inc(self.s5s1c, 1)
            e.wait_ge(self.s5s1c, 3)
            e.activation(out=self.s1_col[:], in_=self.pAng[:, 2:3], func=AF.Copy,
                         scale=1.0 / (D * FF)).then_inc(self.s5s1c, 1)
            for fc in range(NF):
                if fc >= 1:
                    e.wait_ge(self.s5mm, fc * NTT)
                for c in range(16):
                    for w_i, dst in ((0, self.wA), (1, self.wB)):
                        cc = (fc * 16 + c) * 2 + w_i
                        e.wait_ge(self.s5w[cc % 2], 16 * (cc // 2 + 1))
                        e.sign(out=dst[:, c * D:(c + 1) * D],
                               in_=self.w5[cc % 2][:]).then_inc(self.s5sg, 1)
                for tau in range(NTT):
                    it = fc * NTT + tau
                    e.wait_ge(self.s5m0, it + 1)
                    if it >= 1:
                        e.wait_ge(self.s5h, 2 * it)
                    e.activation(out=self.z1s[:], in_=self.pQ[:, 0:1024],
                                 func=AF.Silu,
                                 scale=self.s1_col[:]).then_inc(self.s5e1, 1)
                    e.wait_ge(self.s5h, 2 * it + 1)
                    e.activation(out=self.z1s[:], in_=self.pQ[:, 1024:2048],
                                 func=AF.Silu,
                                 scale=self.s1_col[:]).then_inc(self.s5e1, 1)
                    if it >= 1:
                        self._p5_evac(e, it - 1)
                if fc == NF - 1:
                    self._p5_evac(e, NI - 1)
        elif eng == "tensor":
            e.matmul(out=self.pAng[0:1, 0:1], lhsT=self.s1acc[:],
                     rhs=self.ones_f[:], start=True,
                     stop=True).then_inc(self.s5s1c, 1)
            e.wait_ge(self.s5s1c, 2)
            e.matmul(out=self.pAng[:, 2:3], lhsT=self.onesrow[:],
                     rhs=self.st[0:1, 12:13], start=True,
                     stop=True).then_inc(self.s5s1c, 1)
            for fc in range(NF):
                for tau in range(NTT):
                    it = fc * NTT + tau
                    e.wait_ge(self.s5xq[it % 2], 16 * (it // 2 + 1))
                    e.wait_ge(self.s5sg, (fc * 16 + 16) * 2)
                    if it >= 1:
                        e.wait_ge(self.s5e1, 2 * it)
                    for i in range(16):
                        for oc in range(4):
                            mm = e.matmul(out=self.pQ[:, oc * 512:(oc + 1) * 512],
                                          lhsT=self.xqt[it % 2][:, i * P:(i + 1) * P],
                                          rhs=self.wA[:, i * D + oc * 512:
                                                      i * D + (oc + 1) * 512],
                                          start=(i == 0), stop=(i == 15))
                    mm.then_inc(self.s5m0, 1)
                    if it >= 1:
                        e.wait_ge(self.s5h, 2 * it)
                    for i in range(16):
                        for oc in range(2):
                            mm = e.matmul(out=self.pAng[:, oc * 512:(oc + 1) * 512],
                                          lhsT=self.xqt[it % 2][:, i * P:(i + 1) * P],
                                          rhs=self.wB[:, i * D + oc * 512:
                                                      i * D + (oc + 1) * 512],
                                          start=(i == 0), stop=(i == 15))
                    mm.then_inc(self.s5z0, 1)
                    if it >= 1:
                        self._p5_trans(e, it - 1)
                    e.wait_ge(self.s5h, 2 * it + 1)
                    for i in range(16):
                        for oc in range(2, 4):
                            mm = e.matmul(out=self.pAng[:, (oc - 2) * 512:
                                                        (oc - 1) * 512],
                                          lhsT=self.xqt[it % 2][:, i * P:(i + 1) * P],
                                          rhs=self.wB[:, i * D + oc * 512:
                                                      i * D + (oc + 1) * 512],
                                          start=(i == 0), stop=(i == 15))
                    mm.then_inc(self.s5mm, 1)
            self._p5_trans(e, NI - 1)
        elif eng == "vector":
            for par in range(2):
                e.wait_ge(self.s4spx[par], 16 * (NTT // 2))
            for fc in range(NF):
                for c in range(16):
                    j = fc * 16 + c
                    e.wait_ge(self.s5w[1], 16 * (j + 1))
                    e.reduce_sum(out=self.st[:, 4:5], in_=self.w5[1][:],
                                 axis=AX.X, apply_absolute_value=True)
                    e.tensor_tensor(out=self.s2acc[:], in0=self.s2acc[:],
                                    in1=self.st[:, 4:5],
                                    op=AluOpType.add).then_inc(self.s5ac, 1)
                for tau in range(NTT):
                    it = fc * NTT + tau
                    e.wait_ge(self.s5e1, 2 * it + 1)
                    e.wait_ge(self.s5z0, it + 1)
                    e.tensor_tensor(out=self.hh[:], in0=self.z1s[:],
                                    in1=self.pAng[:, 0:1024],
                                    op=AluOpType.mult).then_inc(self.s5h, 1)
                    if it >= 1:
                        e.wait_ge(self.s5tr, it)
                    self.quant_(e, self.hh[:], self.hqb[:, 0:1024],
                                self.k0i5[:], self.rr5[:], self.u15[:]
                                ).then_inc(self.s5q, 1)
                    e.wait_ge(self.s5e1, 2 * it + 2)
                    e.wait_ge(self.s5mm, it + 1)
                    e.tensor_tensor(out=self.hh[:], in0=self.z1s[:],
                                    in1=self.pAng[:, 0:1024],
                                    op=AluOpType.mult).then_inc(self.s5h, 1)
                    self.quant_(e, self.hh[:], self.hqb[:, 1024:2048],
                                self.k0i5[:], self.rr5[:], self.u15[:]
                                ).then_inc(self.s5q, 1)

    def _p5_trans(self, e, it):
        e.wait_ge(self.s5q, 2 * it + 2)
        if it >= 1:
            e.wait_ge(self.s5te, it)
        for c in range(16):
            tr = e.transpose(out=self.pT[:, c * P:(c + 1) * P],
                             in_=self.hqb[:, c * P:(c + 1) * P],
                             identity=self.ident[:])
        tr.then_inc(self.s5tr, 1)

    def _p5_evac(self, e, it):
        fc, tau = divmod(it, NTT)
        e.wait_ge(self.s5tr, it + 1)
        if it >= 2:
            e.wait_ge(self.s5sp[it % 2], 16 * (it // 2))
        e.copy(out=self.st5[it % 2][:], in_=self.pT[:]).then_inc(self.s5te, 1)
        e.wait_ge(self.s5te, it + 1)
        e.dma_start(out=self.hq_h[fc * 16:(fc + 1) * 16, tau].rearrange(
                        "i p q -> p i q"),
                    in_=self.st5[it % 2][:].rearrange("p (i q) -> p i q", q=P)
                    ).then_inc(self.s5sp[it % 2], 16)

    def p6(self, eng, e):
        if eng == "sync":
            for c in range(64):
                if c >= 2:
                    e.wait_ge(self.s6sg, c - 1)
                    e.wait_ge(self.s6ac, c - 1)
                e.dma_start(out=self.w6[c % 2][:],
                            in_=self.w3t_h[(c % 64) * P:(c % 64 + 1) * P, :]
                            ).then_inc(self.s6w[c % 2], 16)
        elif eng == "scalar":
            for c in range(64):
                e.wait_ge(self.s6w[c % 2], 16 * (c // 2 + 1))
                if c >= 2:
                    e.wait_ge(self.s6sp[c % 2], 16 * (c // 2))
                e.sign(out=self.w3sb[c % 2][:],
                       in_=self.w6[c % 2][:]).then_inc(self.s6sg, 1)
                e.wait_ge(self.s6sg, c + 1)
                e.dma_start(out=self.w3s_h[c],
                            in_=self.w3sb[c % 2][:]).then_inc(self.s6sp[c % 2], 16)
            e.wait_ge(self.s6sc, 1)
            e.copy(out=self.st[0:1, 10:11], in_=self.pAng[0:1, 0:1])
            e.copy(out=self.st[0:1, 11:12],
                   in_=self.pAng[0:1, 1:2]).then_inc(self.s6sc, 1)
            e.wait_ge(self.s6sc, 4)
            e.copy(out=self.s23_col[:],
                   in_=self.pAng[:, 4:5]).then_inc(self.s6sc, 1)
        elif eng == "vector":
            for c in range(64):
                e.wait_ge(self.s6w[c % 2], 16 * (c // 2 + 1))
                e.reduce_sum(out=self.st[:, 4:5], in_=self.w6[c % 2][:],
                             axis=AX.X, apply_absolute_value=True)
                e.tensor_tensor(out=self.s3acc[:], in0=self.s3acc[:],
                                in1=self.st[:, 4:5],
                                op=AluOpType.add).then_inc(self.s6ac, 1)
            e.wait_ge(self.s6sc, 2)
            e.tensor_tensor(out=self.st[0:1, 10:11], in0=self.st[0:1, 10:11],
                            in1=self.st[0:1, 11:12], op=AluOpType.mult)
            e.tensor_scalar(out=self.st[0:1, 10:11], in0=self.st[0:1, 10:11],
                            scalar1=1.0 / (D * FF), scalar2=1.0 / (D * FF),
                            op0=AluOpType.mult,
                            op1=AluOpType.mult).then_inc(self.s6sc, 1)
        elif eng == "tensor":
            e.wait_ge(self.s6ac, 64)
            e.matmul(out=self.pAng[0:1, 0:1], lhsT=self.s2acc[:],
                     rhs=self.ones_f[:], start=True, stop=True)
            e.matmul(out=self.pAng[0:1, 1:2], lhsT=self.s3acc[:],
                     rhs=self.ones_f[:], start=True,
                     stop=True).then_inc(self.s6sc, 1)
            e.wait_ge(self.s6sc, 3)
            e.matmul(out=self.pAng[:, 4:5], lhsT=self.onesrow[:],
                     rhs=self.st[0:1, 10:11], start=True,
                     stop=True).then_inc(self.s6sc, 1)

    def p7(self, eng, e):
        NSUP = 4
        NFC = 4
        if eng == "sync":
            for it in range(NSUP * NFC):
                sup, fc = divmod(it, NFC)
                if it >= 1:
                    e.wait_ge(self.s7mm, it * 8)
                e.dma_start(out=self.wA[:].rearrange("p (j o) -> p j o", o=D),
                            in_=self.w3s_h[fc * 16:(fc + 1) * 16].rearrange(
                                "j p o -> p j o")).then_inc(self.s7w[0], 16)
        elif eng == "gpsimd":
            for par in range(2):
                e.wait_ge(self.s6sp[par], 16 * 32)
                e.wait_ge(self.s5sp[par], 16 * (4 * NTT // 2))
            for it in range(NSUP * NFC):
                sup, fc = divmod(it, NFC)
                for tl in range(8):
                    tau = sup * 8 + tl
                    k = it * 8 + tl
                    if k >= 2:
                        e.wait_ge(self.s7mm, k - 1)
                    e.dma_start(out=self.hqt[k % 2][:].rearrange(
                                    "p (j q) -> p j q", q=P),
                                in_=self.hq_h[fc * 16:(fc + 1) * 16, tau].rearrange(
                                    "j p q -> p j q")
                                ).then_inc(self.s7h[k % 2], 16)
        elif eng == "tensor":
            for it in range(NSUP * NFC):
                sup, fc = divmod(it, NFC)
                e.wait_ge(self.s7w[0], 16 * (it + 1))
                for tl in range(8):
                    tau = sup * 8 + tl
                    k = it * 8 + tl
                    e.wait_ge(self.s7h[k % 2], 16 * (k // 2 + 1))
                    if k >= 2:
                        e.wait_ge(self.s7ac, k - 1)
                    par = k % 2
                    for j in range(16):
                        for oc in range(4):
                            if par == 0:
                                dst = self.pQ[:, oc * 512:(oc + 1) * 512]
                            else:
                                dst = (self.pAng[:, (oc % 2) * 512:(oc % 2 + 1) * 512]
                                       if oc < 2 else
                                       self.pTf[:, (oc % 2) * 512:(oc % 2 + 1) * 512])
                            mm = e.matmul(out=dst,
                                          lhsT=self.hqt[k % 2][:, j * P:(j + 1) * P],
                                          rhs=self.wA[:, j * D + oc * 512:
                                                      j * D + (oc + 1) * 512],
                                          start=(j == 0), stop=(j == 15))
                    mm.then_inc(self.s7mm, 1)
        elif eng == "vector":
            for it in range(NSUP * NFC):
                sup, fc = divmod(it, NFC)
                for tl in range(8):
                    tau = sup * 8 + tl
                    k = it * 8 + tl
                    e.wait_ge(self.s7mm, k + 1)
                    par = k % 2
                    aslc = self.acc[:, tl * D:(tl + 1) * D]
                    if par == 0:
                        srcs = [self.pQ[:, 0:1024], self.pQ[:, 1024:2048]]
                    else:
                        srcs = [self.pAng[:, 0:1024], self.pTf[:, 0:1024]]
                    for half in range(2):
                        a2 = aslc[:, half * 1024:(half + 1) * 1024]
                        if fc == 0:
                            ai = e.tensor_copy(out=a2, in_=srcs[half])
                        else:
                            ai = e.tensor_tensor(out=a2, in0=a2, in1=srcs[half],
                                                 op=AluOpType.add)
                    ai.then_inc(self.s7ac, 1)
                if fc == NFC - 1:
                    for tl in range(8):
                        tau = sup * 8 + tl
                        k2 = sup * 8 + tl
                        e.wait_ge(self.s7x2[k2 % 2], 16 * (k2 // 2 + 1))
                        if k2 >= 2:
                            e.wait_ge(self.s7out[k2 % 2], 16 * (k2 // 2))
                        e.scalar_tensor_tensor(out=self.outb[k2 % 2][:],
                                               in0=aslc_get(self, tl),
                                               scalar=self.s23_col[:],
                                               in1=self.x2r[k2 % 2][:],
                                               op0=AluOpType.mult,
                                               op1=AluOpType.add
                                               ).then_inc(self.s7ev, 1)
        elif eng == "scalar":
            def outd(k2):
                s_, t_ = divmod(k2, NT)
                e.wait_ge(self.s7ev, k2 + 1)
                e.dma_start(out=self.out_h[s_, t_ * P:(t_ + 1) * P, :],
                            in_=self.outb[k2 % 2][:]
                            ).then_inc(self.s7out[k2 % 2], 16)
            for par in range(2):
                e.wait_ge(self.s4spx[par], 16 * (NTT // 2))
                e.wait_ge(self.s6sp[par], 16 * 32)
                e.wait_ge(self.s5sp[par], 16 * (4 * NTT // 2))
            for k2 in range(NTT):
                if k2 >= 2:
                    outd(k2 - 2)
                    e.wait_ge(self.s7ev, k2 - 1)
                s_, t_ = divmod(k2, NT)
                e.dma_start(out=self.x2r[k2 % 2][:],
                            in_=self.x2_h[s_, t_ * P:(t_ + 1) * P, :]
                            ).then_inc(self.s7x2[k2 % 2], 16)
            outd(NTT - 2)
            outd(NTT - 1)
            e.wait_ge(self.s7out[0], 16 * 16)
            e.wait_ge(self.s7out[1], 16 * 16)


def aslc_get(b, tl):
    return b.acc[:, tl * D:(tl + 1) * D]


def build():
    nc = bass.Bass("TRN2", target_bir_lowering=False)
    b = B(nc)
    ENG = ["sync", "gpsimd", "vector", "scalar", "tensor"]

    def phase(fn):
        with nc.Block() as block:
            for eng in ENG:
                dec = getattr(block, eng)

                def mk(eng=eng):
                    def _(e):
                        if eng in ("vector", "scalar", "gpsimd"):
                            e = EP(e)
                        fn(eng, e)
                    return _
                dec(mk())

    def g_a(eng, e):
        if eng in ("vector", "gpsimd"):
            b.init_(eng, e)
        b.p0(eng, e)
        b.p05(eng, e)
        for w in ("q", "k", "v"):
            b.p1(eng, e, w)
    phase(g_a)
    phase(lambda eng, e: b.p3(eng, e, 0))
    phase(lambda eng, e: b.p4(eng, e, 0))
    phase(lambda eng, e: b.p3(eng, e, 1))
    phase(lambda eng, e: b.p4(eng, e, 1))
    phase(lambda eng, e: b.p5(eng, e))
    phase(lambda eng, e: b.p6(eng, e))
    phase(lambda eng, e: b.p7(eng, e))
    return nc, b


def kernel(L0_attn_x, L0_attn_Wq, L0_attn_Wk, L0_attn_Wv, L0_attn_Wo,
           L0_ffn_w1, L0_ffn_w2, L0_ffn_w3):
    x = np.ascontiguousarray(np.asarray(L0_attn_x, dtype=np.float32))
    wqt = np.ascontiguousarray(np.asarray(L0_attn_Wq, dtype=np.float32).T)
    wkt = np.ascontiguousarray(np.asarray(L0_attn_Wk, dtype=np.float32).T)
    wvt = np.ascontiguousarray(np.asarray(L0_attn_Wv, dtype=np.float32).T)
    wot = np.ascontiguousarray(np.asarray(L0_attn_Wo, dtype=np.float32).T)
    w1t = np.ascontiguousarray(np.asarray(L0_ffn_w1, dtype=np.float32).T)
    w2t = np.ascontiguousarray(np.asarray(L0_ffn_w2, dtype=np.float32).T)
    w3t = np.ascontiguousarray(np.asarray(L0_ffn_w3, dtype=np.float32).T)
    nc, _ = build()
    in_maps = [{"x": x[2 * c:2 * c + 2], "wqt": wqt, "wkt": wkt, "wvt": wvt,
                "wot": wot, "w1t": w1t, "w2t": w2t, "w3t": w3t}
               for c in range(8)]
    res = run_bass_kernel_spmd(nc, in_maps, core_ids=list(range(8)))
    return np.concatenate([np.asarray(r["out"]) for r in res.results], axis=0)
